# version 14
# baseline (speedup 1.0000x reference)
import numpy as np
import ml_dtypes
from contextlib import ExitStack

import concourse.bass as bass
import concourse.mybir as mybir
from concourse.bass_utils import run_bass_kernel_spmd

F32 = mybir.dt.float32
BF16 = mybir.dt.bfloat16
AF = mybir.ActivationFunctionType
ALU = mybir.AluOpType
AX = mybir.AxisListType

B = 4
S = 4096
D = 1024
DC = D // 128
NCORE = 8
LN_EPS = 1e-5
BF = ml_dtypes.bfloat16



class Seq:

    def __init__(self, prog, eng, name):
        self.p = prog
        self.e = eng
        self.name = name
        self.sem = prog.sem(name)
        self.n = prog.counts.get(name, 0)

    def op(self, fname, *a, dep=True, **kw):
        if dep and self.n:
            self.e.wait_ge(self.sem, self.n)
        inst = getattr(self.e, fname)(*a, **kw)
        inst.then_inc(self.sem, 1)
        self.n += 1
        self.p.counts[self.name] = self.n
        return inst

    def mark(self, tokname):
        self.p.tokens[self.p.prefix + tokname] = (self.name, self.n)

    def wait_tok(self, tokname, may_miss=False):
        import re
        m = re.search(r"-?\d+$", tokname)
        if m and int(m.group()) < 0:
            return
        if self.p.prev_tokens is None:
            return
        tokname = self.p.prefix + tokname
        if may_miss and tokname not in self.p.prev_tokens:
            return
        semname, cnt = self.p.prev_tokens[tokname]
        if cnt > 0:
            self.e.wait_ge(self.p.sem(semname), cnt)

    def wait(self, semname, val):
        semname = self.p.rsem(semname)
        if val > 0:
            self.e.wait_ge(self.p.sem(semname), val)

    def wait_rel(self, semname, val):
        semname = self.p.rsem(semname)
        base = self.p.pbase.get(semname, 0)
        if base + val > 0:
            self.e.wait_ge(self.p.sem(semname), base + val)

    def dma(self, semname, out, in_, **kw):
        return self.p.dma(self.e, semname, out, in_, **kw)


class Prog:
    MAXC = 16

    def __init__(self, name, prev_tokens=None, prev_consts=None):
        self.nc = bass.Bass("TRN2", target_bir_lowering=False)
        self.es = ExitStack()
        self.sems = {}
        self.dmac = {}
        self.counts = {}
        self.tokens = {}
        self.prev_tokens = prev_tokens
        self.name = name
        self.const_vals = list(prev_consts) if prev_consts else []
        self._ctile = None
        self.prefix = ""
        self.io = {}
        self.pbase = {}
        self.cur_es = self.es

    def sem(self, name):
        name = self.rsem(name) if hasattr(self, "dmac") else name
        if name not in self.sems:
            self.sems[name] = self.es.enter_context(self.nc.semaphore(name))
            self.dmac.setdefault(name, 0)
        return self.sems[name]

    def dma(self, eng, semname, out, in_, **kw):
        if "GpSimd" in type(eng).__name__:
            semname = self.prefix + semname
        s = self.sem(semname)
        eng.dma_start(out=out, in_=in_, **kw).then_inc(s, 16)
        self.dmac[semname] += 16
        return self.dmac[semname]

    def rsem(self, name):
        if self.prefix + name in self.dmac:
            return self.prefix + name
        return name

    def dtok(self, tokname, semname):
        semname = self.rsem(semname)
        self.tokens[self.prefix + tokname] = (semname, self.dmac[semname])

    def phase(self, name):
        from contextlib import contextmanager

        @contextmanager
        def _cm():
            old_prefix, old_es, old_pbase = self.prefix, self.cur_es, self.pbase
            self.prefix = name + "."
            self.pbase = dict(self.dmac)
            self.cur_es = ExitStack()
            try:
                yield
            finally:
                self.cur_es.close()
                self.prefix, self.cur_es, self.pbase = old_prefix, old_es, old_pbase
        return _cm()

    def ctile(self):
        if self._ctile is None:
            self._ctile = self.es.enter_context(
                self.nc.sbuf_tensor("_consts", [128, self.MAXC], F32))
        return self._ctile

    def cc(self, value):
        value = float(value)
        if value not in self.const_vals:
            self.const_vals.append(value)
        assert len(self.const_vals) <= self.MAXC
        return self.ctile()[:, self.const_vals.index(value):self.const_vals.index(value) + 1]

    def emit_consts(self, gseq):
        t = self.ctile()
        vals = self.const_vals + [0.0] * (self.MAXC - len(self.const_vals))
        for j, v in enumerate(vals):
            gseq.op("memset", t[:, j:j + 1], v)
        gseq.mark("consts")

    def sb(self, name, shape, dtype):
        return self.cur_es.enter_context(self.nc.sbuf_tensor(self.prefix + name, shape, dtype))

    def ps(self, name, shape, dtype=F32):
        return self.cur_es.enter_context(self.nc.psum_tensor(self.prefix + name, shape, dtype))

    def din(self, name, shape, dtype=F32):
        if name in self.io:
            return self.io[name]
        return self.nc.dram_tensor(self.prefix + name, shape, dtype, kind="ExternalInput").ap()

    def dout(self, name, shape, dtype=F32):
        if name in self.io:
            return self.io[name]
        return self.nc.dram_tensor(self.prefix + name, shape, dtype, kind="ExternalOutput").ap()

    def close(self):
        self.es.close()
        return self.nc


def build_twice(fn, *args, **kw):
    p1 = Prog("pass1")
    fn(p1, *args, **kw)
    toks = dict(p1.tokens)
    p1.close()
    p2 = Prog("pass2", prev_tokens=toks, prev_consts=p1.const_vals)
    fn(p2, *args, **kw)
    assert p1.counts == p2.counts, (p1.counts, p2.counts)
    p2.close()
    return p2.nc


CQ = 8.0
CK = 8.0


def emit_pre0a(p, TL):
    NT = TL // 128
    nc = p.nc

    x = p.din("x", [TL, D])
    wq = p.din("wq", [D, D]); wk = p.din("wk", [D, D]); wv = p.din("wv", [D, D])
    xn0_o = p.dout("xn0", [TL, D])
    eq_o = p.dout("eq", [D, TL], BF16)
    ctx_o = p.dout("ctx", [D, D + 1])

    xt = p.sb("xt", [128, 2, D], F32)
    xnt = p.sb("xnt", [128, 2, D], F32)
    xnt16 = p.sb("xnt16", [128, 2, D], BF16)
    sqd = p.sb("sqd", [128, D], F32)
    stats = p.sb("stats", [128, 8], F32)
    xnf = p.sb("xnf", [128, DC, TL], BF16)
    ekv = p.sb("ekv", [128, NT, 2 * D], BF16)
    w3 = p.sb("w3", [128, 3, DC, D], BF16)
    ident = p.sb("ident", [128, 128], BF16)
    onesc = p.sb("onesc", [128, 1], BF16)
    eqv = p.sb("eqv", [128, 2, 512], BF16)
    ctxv = p.sb("ctxv", [128, 2, 513], F32)

    pa = p.ps("pa", [128, 512], F32)
    pb = p.ps("pb", [128, 512], F32)
    pt = p.ps("pt", [128, 512], BF16)
    pd = p.ps("pd", [128, 16], F32)

    with nc.Block() as blk:
        @blk.sync
        def _(sy):
            for i in range(NT):
                s = Seq(p, sy, "sy")
                s.wait_tok(f"xtfree{i}", may_miss=True)
                p.dma(sy, f"ld{i % 2}", xt[:, i % 2], x.rearrange("(n p) d -> n p d", p=128)[i])
                p.dtok(f"xld{i}", f"ld{i % 2}")

        @blk.gpsimd
        def _(g):
            gg = Seq(p, g, "g")
            for wi, w in enumerate((wq, wk, wv)):
                for c in range(DC):
                    p.dma(g, "ldw", w3[:, wi, c], w[c * 128:(c + 1) * 128, :])
            p.dtok("wload", "ldw")
            gg.op("memset", ident[:], 1.0)
            gg.op("affine_select", ident[:], ident[:], [[-1, 128]], ALU.is_equal, 0.0,
                  base=0, channel_multiplier=1)
            gg.op("memset", onesc[:], 1.0)
            gg.mark("identdone")
            p.emit_consts(gg)

        @blk.scalar
        def _(sc):
            s = Seq(p, sc, "sc")
            for i in range(NT):
                s.wait_rel(f"ld{i % 2}", 16 * (i // 2 + 1))
                s.wait_tok(f"statfreev{i}", may_miss=True)
                s.op("activation", sqd[:], xt[:, i % 2], AF.Square, accum_out=stats[:, 1:2])
                s.mark(f"ssq{i}")
                s.wait_tok(f"var{i}")
                s.wait_tok("consts")
                s.op("activation", stats[:, 3:4], stats[:, 4:5], AF.Ln, bias=p.cc(LN_EPS), scale=1.0)
                s.op("activation", stats[:, 3:4], stats[:, 3:4], AF.Exp, bias=0.0, scale=-0.5)
                s.mark(f"rstd{i}")

        @blk.vector
        def _(ve):
            v = Seq(p, ve, "v")
            for i in range(NT):
                v.wait_rel(f"ld{i % 2}", 16 * (i // 2 + 1))
                v.wait_tok(f"ssq{i}")
                v.op("reduce_sum", stats[:, 0:1], xt[:, i % 2], axis=AX.X)
                v.op("tensor_scalar", stats[:, 2:3], stats[:, 0:1], 1.0 / D, None, ALU.mult)
                v.op("tensor_scalar", stats[:, 5:6], stats[:, 2:3], stats[:, 2:3], None, ALU.mult)
                v.op("tensor_scalar", stats[:, 4:5], stats[:, 1:2], 1.0 / D, stats[:, 5:6],
                     ALU.mult, ALU.subtract)
                v.mark(f"var{i}")
                v.mark(f"statfreev{i + 1}")
                v.wait_tok(f"rstd{i}")
                v.wait_tok(f"xnstored{i - 2}")
                v.op("tensor_scalar", xnt[:, i % 2], xt[:, i % 2], stats[:, 2:3], stats[:, 3:4],
                     ALU.subtract, ALU.mult)
                v.mark(f"xtfree{i + 2}")
                v.wait_tok(f"tpb{i - 2}")
                v.op("tensor_copy", xnt16[:, i % 2], xnt[:, i % 2])
                v.mark(f"xn16{i}")
                v.wait_tok(f"tpa{i}")
                v.op("tensor_copy", xnf[:, 0:4, i * 128:(i + 1) * 128], pt[:].rearrange("p (c t) -> p c t", t=128))
                v.mark(f"tpaev{i}")
                v.wait_tok(f"tpb{i}")
                v.op("tensor_copy", xnf[:, 4:8, i * 128:(i + 1) * 128], pt[:].rearrange("p (c t) -> p c t", t=128))
                v.mark(f"tpbev{i}")

        @blk.tensor
        def _(te):
            t = Seq(p, te, "t")
            t.wait_tok("identdone")
            for i in range(NT):
                t.wait_tok(f"xn16{i}")
                t.wait_tok(f"tpbev{i - 1}")
                for c in range(4):
                    inst = te.transpose(pt[:, c * 128:(c + 1) * 128], xnt16[:, i % 2, c * 128:(c + 1) * 128], ident[:])
                inst.then_inc(p.sem("t"), 1); t.n += 1; p.counts["t"] = t.n
                t.mark(f"tpa{i}")
                t.wait_tok(f"tpaev{i}")
                for c in range(4):
                    inst = te.transpose(pt[:, c * 128:(c + 1) * 128], xnt16[:, i % 2, (4 + c) * 128:(5 + c) * 128], ident[:])
                inst.then_inc(p.sem("t"), 1); t.n += 1; p.counts["t"] = t.n
                t.mark(f"tpb{i}")

        @blk.gpsimd
        def _(g):
            gg = Seq(p, g, "g2")
            for i in range(NT):
                gg.wait_tok(f"xn16{i}")
                p.dma(g, f"sxn{i % 2}", xn0_o.rearrange("(n p) d -> n p d", p=128)[i], xnt[:, i % 2])
                p.dtok(f"xnstored{i}", f"sxn{i % 2}")

    with nc.Block() as blk2:
        @blk2.tensor
        def _(te):
            t = Seq(p, te, "t2")
            t.wait_tok("wload")
            t.wait_tok(f"tpbev{NT - 1}")
            gi = 0
            for i in range(NT):
                for proj in (1, 2):
                    for nn in range(2):
                        bank = (pa, pb)[gi % 2]
                        t.wait_tok(f"kvev{gi - 2}")
                        for dc in range(DC):
                            inst = te.matmul(bank[:], xnf[:, dc, i * 128:(i + 1) * 128],
                                             w3[:, proj, dc, nn * 512:(nn + 1) * 512],
                                             start=(dc == 0), stop=(dc == DC - 1))
                        inst.then_inc(p.sem("t2"), 1); t.n += 1; p.counts["t2"] = t.n
                        t.mark(f"kv{gi}")
                        gi += 1
            for do in range(DC):
                for ts in range(TL // 512):
                    bank = (pa, pb)[gi % 2]
                    t.wait_tok(f"kvev{gi - 2}")
                    for dc in range(DC):
                        inst = te.matmul(bank[:], w3[:, 0, dc, do * 128:(do + 1) * 128],
                                         xnf[:, dc, ts * 512:(ts + 1) * 512],
                                         start=(dc == 0), stop=(dc == DC - 1))
                    inst.then_inc(p.sem("t2"), 1); t.n += 1; p.counts["t2"] = t.n
                    t.mark(f"kv{gi}")
                    gi += 1
            ci = 0
            for dc in range(DC):
                for nn in range(2):
                    bank = (pa, pb)[gi % 2]
                    t.wait_tok(f"kvev{gi - 2}")
                    t.wait_tok("ekvready")
                    t.wait_tok("ekready")
                    for i in range(NT):
                        inst = te.matmul(bank[:], ekv[:, i, dc * 128:(dc + 1) * 128],
                                         ekv[:, i, D + nn * 512:D + (nn + 1) * 512],
                                         start=(i == 0), stop=(i == NT - 1))
                    inst.then_inc(p.sem("t2"), 1); t.n += 1; p.counts["t2"] = t.n
                    t.mark(f"kv{gi}")
                    gi += 1
                t.wait_tok(f"denev{ci - 1}")
                for i in range(NT):
                    inst = te.matmul(pd[:, 0:1], ekv[:, i, dc * 128:(dc + 1) * 128], onesc[:],
                                     start=(i == 0), stop=(i == NT - 1))
                inst.then_inc(p.sem("t2"), 1); t.n += 1; p.counts["t2"] = t.n
                t.mark(f"den{ci}")
                ci += 1

        @blk2.scalar
        def _(sc):
            s = Seq(p, sc, "sc2")
            s.wait_tok("consts")
            gi = 0
            for i in range(NT):
                for proj in (1, 2):
                    for nn in range(2):
                        if proj == 1:
                            bank = (pa, pb)[gi % 2]
                            s.wait_tok(f"kv{gi}")
                            s.op("activation", ekv[:, i, nn * 512:(nn + 1) * 512], bank[:],
                                 AF.Exp, bias=p.cc(-CK), scale=1.0)
                            s.mark(f"kvev{gi}")
                        gi += 1
            s.mark("ekready")
            for do in range(DC):
                for ts in range(TL // 512):
                    bank = (pa, pb)[gi % 2]
                    s.wait_tok(f"kv{gi}")
                    s.wait_tok(f"eqst{gi - 2}", may_miss=True)
                    s.op("activation", eqv[:, gi % 2], bank[:], AF.Exp, bias=p.cc(-CQ), scale=1.0)
                    s.mark(f"kvev{gi}")
                    s.mark(f"eqv{gi}")
                    gi += 1

        @blk2.vector
        def _(ve):
            v = Seq(p, ve, "v2")
            gi = 0
            for i in range(NT):
                for proj in (1, 2):
                    for nn in range(2):
                        if proj == 2:
                            bank = (pa, pb)[gi % 2]
                            v.wait_tok(f"kv{gi}")
                            v.op("tensor_copy", ekv[:, i, D + nn * 512:D + (nn + 1) * 512], bank[:])
                            v.mark(f"kvev{gi}")
                        gi += 1
            v.mark("ekvready")
            for do in range(DC):
                for ts in range(TL // 512):
                    gi += 1
            ci = 0
            for dc in range(DC):
                for nn in range(2):
                    bank = (pa, pb)[gi % 2]
                    v.wait_tok(f"kv{gi}")
                    v.wait_tok(f"ctxst{(dc * 2 + nn) - 2}", may_miss=True)
                    v.op("tensor_copy", ctxv[:, (dc * 2 + nn) % 2, 0:512], bank[:])
                    v.mark(f"kvev{gi}")
                    v.mark(f"ctxv{dc * 2 + nn}")
                    gi += 1
                v.wait_tok(f"den{ci}")
                v.op("tensor_copy", ctxv[:, (dc * 2 + 1) % 2, 512:513], pd[:, 0:1])
                v.mark(f"denev{ci}")
                ci += 1

        @blk2.gpsimd
        def _(g):
            gi = NT * 4
            for do in range(DC):
                for ts in range(TL // 512):
                    gg = Seq(p, g, "g3")
                    gg.wait_tok(f"eqv{gi}")
                    p.dma(g, f"seq{gi % 2}", eq_o[do * 128:(do + 1) * 128, ts * 512:(ts + 1) * 512],
                          eqv[:, gi % 2])
                    p.dtok(f"eqst{gi}", f"seq{gi % 2}")
                    gi += 1
            for dc in range(DC):
                for nn in range(2):
                    gg = Seq(p, g, "g3")
                    gg.wait_tok(f"ctxv{dc * 2 + nn}")
                    if nn == 1:
                        gg.wait_tok(f"denev{dc}")
                    ap = ctx_o[dc * 128:(dc + 1) * 128, nn * 512:nn * 512 + (513 if nn == 1 else 512)]
                    src = ctxv[:, (dc * 2 + nn) % 2, 0:(513 if nn == 1 else 512)]
                    p.dma(g, f"sctx{(dc * 2 + nn) % 2}", ap, src)
                    p.dtok(f"ctxst{dc * 2 + nn}", f"sctx{(dc * 2 + nn) % 2}")

        @blk2.sync
        def _(sy):
            s = Seq(p, sy, "syf")
            for nm in ("sxn0", "sxn1", "seq0", "seq1", "sctx0", "sctx1"):
                if p.dmac.get(p.rsem(nm)):
                    s.wait(nm, p.dmac[p.rsem(nm)])


def build_pre0a(TL):
    return build_twice(emit_pre0a, TL)



L2_EPS = 1e-12


def emit_pre0b(p, TL):
    NT = TL // 128
    nc = p.nc

    xn = p.din("xn", [TL, D])
    wq = p.din("wq", [D, D]); wk = p.din("wk", [D, D]); wv = p.din("wv", [D, D])
    ln_g = p.din("ln_g", [1, D]); ln_b = p.din("ln_b", [1, D])
    qsc = p.din("qsc", [1, D]); ksc = p.din("ksc", [1, D])
    qh_o = p.dout("qh", [TL, D], BF16)
    kh_o = p.dout("kh", [TL, D], BF16)
    vh_o = p.dout("vh", [TL, D], BF16)

    xt = p.sb("xt", [128, 2, D], F32)
    ht16 = p.sb("ht16", [128, 2, D], BF16)
    sqd = p.sb("sqd", [128, 2048], F32)
    stats = p.sb("stats", [128, 8], F32)
    hf = p.sb("hf", [128, DC, TL], BF16)
    w3 = p.sb("w3", [128, 3, DC, D], BF16)
    ident = p.sb("ident", [128, 128], BF16)
    gamb = p.sb("gamb", [128, 2, D], F32)
    qks = p.sb("qks", [128, 2 * D], F32)
    qkt = p.sb("qkt", [128, 2, 2 * D], F32)
    vt16 = p.sb("vt16", [128, 2, D], BF16)
    qk16 = p.sb("qk16", [128, 2, 2 * D], BF16)
    ssq32 = p.sb("ssq32", [128, 2, 32], F32)

    pa = p.ps("pa", [128, 512], F32)
    pb = p.ps("pb", [128, 512], F32)
    pt = p.ps("pt", [128, 512], BF16)

    with nc.Block() as blk:
        @blk.sync
        def _(sy):
            s = Seq(p, sy, "sy")
            p.dma(sy, "ldc", gamb[:, 0], ln_g.to_broadcast([128, D]))
            p.dma(sy, "ldc", gamb[:, 1], ln_b.to_broadcast([128, D]))
            p.dma(sy, "ldc", qks[:, 0:D], qsc.to_broadcast([128, D]))
            p.dma(sy, "ldc", qks[:, D:2 * D], ksc.to_broadcast([128, D]))
            for i in range(NT):
                s.wait_tok(f"xtfree{i}", may_miss=True)
                p.dma(sy, f"ld{i % 2}", xt[:, i % 2], xn.rearrange("(n p) d -> n p d", p=128)[i])

        @blk.gpsimd
        def _(g):
            gg = Seq(p, g, "g")
            for wi, w in enumerate((wq, wk, wv)):
                for c in range(DC):
                    p.dma(g, "ldw", w3[:, wi, c], w[c * 128:(c + 1) * 128, :])
            p.dtok("wload", "ldw")
            gg.op("memset", ident[:], 1.0)
            gg.op("affine_select", ident[:], ident[:], [[-1, 128]], ALU.is_equal, 0.0,
                  base=0, channel_multiplier=1)
            gg.mark("identdone")
            p.emit_consts(gg)

        @blk.scalar
        def _(sc):
            s = Seq(p, sc, "sc")
            s.wait_tok("consts")
            for i in range(NT):
                s.wait_rel(f"ld{i % 2}", 16 * (i // 2 + 1))
                s.wait_tok(f"statfreev{i}", may_miss=True)
                s.op("activation", sqd[:, 0:D], xt[:, i % 2], AF.Square, accum_out=stats[:, 1:2])
                s.mark(f"ssq{i}")
                s.wait_tok(f"var{i}")
                s.op("activation", stats[:, 3:4], stats[:, 4:5], AF.Ln, bias=p.cc(LN_EPS), scale=1.0)
                s.op("activation", stats[:, 3:4], stats[:, 3:4], AF.Exp, bias=0.0, scale=-0.5)
                s.mark(f"rstd{i}")

        @blk.vector
        def _(ve):
            v = Seq(p, ve, "v")
            v.wait_rel("ldc", 64)
            for i in range(NT):
                v.wait_rel(f"ld{i % 2}", 16 * (i // 2 + 1))
                v.wait_tok(f"ssq{i}")
                v.op("reduce_sum", stats[:, 0:1], xt[:, i % 2], axis=AX.X)
                v.op("tensor_scalar", stats[:, 2:3], stats[:, 0:1], 1.0 / D, None, ALU.mult)
                v.op("tensor_scalar", stats[:, 5:6], stats[:, 2:3], stats[:, 2:3], None, ALU.mult)
                v.op("tensor_scalar", stats[:, 4:5], stats[:, 1:2], 1.0 / D, stats[:, 5:6],
                     ALU.mult, ALU.subtract)
                v.mark(f"var{i}")
                v.mark(f"statfreev{i + 1}")
                v.wait_tok(f"rstd{i}")
                v.op("tensor_scalar", sqd[:, D:2 * D], xt[:, i % 2], stats[:, 2:3], stats[:, 3:4],
                     ALU.subtract, ALU.mult)
                v.mark(f"xtfree{i + 2}")
                v.op("tensor_tensor", sqd[:, D:2 * D], sqd[:, D:2 * D], gamb[:, 0], ALU.mult)
                v.wait_tok(f"tpb{i - 2}")
                v.op("tensor_tensor", ht16[:, i % 2], sqd[:, D:2 * D], gamb[:, 1], ALU.add)
                v.mark(f"h16{i}")
                v.wait_tok(f"tpa{i}")
                v.op("tensor_copy", hf[:, 0:4, i * 128:(i + 1) * 128],
                     pt[:].rearrange("p (c t) -> p c t", t=128))
                v.mark(f"tpaev{i}")
                v.wait_tok(f"tpb{i}")
                v.op("tensor_copy", hf[:, 4:8, i * 128:(i + 1) * 128],
                     pt[:].rearrange("p (c t) -> p c t", t=128))
                v.mark(f"tpbev{i}")

        @blk.tensor
        def _(te):
            t = Seq(p, te, "t")
            t.wait_tok("identdone")
            for i in range(NT):
                t.wait_tok(f"h16{i}")
                t.wait_tok(f"tpbev{i - 1}")
                for c in range(4):
                    inst = te.transpose(pt[:, c * 128:(c + 1) * 128],
                                        ht16[:, i % 2, c * 128:(c + 1) * 128], ident[:])
                inst.then_inc(p.sem("t"), 1); t.n += 1; p.counts["t"] = t.n
                t.mark(f"tpa{i}")
                t.wait_tok(f"tpaev{i}")
                for c in range(4):
                    inst = te.transpose(pt[:, c * 128:(c + 1) * 128],
                                        ht16[:, i % 2, (4 + c) * 128:(5 + c) * 128], ident[:])
                inst.then_inc(p.sem("t"), 1); t.n += 1; p.counts["t"] = t.n
                t.mark(f"tpb{i}")

    with nc.Block() as blk2:
        @blk2.tensor
        def _(te):
            t = Seq(p, te, "t2")
            t.wait_tok("wload")
            t.wait_tok(f"tpbev{NT - 1}")
            gi = 0
            for i in range(NT):
                for proj in range(3):
                    for nn in range(2):
                        bank = (pa, pb)[gi % 2]
                        t.wait_tok(f"pev{gi - 2}")
                        for dc in range(DC):
                            inst = te.matmul(bank[:], hf[:, dc, i * 128:(i + 1) * 128],
                                             w3[:, proj, dc, nn * 512:(nn + 1) * 512],
                                             start=(dc == 0), stop=(dc == DC - 1))
                        inst.then_inc(p.sem("t2"), 1); t.n += 1; p.counts["t2"] = t.n
                        t.mark(f"p{gi}")
                        gi += 1

        @blk2.vector
        def _(ve):
            v = Seq(p, ve, "v2")
            gi = 0
            for i in range(NT):
                for proj in range(3):
                    for nn in range(2):
                        bank = (pa, pb)[gi % 2]
                        v.wait_tok(f"p{gi}")
                        if proj < 2:
                            v.wait_tok(f"qkst{i - 2}", may_miss=True)
                            v.op("tensor_copy", qkt[:, i % 2, proj * D + nn * 512:proj * D + (nn + 1) * 512],
                                 bank[:])
                        else:
                            v.wait_tok(f"vst{i - 2}", may_miss=True)
                            v.op("tensor_copy", vt16[:, i % 2, nn * 512:(nn + 1) * 512], bank[:])
                            if nn == 1:
                                v.mark(f"v16{i}")
                        v.mark(f"pev{gi}")
                        gi += 1
                v.wait_tok(f"sq{i}")
                v.op("tensor_reduce", ssq32[:, i % 2], sqd[:].rearrange("p (g w) -> p g w", w=64),
                     AX.X, ALU.add)
                v.mark(f"ssqred{i}")
                v.wait_tok(f"rln{i}")
                v.op("tensor_tensor", qkt[:, i % 2].rearrange("p (g w) -> p g w", w=64),
                     qkt[:, i % 2].rearrange("p (g w) -> p g w", w=64),
                     ssq32[:, i % 2].to_broadcast([128, 32, 64]), ALU.mult)
                v.op("tensor_tensor", qk16[:, i % 2], qkt[:, i % 2], qks[:], ALU.mult)
                v.mark(f"qk16{i}")

        @blk2.scalar
        def _(sc):
            s = Seq(p, sc, "sc2")
            for i in range(NT):
                s.wait_tok(f"pev{i * 6 + 3}")
                s.op("activation", sqd[:], qkt[:, i % 2], AF.Square)
                s.mark(f"sq{i}")
                s.wait_tok(f"ssqred{i}")
                s.op("activation", ssq32[:, i % 2], ssq32[:, i % 2], AF.Ln, bias=p.cc(L2_EPS), scale=1.0)
                s.op("activation", ssq32[:, i % 2], ssq32[:, i % 2], AF.Exp, bias=0.0, scale=-0.5)
                s.mark(f"rln{i}")

        @blk2.gpsimd
        def _(g):
            gg = Seq(p, g, "g3")
            for i in range(NT):
                gg.wait_tok(f"qk16{i}")
                p.dma(g, f"sq{i % 2}", qh_o.rearrange("(n p) d -> n p d", p=128)[i], qk16[:, i % 2, 0:D])
                p.dma(g, f"sq{i % 2}", kh_o.rearrange("(n p) d -> n p d", p=128)[i], qk16[:, i % 2, D:2 * D])
                p.dtok(f"qkst{i}", f"sq{i % 2}")
                gg.wait_tok(f"v16{i}")
                p.dma(g, f"sv{i % 2}", vh_o.rearrange("(n p) d -> n p d", p=128)[i], vt16[:, i % 2])
                p.dtok(f"vst{i}", f"sv{i % 2}")

        @blk2.sync
        def _(sy):
            s = Seq(p, sy, "syf")
            for nm in ("sq0", "sq1", "sv0", "sv1"):
                if p.dmac.get(p.rsem(nm)):
                    s.wait(nm, p.dmac[p.rsem(nm)])


def build_pre0b(TL):
    return build_twice(emit_pre0b, TL)



QKSCALE = 10.0


def emit_attn0(p, SS, NH=8, scale=QKSCALE, gated=False, sel=False):
    NQT = SS // 128
    NPAIR = NH // 2
    nc = p.nc

    q_i = p.din("q", [SS, NH * 64], BF16)
    k_i = p.din("k", [SS, NH * 64], BF16)
    v_i = p.din("v", [SS, NH * 64], BF16)
    wo_i = p.din("wo", [NH * 64, D])
    if gated:
        go_i = p.din("go", [SS, NH * 64], BF16)
    if sel:
        qI_i = p.din("qIs", [SS, 64])
        kI_i = p.din("kIs", [SS, 64])
        thr_i = p.din("thr", [SS, NH])
    yT_o = p.dout("yT", [2, D, SS // 2])

    qf = p.sb("qf", [128, 2, NPAIR, 128], BF16)
    kf = p.sb("kf", [128, NPAIR, SS], BF16)
    vt = p.sb("vt", [128, SS // 128, NH * 64], BF16)
    gof = p.sb("gof", [128, NPAIR, SS], BF16) if gated else None
    wos = p.sb("wos", [64, NH, D], BF16)
    ssb = p.sb("ssb", [128, SS], F32)
    p16 = p.sb("p16", [128, SS], BF16)
    pT = p.sb("pT", [128, SS // 128, 128], BF16)
    racc = p.sb("racc", [128, 2, 8], F32)
    rst = p.sb("rst", [128, 4], F32)
    poS = p.sb("poS", [64, 128], BF16)
    yev = p.sb("yev", [128, 2, DC * 128], F32)
    ident = p.sb("ident", [128, 128], BF16)
    cmask = p.sb("cmask", [128, 4, 512], F32)
    if sel:
        qIt = p.sb("qIt", [128, 2, 64], F32)
        kIt = p.sb("kIt", [128, 2, 64], F32)
        qIf = p.sb("qIf", [64, SS], F32)
        kIf = p.sb("kIf", [64, SS], F32)
        thrs = p.sb("thrs", [128, SS // 128, NH], F32)
        madd = p.sb("madd", [128, 512], F32)
        ident32 = p.sb("ident32", [128, 128], F32)

    ps1 = p.ps("ps1", [128, 512], F32)
    ps2 = p.ps("ps2", [128, 512], F32)
    ptt = p.ps("ptt", [128, 1024], BF16)
    po = p.ps("po", [64, 128], F32)
    py = p.ps("py", [128, DC * 128], F32)
    if sel:
        pm = p.ps("pm", [128, 512], F32)
        ptq = p.ps("ptq", [64, 512], F32)

    NKB = lambda qt: qt // 4 + 1

    with nc.Block() as blk:
        @blk.sync
        def _(sy):
            for pc in range(NPAIR):
                p.dma(sy, "ldq", kf[:, pc], k_i[:, pc * 128:(pc + 1) * 128], transpose=True)
                if gated:
                    p.dma(sy, "ldq", gof[:, pc], go_i[:, pc * 128:(pc + 1) * 128], transpose=True)
            p.dma(sy, "ldq", vt[:], v_i.rearrange("(c p) n -> p c n", p=128))
            if sel:
                p.dma(sy, "ldq", thrs[:], thr_i.rearrange("(n p) h -> p n h", p=128))
            p.dtok("qkvload", "ldq")
            if sel:
                for tt in range(SS // 128):
                    s = Seq(p, sy, "sy")
                    s.wait_tok(f"qkI{tt - 2}")
                    p.dma(sy, f"lqi{tt % 2}", qIt[:, tt % 2], qI_i.rearrange("(n p) c -> n p c", p=128)[tt])
                    p.dma(sy, f"lqi{tt % 2}", kIt[:, tt % 2], kI_i.rearrange("(n p) c -> n p c", p=128)[tt])
            for qt in range(NQT):
                s = Seq(p, sy, "sy")
                s.wait_tok(f"qdone{qt - 2}")
                for pc in range(NPAIR):
                    p.dma(sy, f"lqt{qt % 2}", qf[:, qt % 2, pc],
                          q_i[qt * 128:(qt + 1) * 128, pc * 128:(pc + 1) * 128], transpose=True)

        @blk.gpsimd
        def _(g):
            gg = Seq(p, g, "g")
            for h in range(NH):
                p.dma(g, "ldw", wos[:, h], wo_i[h * 64:(h + 1) * 64, :])
            p.dtok("wload", "ldw")
            gg.op("memset", ident[:], 1.0)
            gg.op("affine_select", ident[:], ident[:], [[-1, 128]], ALU.is_equal, 0.0,
                  base=0, channel_multiplier=1)
            if sel:
                gg.op("memset", ident32[:], 1.0)
                gg.op("affine_select", ident32[:], ident32[:], [[-1, 128]], ALU.is_equal, 0.0,
                      base=0, channel_multiplier=1)
            gg.op("memset", cmask[:], 0.0)
            for m_ in range(4):
                gg.op("affine_select", cmask[:, m_], cmask[:, m_], [[-1, 512]], ALU.is_ge, -1e30,
                      base=128 * m_, channel_multiplier=1)
            gg.mark("identdone")
            p.emit_consts(gg)
            for qt in range(NQT):
                gg.wait_tok(f"yev{qt}")
                _r = (qt * 128) // (SS // 2)
                _off = qt * 128 - _r * (SS // 2)
                p.dma(g, f"sy{qt % 2}", yT_o[_r].rearrange("(c q) s -> q c s", q=128)[:, :, _off:_off + 128],
                      yev[:, qt % 2].rearrange("p (c t) -> p c t", t=128))
                p.dtok(f"yst{qt}", f"sy{qt % 2}")

        @blk.tensor
        def _(te):
            t = Seq(p, te, "t")
            t.wait_tok("identdone")
            t.wait_tok("qkvload")
            t.wait_tok("wload")
            if sel:
                for tt in range(SS // 128):
                    t.wait_rel(f"lqi{tt % 2}", 32 * (tt // 2 + 1))
                    t.wait_tok(f"qkIev{tt - 1}")
                    inst = te.transpose(ptq[:, 0:128], qIt[:, tt % 2, :], ident32[:])
                    inst.then_inc(p.sem("t"), 1); t.n += 1; p.counts["t"] = t.n
                    inst = te.transpose(ptq[:, 128:256], kIt[:, tt % 2, :], ident32[:])
                    inst.then_inc(p.sem("t"), 1); t.n += 1; p.counts["t"] = t.n
                    t.mark(f"qkI{tt}")
                t.wait_tok(f"qkIev{SS // 128 - 1}")
            p._pmlast = None
            for qt in range(NQT):
                nb = NKB(qt)
                for h in range(NH):
                    pc, ho = h // 2, (h % 2) * 64
                    for kb in range(nb):
                        bank = (ps1, ps2)[kb % 2]
                        t.wait_tok(f"sev{qt}_{h}_{kb - 2}")
                        if h == 0 and kb == 0:
                            t.wait_rel(f"lqt{qt % 2}", 16 * NPAIR * (qt // 2 + 1))
                        inst = te.matmul(bank[:], qf[ho:ho + 64, qt % 2, pc, :],
                                         kf[ho:ho + 64, pc, kb * 512:(kb + 1) * 512],
                                         start=True, stop=True)
                        inst.then_inc(p.sem("t"), 1); t.n += 1; p.counts["t"] = t.n
                        t.mark(f"s{qt}_{h}_{kb}")
                        if h == NH - 1 and kb == nb - 1:
                            t.mark(f"qdone{qt}")
                        if sel:
                            il = h // 4
                            if p._pmlast is not None:
                                t.wait_tok(p._pmlast)
                            inst = te.matmul(pm[:], qIf[il * 32:(il + 1) * 32, qt * 128:(qt + 1) * 128],
                                             kIf[il * 32:(il + 1) * 32, kb * 512:(kb + 1) * 512],
                                             start=True, stop=True)
                            inst.then_inc(p.sem("t"), 1); t.n += 1; p.counts["t"] = t.n
                            t.mark(f"mp{qt}_{h}_{kb}")
                            p._pmlast = f"mpev{qt}_{h}_{kb}"
                    nch = (qt + 1)
                    t.wait_tok(f"pnorm{qt}_{h}")
                    for grp in range((nch + 7) // 8):
                        c0, c1 = grp * 8, min(grp * 8 + 8, nch)
                        t.wait_tok(f"ptev{qt}_{h}_{grp - 1}")
                        for c in range(c0, c1):
                            inst = te.transpose(ptt[:, (c - c0) * 128:(c - c0 + 1) * 128],
                                                p16[:, c * 128:(c + 1) * 128], ident[:])
                        inst.then_inc(p.sem("t"), 1); t.n += 1; p.counts["t"] = t.n
                        t.mark(f"pt{qt}_{h}_{grp}")
                    t.wait_tok(f"ptev{qt}_{h}_{(nch + 7) // 8 - 1}")
                    t.wait_tok(f"poev{qt}_{h - 1}")
                    for c in range(nch):
                        inst = te.matmul(po[:], vt[:, c, h * 64:(h + 1) * 64], pT[:, c, :],
                                         start=(c == 0), stop=(c == nch - 1))
                    inst.then_inc(p.sem("t"), 1); t.n += 1; p.counts["t"] = t.n
                    t.mark(f"po{qt}_{h}")
                    t.wait_tok(f"poS{qt}_{h}")
                    if h == 0:
                        t.wait_tok(f"yev{qt - 1}")
                    for dco in range(DC):
                        inst = te.matmul(py[:, dco * 128:(dco + 1) * 128], wos[:, h, dco * 128:(dco + 1) * 128],
                                         poS[:], start=(h == 0 and dco % 4 == 0), stop=(h == NH - 1),
                                         skip_group_check=True)
                    inst.then_inc(p.sem("t"), 1); t.n += 1; p.counts["t"] = t.n
                    t.mark(f"y{qt}_{h}")

        @blk.vector
        def _(ve):
            v = Seq(p, ve, "v")
            if sel:
                for tt in range(SS // 128):
                    v.wait_tok(f"qkI{tt}")
                    v.op("tensor_copy", qIf[:, tt * 128:(tt + 1) * 128], ptq[0:64, 0:128])
                    v.op("tensor_copy", kIf[:, tt * 128:(tt + 1) * 128], ptq[0:64, 128:256])
                    v.mark(f"qkIev{tt}")
            for qt in range(NQT):
                nb = NKB(qt)
                dkb = qt // 4
                for h in range(NH):
                    prev = f"{qt}_{h - 1}" if h > 0 else (f"{qt - 1}_{NH - 1}" if qt > 0 else None)
                    if prev is not None:
                        v.wait_tok(f"expd{prev}")
                    for kb in range(nb):
                        bank = (ps1, ps2)[kb % 2]
                        v.wait_tok(f"s{qt}_{h}_{kb}")
                        if sel:
                            v.wait_tok(f"mp{qt}_{h}_{kb}")
                            v.op("tensor_scalar", madd[:], pm[:], thrs[:, qt, h:h + 1],
                                 None, ALU.is_gt)
                            v.mark(f"mpev{qt}_{h}_{kb}")
                            if kb == 0:
                                v.op("memset", madd[:, 0:4], 1.0)
                            v.op("tensor_scalar", madd[:], madd[:], 1.0, 1e30, ALU.subtract, ALU.mult)
                            if kb == dkb:
                                v.op("tensor_tensor", madd[:], madd[:], cmask[:, qt % 4], ALU.add)
                            v.op("tensor_tensor", ssb[:, kb * 512:(kb + 1) * 512], bank[:],
                                 madd[:], ALU.add)
                            v.mark(f"sev{qt}_{h}_{kb}")
                            v.op("reduce_max", racc[:, 0, kb:kb + 1], ssb[:, kb * 512:(kb + 1) * 512],
                                 axis=AX.X)
                        elif kb == dkb:
                            v.op("tensor_tensor", ssb[:, kb * 512:(kb + 1) * 512], bank[:],
                                 cmask[:, qt % 4], ALU.add)
                            v.mark(f"sev{qt}_{h}_{kb}")
                            v.op("reduce_max", racc[:, 0, kb:kb + 1], ssb[:, kb * 512:(kb + 1) * 512],
                                 axis=AX.X)
                        else:
                            v.op("tensor_scalar", ssb[:, kb * 512:(kb + 1) * 512], bank[:],
                                 0.0, None, ALU.bypass, ALU.max, accum_out=racc[:, 0, kb:kb + 1])
                            v.mark(f"sev{qt}_{h}_{kb}")
                    v.mark(f"sev{qt}_{h}_x")
                    v.op("reduce_max", rst[:, 0:1], racc[:, 0, 0:nb], axis=AX.X)
                    v.op("tensor_scalar", rst[:, 1:2], rst[:, 0:1], -scale, None, ALU.mult)
                    v.mark(f"bias{qt}_{h}")
                    v.wait_tok(f"expd{qt}_{h}")
                    v.op("reduce_sum", rst[:, 2:3], racc[:, 1, 0:nb], axis=AX.X)
                    v.op("reciprocal", rst[:, 3:4], rst[:, 2:3])
                    v.op("tensor_scalar", p16[:, 0:nb * 512], p16[:, 0:nb * 512], rst[:, 3:4],
                         None, ALU.mult)
                    v.mark(f"pnorm{qt}_{h}")
                    if prev is not None:
                        v.wait_tok(f"po{prev}")
                    nch = qt + 1
                    for grp in range((nch + 7) // 8):
                        c0, c1 = grp * 8, min(grp * 8 + 8, nch)
                        v.wait_tok(f"pt{qt}_{h}_{grp}")
                        v.op("tensor_copy", pT[:, c0:c1, :],
                             ptt[:, 0:(c1 - c0) * 128].rearrange("p (c t) -> p c t", t=128))
                        v.mark(f"ptev{qt}_{h}_{grp}")
                    v.wait_tok(f"po{qt}_{h}")
                    if prev is not None:
                        v.wait_tok(f"y{prev}")
                    pc, ho = h // 2, (h % 2) * 64
                    if gated:
                        v.op("tensor_tensor", poS[:], po[:], gof[ho:ho + 64, pc, qt * 128:(qt + 1) * 128],
                             ALU.mult)
                    else:
                        v.op("tensor_copy", poS[:], po[:])
                    v.mark(f"poS{qt}_{h}")
                    v.mark(f"poev{qt}_{h}")
                v.wait_tok(f"y{qt}_{NH - 1}")
                v.wait_tok(f"yst{qt - 2}", may_miss=True)
                v.op("tensor_copy", yev[:, qt % 2], py[:])
                v.mark(f"yev{qt}")

        @blk.scalar
        def _(sc):
            s = Seq(p, sc, "sc")
            s.wait_tok("consts")
            for qt in range(NQT):
                nb = NKB(qt)
                for h in range(NH):
                    prevs = f"{qt}_{h - 1}" if h > 0 else (f"{qt - 1}_{NH - 1}" if qt > 0 else None)
                    if prevs is not None:
                        pq = qt if h > 0 else qt - 1
                        s.wait_tok(f"pt{prevs}_{(pq + 1 + 7) // 8 - 1}")
                    s.wait_tok(f"bias{qt}_{h}")
                    for kb in range(nb):
                        s.op("activation", p16[:, kb * 512:(kb + 1) * 512],
                             ssb[:, kb * 512:(kb + 1) * 512], AF.Exp, bias=rst[:, 1:2],
                             scale=scale, accum_out=racc[:, 1, kb:kb + 1])
                    s.mark(f"expd{qt}_{h}")

        @blk.sync
        def _(sy):
            s = Seq(p, sy, "syf")
            for nm in ("sy0", "sy1"):
                if p.dmac.get(p.rsem(nm)):
                    s.wait(nm, p.dmac[p.rsem(nm)])


def build_attn0(SS, NH=8, scale=QKSCALE, gated=False):
    return build_twice(emit_attn0, SS, NH, scale, gated)


def build_attn1(SS, NH=8):
    return build_twice(emit_attn0, SS, NH, 0.125, True, True)



def emit_post0(p, TL, presummed=False):
    NT = TL // 128
    _lysem = "lya%d"
    nc = p.nc

    xn = p.din("xn", [TL, D])
    eq_i = p.din("eq", [D, TL], BF16)
    ctxa = p.din("ctxa", [D, D + 1])
    ctxb = None if presummed else p.din("ctxb", [D, D + 1])
    wo_i = p.din("wo", [D, D])
    bo_i = p.din("bo", [1, D])
    gam_i = p.din("gam", [1, D]); bet_i = p.din("bet", [1, D])
    ya_i = p.din("ya", [D, TL])
    yb_i = None if presummed else p.din("yb", [D, TL])
    lng_i = p.din("lng", [1, D]); lnb_i = p.din("lnb", [1, D])
    s0_o = p.dout("s0", [TL, D])
    h2_o = p.dout("h2", [TL, D], BF16)

    eqf = p.sb("eqf", [128, DC, TL], BF16)
    cnb = p.sb("cnb", [128, DC, D], BF16)
    cnt_ = p.sb("cnt_", [128, DC, D + 1], F32)
    araw = p.sb("araw", [128, DC, TL], BF16)
    woL = p.sb("woL", [128, DC, D], BF16)
    cons = p.sb("cons", [128, 5, D], F32)
    rseq = p.sb("rseq", [128, NT], F32)
    rr = p.sb("rr", [128, NT], F32)
    onesb = p.sb("onesb", [128, 1], BF16)
    xt = p.sb("xt", [128, 2, D], F32)
    o0 = p.sb("o0", [128, 2, D], F32)
    yab = p.sb("yab", [128, 2, 2, DC * 128], F32)
    yft16 = p.sb("yft16", [128, 2, D], F32)
    h2t = p.sb("h2t", [128, 2, D], F32)
    h16 = p.sb("h16", [128, 2, D], BF16)
    sqd = p.sb("sqd", [128, D], F32)
    stats = p.sb("stats", [128, 8], F32)
    ident32 = p.sb("ident32", [128, 128], F32)

    pa = p.ps("pa", [128, 512], F32)
    pb = p.ps("pb", [128, 512], F32)
    pr = p.ps("pr", [128, 16], F32)
    pt1 = p.ps("pt1", [128, 512], F32)
    pt2 = p.ps("pt2", [128, 512], F32)

    with nc.Block() as blk:
        @blk.sync
        def _(sy):
            p.dma(sy, "ldc", cons[:, 0], bo_i.to_broadcast([128, D]))
            p.dma(sy, "ldc", cons[:, 1], gam_i.to_broadcast([128, D]))
            p.dma(sy, "ldc", cons[:, 2], bet_i.to_broadcast([128, D]))
            p.dma(sy, "ldc", cons[:, 3], lng_i.to_broadcast([128, D]))
            p.dma(sy, "ldc", cons[:, 4], lnb_i.to_broadcast([128, D]))
            p.dma(sy, "ldq", eqf[:], eq_i.rearrange("(c p) t -> p c t", p=128))
            p.dma(sy, "ldq", cnt_[:], ctxa.rearrange("(c p) e -> p c e", p=128))
            p.dtok("ld0", "ldq")
            for i in range(NT):
                s = Seq(p, sy, "sy")
                s.wait_tok(f"xtfree{i}", may_miss=True)
                p.dma(sy, f"ldx{i % 2}", xt[:, i % 2], xn.rearrange("(n p) d -> n p d", p=128)[i])

        @blk.gpsimd
        def _(g):
            gg = Seq(p, g, "g")
            for c in range(DC):
                p.dma(g, "ldw", woL[:, c], wo_i[c * 128:(c + 1) * 128, :])
            p.dtok("wload", "ldw")
            gg.op("memset", ident32[:], 1.0)
            gg.op("affine_select", ident32[:], ident32[:], [[-1, 128]], ALU.is_equal, 0.0,
                  base=0, channel_multiplier=1)
            gg.op("memset", onesb[:], 1.0)
            gg.mark("identdone")
            p.emit_consts(gg)
            if not presummed:
                g.wait_ge(p.sem("ldq"), p.pbase.get(p.rsem("ldq"), 0) + 32)
                p.dma(g, "ldq2", cnt_[:], ctxb.rearrange("(c p) e -> p c e", p=128),
                      accum_op=ALU.add)
            for i in range(NT):
                gg.wait_tok(f"s0d{i}")
                p.dma(g, f"ss{i % 2}", s0_o.rearrange("(n p) d -> n p d", p=128)[i], o0[:, i % 2])
                p.dtok(f"s0st{i}", f"ss{i % 2}")
            for i in range(NT + 2):
                if i < NT:
                    gg.wait_tok(f"yabfree{i}", may_miss=True)
                    gg.wait_tok(f"ytpA{i - 1}")
                    p.dma(g, f"lya{i % 2}", yab[:, i % 2, 0].rearrange("p (c t) -> p c t", t=128),
                          ya_i[:, i * 128:(i + 1) * 128].rearrange("(c p) t -> p c t", p=128))
                    if not presummed:
                        p.dma(g, f"lya{i % 2}", yab[:, i % 2, 1].rearrange("p (c t) -> p c t", t=128),
                              yb_i[:, i * 128:(i + 1) * 128].rearrange("(c p) t -> p c t", p=128))
                if i >= 2:
                    gg.wait_tok(f"h16d{i - 2}")
                    p.dma(g, f"sh{i % 2}", h2_o.rearrange("(n p) d -> n p d", p=128)[i - 2], h16[:, (i - 2) % 2])
                    p.dtok(f"h2st{i - 2}", f"sh{i % 2}")

        @blk.vector
        def _(ve):
            v = Seq(p, ve, "v")
            v.wait("ldq", p.dmac.get(p.rsem("ldq"), 0))
            if not presummed:
                v.wait_rel("ldq2", 16)
            v.wait("ldc", p.dmac.get(p.rsem("ldc"), 0))
            for dc in range(DC):
                v.op("reciprocal", stats[:, 0:1], cnt_[:, dc, D:D + 1])
                v.op("tensor_scalar", cnb[:, dc], cnt_[:, dc, 0:D], stats[:, 0:1], None, ALU.mult)
            v.mark("cnready")
            gi = 0
            for ec in range(DC):
                for ts in range(TL // 512):
                    bank = (pa, pb)[gi % 2]
                    v.wait_tok(f"ar{gi}")
                    v.op("tensor_copy", araw[:, ec, ts * 512:(ts + 1) * 512], bank[:])
                    v.mark(f"arev{gi}")
                    gi += 1
            v.wait_tok("rseqd")
            v.op("tensor_copy", rseq[:], pr[:, 0:NT])
            v.op("reciprocal", rr[:], rseq[:])
            v.op("tensor_scalar", rr[:], rr[:], 1.0 / 32.0, None, ALU.mult)
            v.mark("rrready")
            for i in range(NT):
                v.wait_tok(f"s0st{i - 2}", may_miss=True)
                for nn in range(2):
                    bank = (pa, pb)[(i * 2 + nn) % 2]
                    v.wait_tok(f"o0_{i}_{nn}")
                    v.op("tensor_scalar", o0[:, i % 2, nn * 512:(nn + 1) * 512], bank[:],
                         rr[:, i:i + 1], None, ALU.mult)
                    v.mark(f"o0ev{i * 2 + nn}")
                v.wait_rel(f"ldx{i % 2}", 16 * (i // 2 + 1))
                v.op("tensor_add", o0[:, i % 2], o0[:, i % 2], cons[:, 0])
                v.op("tensor_add", o0[:, i % 2], o0[:, i % 2], xt[:, i % 2])
                v.mark(f"xtfree{i + 2}")
                v.mark(f"r0{i}")
                v.wait_tok(f"ssq{i}")
                v.op("reduce_sum", stats[:, 0:1], o0[:, i % 2], axis=AX.X)
                v.op("tensor_scalar", stats[:, 2:3], stats[:, 0:1], 1.0 / D, None, ALU.mult)
                v.op("tensor_scalar", stats[:, 5:6], stats[:, 2:3], stats[:, 2:3], None, ALU.mult)
                v.op("tensor_scalar", stats[:, 4:5], stats[:, 1:2], 1.0 / D, stats[:, 5:6],
                     ALU.mult, ALU.subtract)
                v.mark(f"var{i}")
                v.wait_tok(f"rstd{i}")
                v.op("tensor_scalar", o0[:, i % 2], o0[:, i % 2], stats[:, 2:3], stats[:, 3:4],
                     ALU.subtract, ALU.mult)
                v.op("tensor_tensor", o0[:, i % 2], o0[:, i % 2], cons[:, 1], ALU.mult)
                v.op("tensor_tensor", o0[:, i % 2], o0[:, i % 2], cons[:, 2], ALU.add)
                v.mark(f"s0d{i}")
            for i in range(NT):
                if presummed:
                    v.wait_rel(f"lya{i % 2}", 16 * (i // 2 + 1))
                else:
                    v.wait_rel(f"lya{i % 2}", 32 * (i // 2 + 1))
                    v.op("tensor_add", yab[:, i % 2, 0], yab[:, i % 2, 0], yab[:, i % 2, 1])
                v.mark(f"ysum{i}")
                v.wait_tok(f"ytpA{i}")
                v.op("tensor_copy", yft16[:, i % 2, 0:512], pt1[:])
                v.mark(f"ytpeA{i}")
                v.wait_tok(f"ytpB{i}")
                v.op("tensor_copy", yft16[:, i % 2, 512:1024], pt2[:])
                v.mark(f"yabfree{i + 2}")
                v.mark(f"ytpeB{i}")
                v.wait_tok(f"hssq{i}")
                v.op("reduce_sum", stats[:, 0:1], yft16[:, i % 2], axis=AX.X)
                v.op("tensor_scalar", stats[:, 2:3], stats[:, 0:1], 1.0 / D, None, ALU.mult)
                v.op("tensor_scalar", stats[:, 5:6], stats[:, 2:3], stats[:, 2:3], None, ALU.mult)
                v.op("tensor_scalar", stats[:, 4:5], stats[:, 1:2], 1.0 / D, stats[:, 5:6],
                     ALU.mult, ALU.subtract)
                v.mark(f"hvar{i}")
                v.wait_tok(f"hrstd{i}")
                v.op("tensor_scalar", h2t[:, i % 2], yft16[:, i % 2], stats[:, 2:3], stats[:, 3:4],
                     ALU.subtract, ALU.mult)
                v.op("tensor_tensor", h2t[:, i % 2], h2t[:, i % 2], cons[:, 3], ALU.mult)
                v.wait_tok(f"h2st{i - 2}", may_miss=True)
                v.op("tensor_tensor", h16[:, i % 2], h2t[:, i % 2], cons[:, 4], ALU.add)
                v.mark(f"h16d{i}")

        @blk.tensor
        def _(te):
            t = Seq(p, te, "t")
            t.wait_tok("identdone")
            t.wait_tok("cnready")
            gi = 0
            for ec in range(DC):
                for ts in range(TL // 512):
                    t.wait_tok(f"arev{gi - 2}")
                    bank = (pa, pb)[gi % 2]
                    for dc in range(DC):
                        inst = te.matmul(bank[:], cnb[:, dc, ec * 128:(ec + 1) * 128],
                                         eqf[:, dc, ts * 512:(ts + 1) * 512],
                                         start=(dc == 0), stop=(dc == DC - 1))
                    inst.then_inc(p.sem("t"), 1); t.n += 1; p.counts["t"] = t.n
                    t.mark(f"ar{gi}")
                    gi += 1
            for i in range(NT):
                for dc in range(DC):
                    inst = te.matmul(pr[:, i % 16:i % 16 + 1], eqf[:, dc, i * 128:(i + 1) * 128],
                                     onesb[:], start=(dc == 0), stop=(dc == DC - 1),
                                     skip_group_check=True)
                inst.then_inc(p.sem("t"), 1); t.n += 1; p.counts["t"] = t.n
            t.mark("rseqd")
            t.wait_tok("wload")
            t.wait_tok(f"arev{DC * (TL // 512) - 1}")
            for i in range(NT):
                for nn in range(2):
                    gi2 = i * 2 + nn
                    t.wait_tok(f"o0ev{gi2 - 2}")
                    bank = (pa, pb)[gi2 % 2]
                    for ec in range(DC):
                        inst = te.matmul(bank[:], araw[:, ec, i * 128:(i + 1) * 128],
                                         woL[:, ec, nn * 512:(nn + 1) * 512],
                                         start=(ec == 0), stop=(ec == DC - 1))
                    inst.then_inc(p.sem("t"), 1); t.n += 1; p.counts["t"] = t.n
                    t.mark(f"o0_{i}_{nn}")
            t.wait_tok("identdone")
            for i in range(NT):
                t.wait_tok(f"ysum{i}")
                t.wait_rel(_lysem % (i % 2), (16 if presummed else 32) * (i // 2 + 1))
                t.wait_tok(f"ytpeA{i - 1}")
                for c in range(4):
                    inst = te.transpose(pt1[:, c * 128:(c + 1) * 128],
                                        yab[:, i % 2, 0, c * 128:(c + 1) * 128], ident32[:])
                inst.then_inc(p.sem("t"), 1); t.n += 1; p.counts["t"] = t.n
                t.mark(f"ytpA{i}")
                t.wait_tok(f"ytpeB{i - 1}")
                for c in range(4):
                    inst = te.transpose(pt2[:, c * 128:(c + 1) * 128],
                                        yab[:, i % 2, 0, (4 + c) * 128:(5 + c) * 128], ident32[:])
                inst.then_inc(p.sem("t"), 1); t.n += 1; p.counts["t"] = t.n
                t.mark(f"ytpB{i}")

        @blk.scalar
        def _(sc):
            s = Seq(p, sc, "sc")
            s.wait_tok("consts")
            for i in range(NT):
                s.wait_tok(f"r0{i}")
                s.op("activation", sqd[:], o0[:, i % 2], AF.Square, accum_out=stats[:, 1:2])
                s.mark(f"ssq{i}")
                s.wait_tok(f"var{i}")
                s.op("activation", stats[:, 3:4], stats[:, 4:5], AF.Ln, bias=p.cc(LN_EPS), scale=1.0)
                s.op("activation", stats[:, 3:4], stats[:, 3:4], AF.Exp, bias=0.0, scale=-0.5)
                s.mark(f"rstd{i}")
            for i in range(NT):
                s.wait_tok(f"ytpeB{i}")
                s.op("activation", sqd[:], yft16[:, i % 2], AF.Square, accum_out=stats[:, 1:2])
                s.mark(f"hssq{i}")
                s.wait_tok(f"hvar{i}")
                s.op("activation", stats[:, 3:4], stats[:, 4:5], AF.Ln, bias=p.cc(LN_EPS), scale=1.0)
                s.op("activation", stats[:, 3:4], stats[:, 3:4], AF.Exp, bias=0.0, scale=-0.5)
                s.mark(f"hrstd{i}")

        @blk.sync
        def _(sy):
            s = Seq(p, sy, "syf")
            for nm in ("ss0", "ss1", "sh0", "sh1"):
                if p.dmac.get(p.rsem(nm)):
                    s.wait(nm, p.dmac[p.rsem(nm)])


def build_post0(TL):
    return build_twice(emit_post0, TL)



GC1 = 1.5957691216
GC2 = 0.07135480787
FF0 = 4096


def emit_moe0(p, TL):
    NT = TL // 128
    NTS = TL // 512
    nc = p.nc

    h2_i = p.din("h2", [TL, D], BF16)
    gw_i = p.din("gw", [D, 8])
    gb_i = p.din("gb", [1, 8])
    w1_i = p.din("w1", [8, D, FF0])
    b1_i = p.din("b1", [8, FF0])
    w2_i = p.din("w2", [8, FF0, D])
    b2_i = p.din("b2", [8, D])
    lng_i = p.din("lng", [1, D]); lnb_i = p.din("lnb", [1, D])
    x_i = p.din("x", [TL, D])
    s0_i = p.din("s0", [TL, D])
    x1_o = p.dout("x1", [TL, D])

    h2f = p.sb("h2f", [128, DC, TL], BF16)
    w1s = p.sb("w1s", [128, 2, DC, 512], BF16)
    w2s = p.sb("w2s", [128, 2, 4, D], BF16)
    b1q = p.sb("b1q", [1, 2, 512], BF16)
    b2s = p.sb("b2s", [8, D], BF16)
    gw16 = p.sb("gw16", [128, DC, 8], BF16)
    gbb = p.sb("gbb", [128, 8], F32)
    gate = p.sb("gate", [128, NT, 8], F32)
    gateT = p.sb("gateT", [8, NT, 128], BF16)
    ysb = p.sb("ysb", [128, NT, D], F32)
    hid16 = p.sb("hid16", [128, 2, 4, 512], BF16)
    sq1 = p.sb("sq1", [128, 512], F32)
    sq2 = p.sb("sq2", [128, 512], F32)
    onesr = p.sb("onesr", [1, 512], BF16)
    cons = p.sb("cons", [128, 2, D], F32)
    stats = p.sb("stats", [128, 8], F32)
    sqd = p.sb("sqd", [128, D], F32)
    xs0 = p.sb("xs0", [128, 2, 2, D], F32)
    g0t = p.sb("g0t", [128, 2, D], F32)
    ident32 = p.sb("ident32", [128, 128], F32)

    ph1 = p.ps("ph1", [128, 512], F32)
    ph2 = p.ps("ph2", [128, 512], F32)
    py1 = p.ps("py1", [128, 512], F32)
    py2 = p.ps("py2", [128, 512], F32)
    pg = p.ps("pg", [128, 128], F32)

    with nc.Block() as blk:
        @blk.sync
        def _(sy):
            for c in range(DC):
                p.dma(sy, "ldh", h2f[:, c], h2_i[:, c * 128:(c + 1) * 128], transpose=True)
            p.dma(sy, "ldc", gbb[:], gb_i.to_broadcast([128, 8]))
            p.dma(sy, "ldc", cons[:, 0], lng_i.to_broadcast([128, D]))
            p.dma(sy, "ldc", cons[:, 1], lnb_i.to_broadcast([128, D]))
            p.dtok("hload", "ldh")

        @blk.gpsimd
        def _(g):
            gg = Seq(p, g, "g")
            for c in range(DC):
                p.dma(g, "ldg", gw16[:, c], gw_i[c * 128:(c + 1) * 128, :])
            p.dma(g, "ldg", b2s[:], b2_i)
            p.dtok("gload", "ldg")
            gg.op("memset", onesr[:], 1.0)
            gg.op("memset", ident32[:], 1.0)
            gg.op("affine_select", ident32[:], ident32[:], [[-1, 128]], ALU.is_equal, 0.0,
                  base=0, channel_multiplier=1)
            gg.op("memset", ysb[:], 0.0)
            gg.mark("init")
            p.emit_consts(gg)
            for e in range(8):
                for q in range(8):
                    j = (e * 8 + q)
                    gg.wait_tok(f"wfree{j - 2}")
                    p.dma(g, f"lw{j % 2}", w1s[:, j % 2],
                          w1_i[e].rearrange("(c k) f -> k c f", k=128)[:, :, q * 512:(q + 1) * 512])
                    p.dma(g, f"lw{j % 2}", w2s[:, j % 2],
                          w2_i[e, q * 512:(q + 1) * 512].rearrange("(c k) f -> k c f", k=128))
                    p.dma(g, f"lw{j % 2}", b1q[:, j % 2], b1_i[e:e + 1, q * 512:(q + 1) * 512])
                    p.dtok(f"wld{j}", f"lw{j % 2}")
            gg.wait_tok("yready")
            for i in range(NT + 2):
                if i < NT:
                    gg.wait_tok(f"x1d{i - 2}")
                    p.dma(g, f"lx{i % 2}", xs0[:, i % 2, 0], x_i.rearrange("(n p) d -> n p d", p=128)[i])
                    p.dma(g, f"lx{i % 2}", xs0[:, i % 2, 1], s0_i.rearrange("(n p) d -> n p d", p=128)[i])
                if i >= 2:
                    gg.wait_tok(f"x1d{i - 2}")
                    p.dma(g, f"sx{i % 2}", x1_o.rearrange("(n p) d -> n p d", p=128)[i - 2], g0t[:, (i - 2) % 2])
                    p.dtok(f"x1st{i - 2}", f"sx{i % 2}")

        @blk.tensor
        def _(te):
            t = Seq(p, te, "t")
            t.wait_tok("init")
            t.wait_tok("hload")
            t.wait_tok("gload")
            for i in range(NT):
                t.wait_tok(f"gTev{i - 1}")
                for dc in range(DC):
                    inst = te.matmul(pg[:, 0:8], h2f[:, dc, i * 128:(i + 1) * 128],
                                     gw16[:, dc], start=(dc == 0), stop=(dc == DC - 1),
                                     skip_group_check=True)
                inst.then_inc(p.sem("t"), 1); t.n += 1; p.counts["t"] = t.n
                t.mark(f"glog{i}")
                t.wait_tok(f"gsm{i}")
                inst = te.transpose(pg[0:8, 0:128], gate[:, i], ident32[:])
                inst.then_inc(p.sem("t"), 1); t.n += 1; p.counts["t"] = t.n
                t.mark(f"gT{i}")
            for i in range(NT):
                t.wait_tok(f"gTev{i}")
                for nn in range(2):
                    jj = i * 2 + nn
                    t.wait_tok(f"b2ev{jj - 2}")
                    bank = (py1, py2)[jj % 2]
                    inst = te.matmul(bank[:], gateT[:, i, :], b2s[:, nn * 512:(nn + 1) * 512],
                                     start=True, stop=True)
                    inst.then_inc(p.sem("t"), 1); t.n += 1; p.counts["t"] = t.n
                    t.mark(f"b2_{i}_{nn}")
            t.wait_tok(f"b2ev{2 * NT - 1}")
            for e in range(8):
                for q in range(8):
                    j = e * 8 + q
                    t.wait_rel("lw%d" % (j % 2), 48 * (j // 2 + 1))
                    for ts in range(NTS):
                        for fc in range(4):
                            gidx = (ts * 4 + fc)
                            bank = (ph1, ph2)[gidx % 2]
                            if gidx >= 2:
                                t.wait_tok(f"hg{j}_{gidx - 2}")
                            elif j > 0:
                                t.wait_tok(f"hg{j - 1}_{NTS * 4 - 2 + gidx}")
                            for dc in range(DC):
                                te.matmul(bank[:], w1s[:, j % 2, dc, fc * 128:(fc + 1) * 128],
                                          h2f[:, dc, ts * 512:(ts + 1) * 512],
                                          start=(dc == 0), stop=False)
                            inst = te.matmul(bank[:], b1q[0:1, j % 2, fc * 128:(fc + 1) * 128],
                                             onesr[:], start=False, stop=True)
                            inst.then_inc(p.sem("t"), 1); t.n += 1; p.counts["t"] = t.n
                            t.mark(f"h{j}_{gidx}")
                        for sub in range(4):
                            i = ts * 4 + sub
                            for nn in range(2):
                                jj = (i * 2 + nn)
                                bank = (py1, py2)[jj % 2]
                                t.wait_tok(f"yev{j}_{jj - 2}")
                                t.wait_tok(f"hg{j}_{ts * 4 + 3}")
                                for fc in range(4):
                                    inst = te.matmul(bank[:], hid16[:, j % 2, fc, sub * 128:(sub + 1) * 128],
                                                     w2s[:, j % 2, fc, nn * 512:(nn + 1) * 512],
                                                     start=(fc == 0), stop=(fc == 3))
                                inst.then_inc(p.sem("t"), 1); t.n += 1; p.counts["t"] = t.n
                                t.mark(f"y{j}_{jj}")
                    t.mark(f"wfree{j}")

        @blk.vector
        def _(ve):
            v = Seq(p, ve, "v")
            v.wait_tok("init")
            v.wait_rel("ldc", 48)
            for i in range(NT):
                v.wait_tok(f"glog{i}")
                v.op("tensor_tensor", gate[:, i], pg[:, 0:8], gbb[:], ALU.add)
                v.mark(f"gev{i}")
                v.op("reduce_max", stats[:, 0:1], gate[:, i], axis=AX.X)
                v.op("tensor_scalar", stats[:, 1:2], stats[:, 0:1], -1.0, None, ALU.mult)
                v.mark(f"gmax{i}")
                v.wait_tok(f"gexp{i}")
                v.op("reciprocal", stats[:, 2:3], stats[:, 3:4])
                v.op("tensor_scalar", gate[:, i], gate[:, i], stats[:, 2:3], None, ALU.mult)
                v.mark(f"gsm{i}")
                v.wait_tok(f"gT{i}")
                v.op("tensor_copy", gateT[:, i, :], pg[0:8, 0:128])
                v.mark(f"gTev{i}")
            for i in range(NT):
                for nn in range(2):
                    jj = i * 2 + nn
                    bank = (py1, py2)[jj % 2]
                    v.wait_tok(f"b2_{i}_{nn}")
                    v.op("tensor_add", ysb[:, i, nn * 512:(nn + 1) * 512], bank[:],
                         ysb[:, i, nn * 512:(nn + 1) * 512])
                    v.mark(f"b2ev{jj}")
            for e in range(8):
                for q in range(8):
                    j = e * 8 + q
                    for ts in range(NTS):
                        for fc in range(4):
                            gidx = ts * 4 + fc
                            bank = (ph1, ph2)[gidx % 2]
                            v.wait_tok(f"h{j}_{gidx}")
                            v.wait_tok(f"sqf{j}_{gidx}")
                            v.op("tensor_scalar", sq1[:], sq1[:], GC2, GC1, ALU.mult, ALU.add)
                            v.op("tensor_tensor", sq1[:], sq1[:], bank[:], ALU.mult)
                            v.mark(f"t3{j}_{gidx}")
                            v.wait_tok(f"sig{j}_{gidx}")
                            if ts > 0:
                                v.wait_tok(f"y{j}_{ts * 8 - 1}")
                            elif j >= 2:
                                v.wait_tok(f"y{j - 2}_{NTS * 8 - 1}")
                            v.op("tensor_tensor", hid16[:, j % 2, fc, :], sq2[:], bank[:], ALU.mult)
                            v.mark(f"hg{j}_{gidx}")
                        for sub in range(4):
                            i = ts * 4 + sub
                            for nn in range(2):
                                jj = i * 2 + nn
                                bank = (py1, py2)[jj % 2]
                                v.wait_tok(f"y{j}_{jj}")
                                v.op("scalar_tensor_tensor", ysb[:, i, nn * 512:(nn + 1) * 512],
                                     bank[:], gate[:, i, e:e + 1], ysb[:, i, nn * 512:(nn + 1) * 512],
                                     ALU.mult, ALU.add)
                                v.mark(f"yev{j}_{jj}")
            v.mark("yready")
            for i in range(NT):
                v.wait_tok(f"essq{i}")
                v.op("reduce_sum", stats[:, 0:1], ysb[:, i], axis=AX.X)
                v.op("tensor_scalar", stats[:, 2:3], stats[:, 0:1], 1.0 / D, None, ALU.mult)
                v.op("tensor_scalar", stats[:, 5:6], stats[:, 2:3], stats[:, 2:3], None, ALU.mult)
                v.op("tensor_scalar", stats[:, 4:5], stats[:, 1:2], 1.0 / D, stats[:, 5:6],
                     ALU.mult, ALU.subtract)
                v.mark(f"evar{i}")
                v.wait_tok(f"erstd{i}")
                v.wait_tok(f"x1st{i - 2}", may_miss=True)
                v.op("tensor_scalar", g0t[:, i % 2], ysb[:, i], stats[:, 2:3], stats[:, 3:4],
                     ALU.subtract, ALU.mult)
                v.op("tensor_tensor", g0t[:, i % 2], g0t[:, i % 2], cons[:, 0], ALU.mult)
                v.op("tensor_tensor", g0t[:, i % 2], g0t[:, i % 2], cons[:, 1], ALU.add)
                v.mark(f"g0pre{i}")
                v.wait_tok(f"silu{i}")
                v.wait_rel(f"lx{i % 2}", 32 * (i // 2 + 1))
                v.op("tensor_tensor", g0t[:, i % 2], g0t[:, i % 2], xs0[:, i % 2, 1], ALU.mult)
                v.op("tensor_tensor", g0t[:, i % 2], g0t[:, i % 2], xs0[:, i % 2, 0], ALU.add)
                v.mark(f"x1d{i}")

        @blk.scalar
        def _(sc):
            s = Seq(p, sc, "sc")
            s.wait_tok("consts")
            for i in range(NT):
                s.wait_tok(f"gmax{i}")
                s.op("activation", gate[:, i], gate[:, i], AF.Exp, bias=stats[:, 1:2],
                     scale=1.0, accum_out=stats[:, 3:4])
                s.mark(f"gexp{i}")
            for e in range(8):
                for q in range(8):
                    j = e * 8 + q
                    for ts in range(NTS):
                        for fc in range(4):
                            gidx = ts * 4 + fc
                            bank = (ph1, ph2)[gidx % 2]
                            s.wait_tok(f"h{j}_{gidx}")
                            s.op("activation", sq1[:], bank[:], AF.Square)
                            s.mark(f"sqf{j}_{gidx}")
                            s.wait_tok(f"t3{j}_{gidx}")
                            s.op("activation", sq2[:], sq1[:], AF.Sigmoid)
                            s.mark(f"sig{j}_{gidx}")
                            s.mark(f"hev{j}_{gidx}")
            s.wait_tok("yready")
            for i in range(NT):
                s.wait_tok(f"evar{i - 1}")
                s.op("activation", sqd[:], ysb[:, i], AF.Square, accum_out=stats[:, 1:2])
                s.mark(f"essq{i}")
                s.wait_tok(f"evar{i}")
                s.op("activation", stats[:, 3:4], stats[:, 4:5], AF.Ln, bias=p.cc(LN_EPS), scale=1.0)
                s.op("activation", stats[:, 3:4], stats[:, 3:4], AF.Exp, bias=0.0, scale=-0.5)
                s.mark(f"erstd{i}")
                s.wait_tok(f"g0pre{i}")
                s.op("activation", g0t[:, i % 2], g0t[:, i % 2], AF.Silu)
                s.mark(f"silu{i}")

        @blk.sync
        def _(sy):
            s = Seq(p, sy, "syf")
            for nm in ("sx0", "sx1"):
                if p.dmac.get(p.rsem(nm)):
                    s.wait(nm, p.dmac[p.rsem(nm)])


def build_moe0(TL):
    return build_twice(emit_moe0, TL)



def emit_pre1(p, TL):
    NT = TL // 128
    nc = p.nc

    x_i = p.din("x1", [TL, D])
    wI_i = p.din("wI", [D, 268])
    rb_i = p.din("rb", [1, 8])
    w5_i = p.din("w5", [5, D, D])
    ch_i = p.din("ch", [TL, 32])
    sh_i = p.din("sh", [TL, 32])
    xn1b_o = p.dout("xn1b", [TL, D], BF16)
    qI_o = p.dout("qI", [TL, 128])
    kI_o = p.dout("kI", [TL, 128])
    wsig_o = p.dout("wsig", [TL, 4])
    eps_o = p.dout("eps", [128, TL // 128, 2])
    rq_o = p.dout("rq", [TL, D], BF16)
    rk_o = p.dout("rk", [TL, D], BF16)
    vg_o = p.dout("vg", [TL, D], BF16)
    gos_o = p.dout("gos", [TL, D], BF16)

    xt = p.sb("xt", [128, 2, D], F32)
    xnt = p.sb("xnt", [128, 2, D], F32)
    xn16 = p.sb("xn16", [128, 2, D], BF16)
    xf32 = p.sb("xf32", [128, 2, DC, 128], F32)
    xf16 = p.sb("xf16", [128, 2, DC, 128], BF16)
    wI = p.sb("wIs", [128, DC, 268], F32)
    w5 = p.sb("w5s", [128, 5, DC, D], BF16)
    rbb = p.sb("rbb", [128, 8], F32)
    chs = p.sb("chs", [128, NT, 64], F32)
    sel = p.sb("sel", [128, 2, 268], F32)
    selo = p.sb("selo", [128, 2, 144], F32)
    eidx = p.sb("eidx", [128, 8], mybir.dt.uint32)
    emax = p.sb("emax", [128, 8], F32)
    small = p.sb("small", [128, 2, 8], F32)
    qkf = p.sb("qkf", [128, 2, 2048], F32)
    rop = p.sb("rop", [128, 2048], F32)
    rqk16 = p.sb("rqk16", [128, 2, 2048], BF16)
    vv = p.sb("vv", [128, D], F32)
    gg_ = p.sb("gg_", [128, D], F32)
    vg16 = p.sb("vg16", [128, 2, D], BF16)
    go16 = p.sb("go16", [128, 2, D], BF16)
    sqd = p.sb("sqd", [128, D], F32)
    stats = p.sb("stats", [128, 8], F32)
    ident32 = p.sb("ident32", [128, 128], F32)

    pa = p.ps("pa", [128, 512], F32)
    pb = p.ps("pb", [128, 512], F32)
    pt1 = p.ps("pt1", [128, 512], F32)
    pt2 = p.ps("pt2", [128, 512], F32)
    psl = p.ps("psl", [128, 268], F32)

    with nc.Block() as blk:
        @blk.sync
        def _(sy):
            p.dma(sy, "ldc", rbb[:], rb_i.to_broadcast([128, 8]))
            p.dma(sy, "ldc", chs[:].rearrange("p n c -> p n c")[:, :, 0:32],
                  ch_i.rearrange("(n p) c -> p n c", p=128))
            p.dma(sy, "ldc", chs[:, :, 32:64], sh_i.rearrange("(n p) c -> p n c", p=128))
            for c in range(DC):
                p.dma(sy, "ldw32", wI[:, c], wI_i[c * 128:(c + 1) * 128, :])
            p.dtok("wIload", "ldw32")
            for i in range(NT):
                s = Seq(p, sy, "sy")
                s.wait_tok(f"xtfree{i}", may_miss=True)
                p.dma(sy, f"ld{i % 2}", xt[:, i % 2], x_i.rearrange("(n p) d -> n p d", p=128)[i])

        @blk.gpsimd
        def _(g):
            gg = Seq(p, g, "g")
            for w in range(5):
                for c in range(DC):
                    p.dma(g, "ldw", w5[:, w, c], w5_i[w, c * 128:(c + 1) * 128, :])
            p.dtok("wload", "ldw")
            gg.op("memset", ident32[:], 1.0)
            gg.op("affine_select", ident32[:], ident32[:], [[-1, 128]], ALU.is_equal, 0.0,
                  base=0, channel_multiplier=1)
            gg.mark("identdone")
            p.emit_consts(gg)
            for i in range(NT):
                gg.wait_tok(f"xn16d{i}")
                p.dma(g, f"s0{i % 2}", xn1b_o.rearrange("(n p) d -> n p d", p=128)[i], xn16[:, i % 2])
                p.dtok(f"xnst{i}", f"s0{i % 2}")
                gg.wait_tok(f"seld{i}")
                p.dma(g, f"s1{i % 2}", qI_o.rearrange("(n p) d -> n p d", p=128)[i], sel[:, i % 2, 0:128])
                p.dma(g, f"s1{i % 2}", kI_o.rearrange("(n p) d -> n p d", p=128)[i], sel[:, i % 2, 128:256])
                p.dma(g, f"s1{i % 2}", wsig_o.rearrange("(n p) d -> n p d", p=128)[i], sel[:, i % 2, 256:260])
                p.dtok(f"selst{i}", f"s1{i % 2}")
                gg.wait_tok(f"routd{i}")
                p.dma(g, f"s2{i % 2}", eps_o[:, i], small[:, i % 2, 0:2])
                p.dtok(f"routst{i}", f"s2{i % 2}")
                gg.wait_tok(f"roped{i}")
                p.dma(g, f"s3{i % 2}", rq_o.rearrange("(n p) d -> n p d", p=128)[i], rqk16[:, i % 2, 0:D])
                p.dma(g, f"s3{i % 2}", rk_o.rearrange("(n p) d -> n p d", p=128)[i], rqk16[:, i % 2, D:2 * D])
                p.dtok(f"ropest{i}", f"s3{i % 2}")
                gg.wait_tok(f"vgd{i}")
                p.dma(g, f"s4{i % 2}", vg_o.rearrange("(n p) d -> n p d", p=128)[i], vg16[:, i % 2])
                gg.wait_tok(f"god{i}")
                p.dma(g, f"s4{i % 2}", gos_o.rearrange("(n p) d -> n p d", p=128)[i], go16[:, i % 2])
                p.dtok(f"vgst{i}", f"s4{i % 2}")

        @blk.scalar
        def _(sc):
            s = Seq(p, sc, "sc")
            s.wait_tok("consts")
            for i in range(NT):
                s.wait_rel(f"ld{i % 2}", 16 * (i // 2 + 1))
                s.wait_tok(f"statfreev{i}", may_miss=True)
                s.op("activation", sqd[:], xt[:, i % 2], AF.Square, accum_out=stats[:, 1:2])
                s.mark(f"ssq{i}")
                s.wait_tok(f"var{i}")
                s.op("activation", stats[:, 3:4], stats[:, 4:5], AF.Ln, bias=p.cc(LN_EPS), scale=1.0)
                s.op("activation", stats[:, 3:4], stats[:, 3:4], AF.Exp, bias=0.0, scale=-0.5)
                s.mark(f"rstd{i}")
                s.wait_tok(f"selmm{i}")
                s.wait_tok(f"selst{i - 2}", may_miss=True)
                s.op("activation", sel[:, i % 2, 256:260], psl[:, 256:260], AF.Sigmoid)
                s.mark(f"wsigd{i}")
                s.wait_tok(f"rmax{i}")
                s.op("activation", emax[:], psl[:, 260:268], AF.Exp, bias=stats[:, 6:7],
                     scale=1.0, accum_out=stats[:, 7:8])
                s.mark(f"rexp{i}")
                s.wait_tok(f"gvev{i}")
                s.op("activation", gg_[:], gg_[:], AF.Sigmoid)
                s.mark(f"gvsig{i}")
                s.wait_tok(f"goev{i}")
                s.wait_tok(f"vgst{i - 2}", may_miss=True)
                s.op("activation", go16[:, i % 2], sqd[:], AF.Sigmoid)
                s.mark(f"god{i}")

        @blk.vector
        def _(ve):
            v = Seq(p, ve, "v")
            v.wait_rel("ldc", 48)
            for i in range(NT):
                v.wait_rel(f"ld{i % 2}", 16 * (i // 2 + 1))
                v.wait_tok(f"ssq{i}")
                v.op("reduce_sum", stats[:, 0:1], xt[:, i % 2], axis=AX.X)
                v.op("tensor_scalar", stats[:, 2:3], stats[:, 0:1], 1.0 / D, None, ALU.mult)
                v.op("tensor_scalar", stats[:, 5:6], stats[:, 2:3], stats[:, 2:3], None, ALU.mult)
                v.op("tensor_scalar", stats[:, 4:5], stats[:, 1:2], 1.0 / D, stats[:, 5:6],
                     ALU.mult, ALU.subtract)
                v.mark(f"var{i}")
                v.mark(f"statfreev{i + 1}")
                v.wait_tok(f"rstd{i}")
                v.wait_tok(f"xnuse{i - 2}", may_miss=True)
                v.op("tensor_scalar", xnt[:, i % 2], xt[:, i % 2], stats[:, 2:3], stats[:, 3:4],
                     ALU.subtract, ALU.mult)
                v.mark(f"xtfree{i + 2}")
                v.mark(f"xn1d{i}")
                v.wait_tok(f"xnst{i - 2}", may_miss=True)
                v.op("tensor_copy", xn16[:, i % 2], xnt[:, i % 2])
                v.mark(f"xn16d{i}")
                for w2_ in range(2):
                    v.wait_tok(f"tp{i}_{w2_}")
                    v.op("tensor_copy", xf32[:, i % 2, 4 * w2_:4 * w2_ + 4, :],
                         (pt1, pt2)[w2_][:].rearrange("p (c t) -> p c t", t=128))
                v.mark(f"tpev{i}")
                v.op("tensor_copy", xf16[:, i % 2], xf32[:, i % 2])
                v.mark(f"xf16d{i}")
                v.wait_tok(f"selmm{i}")
                v.wait_tok(f"selst{i - 2}", may_miss=True)
                v.op("tensor_copy", sel[:, i % 2, 0:256], psl[:, 0:256])
                v.mark(f"qkId{i}")
                v.op("tensor_tensor", psl[:, 260:268], psl[:, 260:268], rbb[:], ALU.add)
                v.op("reduce_max", stats[:, 6:7], psl[:, 260:268], axis=AX.X)
                v.op("tensor_scalar", stats[:, 6:7], stats[:, 6:7], -1.0, None, ALU.mult)
                v.mark(f"rmax{i}")
                v.wait_tok(f"rexp{i}")
                v.wait_tok(f"routst{i - 2}", may_miss=True)
                v.op("reciprocal", small[:, i % 2, 1:2], stats[:, 7:8])
                v.op("max", emax[:], psl[:, 260:268])
                v.op("max_index", eidx[:], emax[:], psl[:, 260:268])
                v.op("tensor_copy", small[:, i % 2, 0:1], eidx[:, 0:1])
                v.mark(f"routd{i}")
                v.mark(f"seld{i}")
                for pj in range(5):
                    for nn in range(2):
                        gi = (i * 10) + pj * 2 + nn
                        bank = (pa, pb)[gi % 2]
                        v.wait_tok(f"pj{gi}")
                        if pj < 2:
                            v.wait_tok(f"ropest{i - 2}" if (pj == 0 and nn == 0) else "zz-1", may_miss=True)
                            v.op("tensor_copy", qkf[:, i % 2, pj * 1024 + nn * 512:pj * 1024 + (nn + 1) * 512],
                                 bank[:])
                        elif pj == 2:
                            v.op("tensor_copy", vv[:, nn * 512:(nn + 1) * 512], bank[:])
                        elif pj == 3:
                            v.op("tensor_copy", gg_[:, nn * 512:(nn + 1) * 512], bank[:])
                            if nn == 1:
                                v.mark(f"gvev{i}")
                        else:
                            v.op("tensor_copy", sqd[:, nn * 512:(nn + 1) * 512], bank[:])
                            if nn == 1:
                                v.mark(f"goev{i}")
                        v.mark(f"pjev{gi}")
                v.wait_tok(f"gvsig{i}")
                v.wait_tok(f"vgst{i - 2}", may_miss=True)
                v.op("tensor_tensor", vg16[:, i % 2], vv[:], gg_[:], ALU.mult)
                v.mark(f"vgd{i}")
                ch3 = chs[:, i, 0:32].to_broadcast([128, 32, 16]).rearrange("p w h -> p h w")
                sh3 = chs[:, i, 32:64].to_broadcast([128, 32, 16]).rearrange("p w h -> p h w")
                for pj in range(2):
                    qk = qkf[:, i % 2, pj * 1024:(pj + 1) * 1024]
                    qe = qk.rearrange("p (h w) -> p h w", w=64)[:, :, 0:64:2]
                    qo = qk.rearrange("p (h w) -> p h w", w=64)[:, :, 1:64:2]
                    r3 = rop[:, 0:1024].rearrange("p (h w) -> p h w", w=64)
                    o16 = rqk16[:, i % 2, pj * 1024:(pj + 1) * 1024].rearrange("p (h w) -> p h w", w=64)
                    v.op("tensor_tensor", r3[:, :, 0:32], qe, ch3, ALU.mult)
                    v.op("tensor_tensor", r3[:, :, 32:64], qo, sh3, ALU.mult)
                    v.op("tensor_tensor", o16[:, :, 0:32], r3[:, :, 0:32], r3[:, :, 32:64], ALU.subtract)
                    v.op("tensor_tensor", r3[:, :, 0:32], qe, sh3, ALU.mult)
                    v.op("tensor_tensor", r3[:, :, 32:64], qo, ch3, ALU.mult)
                    v.op("tensor_tensor", o16[:, :, 32:64], r3[:, :, 0:32], r3[:, :, 32:64], ALU.add)
                v.mark(f"roped{i}")
                v.mark(f"xnuse{i}")

        @blk.tensor
        def _(te):
            t = Seq(p, te, "t")
            t.wait_tok("identdone")
            t.wait_tok("wIload")
            t.wait_tok("wload")
            for i in range(NT):
                t.wait_tok(f"xn1d{i}")
                t.wait_tok(f"tpev{i - 1}")
                for w2_ in range(2):
                    for c in range(4):
                        inst = te.transpose((pt1, pt2)[w2_][:, c * 128:(c + 1) * 128],
                                            xnt[:, i % 2, (4 * w2_ + c) * 128:(4 * w2_ + c + 1) * 128],
                                            ident32[:])
                    inst.then_inc(p.sem("t"), 1); t.n += 1; p.counts["t"] = t.n
                    t.mark(f"tp{i}_{w2_}")
                t.wait_tok(f"tpev{i}")
                t.wait_tok(f"routd{i - 1}")
                for dc in range(DC):
                    inst = te.matmul(psl[:], xf32[:, i % 2, dc, :], wI[:, dc],
                                     start=(dc == 0), stop=(dc == DC - 1))
                inst.then_inc(p.sem("t"), 1); t.n += 1; p.counts["t"] = t.n
                t.mark(f"selmm{i}")
                t.wait_tok(f"xf16d{i}")
                for pj in range(5):
                    for nn in range(2):
                        gi = (i * 10) + pj * 2 + nn
                        bank = (pa, pb)[gi % 2]
                        t.wait_tok(f"pjev{gi - 2}")
                        for dc in range(DC):
                            inst = te.matmul(bank[:], xf16[:, i % 2, dc, :],
                                             w5[:, pj, dc, nn * 512:(nn + 1) * 512],
                                             start=(dc == 0), stop=(dc == DC - 1))
                        inst.then_inc(p.sem("t"), 1); t.n += 1; p.counts["t"] = t.n
                        t.mark(f"pj{gi}")

        @blk.sync
        def _(sy):
            s = Seq(p, sy, "syf")
            for nm in ("s00", "s01", "s10", "s11", "s20", "s21", "s30", "s31", "s40", "s41"):
                if p.dmac.get(p.rsem(nm)):
                    s.wait(nm, p.dmac[p.rsem(nm)])


def build_pre1(TL):
    return build_twice(emit_pre1, TL)



NBIS = 26
K_BASE1, K_MIN1, K_MAX1 = 512.0, 32.0, 1024.0


def emit_sel1(p, SS):
    NTT = SS // 128
    nc = p.nc

    qI_i = p.din("qI", [2, SS, 32])
    kI_i = p.din("kI", [2, SS, 32])
    ws_i = p.din("ws", [SS, 2])
    hb_i = p.din("hb", [2, 1, 4])
    gbx_i = p.din("gbx", [2, 1, 1])
    vema_i = p.din("vema", [1, 1])
    thr_o = p.dout("thr", [SS, 2, 4])
    facd = p.nc.dram_tensor("facd", [2, 1, 4], F32, kind="Internal").ap()

    qIs = p.sb("qIs", [128, NTT, 32], F32)
    kIs = p.sb("kIs", [128, NTT, 32], F32)
    wss = p.sb("wss", [128, NTT, 2], F32)
    qIf = p.sb("qIf", [32, SS], F32)
    kIf = p.sb("kIf", [32, SS], F32)
    hbs = p.sb("hbs", [1, 4], F32)
    gbs = p.sb("gbs", [128, 2], F32)
    vem = p.sb("vem", [1, 1], F32)
    facb = p.sb("facb", [128, 4], F32)
    Ms = p.sb("Ms", [128, SS], F32)
    mgs = p.sb("mgs", [128, SS], F32)
    st_ = p.sb("st_", [128, 16], F32)
    bis = p.sb("bis", [128, 4, 4], F32)
    cnt = p.sb("cnt", [128, 4], F32)
    kti = p.sb("kti", [128, 4], mybir.dt.int32)
    ktf = p.sb("ktf", [128, 4], F32)
    thrt = p.sb("thrt", [128, 2, 4], F32)
    ident32 = p.sb("ident32", [128, 128], F32)
    cmask4 = p.sb("cmask4", [128, 4, 512], F32)
    sdump = p.sb("sdump", [128, SS], F32)
    nmid = p.sb("nmid", [128, 2], F32)
    scnt = p.sb("scnt", [128, 2], F32)

    pms = p.ps("pms", [128, 512], F32)
    pms2 = p.ps("pms2", [128, 512], F32)
    ptq = p.ps("ptq", [32, 512], F32)

    NBM = lambda tt: (tt * 128) // 512 + 1

    with nc.Block() as blk:
        @blk.sync
        def _(sy):
            pass

        @blk.gpsimd
        def _(g):
            gg = Seq(p, g, "g")
            gg.op("memset", ident32[:], 1.0)
            gg.op("affine_select", ident32[:], ident32[:], [[-1, 128]], ALU.is_equal, 0.0,
                  base=0, channel_multiplier=1)
            gg.op("memset", cmask4[:], 0.0)
            for m_ in range(4):
                gg.op("affine_select", cmask4[:, m_], cmask4[:, m_], [[-1, 512]], ALU.is_ge, -1e30,
                      base=128 * m_, channel_multiplier=1)
            gg.mark("identdone")
            p.emit_consts(gg)
            for u in range(2):
                p.dma(g, f"ldu{u % 2}", qIs[:], qI_i[u].rearrange("(n p) c -> p n c", p=128))
                p.dma(g, f"ldu{u % 2}", kIs[:], kI_i[u].rearrange("(n p) c -> p n c", p=128))
                p.dma(g, f"ldu{u % 2}", wss[:], ws_i.rearrange("(n p) c -> p n c", p=128))
                p.dma(g, f"ldu{u % 2}", hbs[:], hb_i[u])
                p.dma(g, f"ldu{u % 2}", gbs[:, u:u + 1], gbx_i[u].to_broadcast([128, 1]))
                if u == 0:
                    p.dma(g, f"ldu{u % 2}", vem[:], vema_i)
                p.dtok(f"uload{u}", f"ldu{u % 2}")
                gg.wait_tok(f"facready{u}")
                p.dma(g, f"facs{u}", facd[u], hbs[:])
                gg.wait_rel(f"facs{u}", 16)
                p.dma(g, f"facl{u}", facb[:], facd[u].to_broadcast([128, 4]))
                p.dtok(f"facload{u}", f"facl{u}")
                for tt in range(NTT):
                    gg.wait_tok(f"thrd{u}_{tt}")
                    p.dma(g, f"sthr{tt % 2}", thr_o[tt * 128:(tt + 1) * 128, u, :], thrt[:, tt % 2])
                    p.dtok(f"thrst{u}_{tt}", f"sthr{tt % 2}")

        @blk.scalar
        def _(sc):
            s = Seq(p, sc, "sc")
            s.wait_tok("consts")
            for u in range(2):
                s.wait_rel(f"ldu{u % 2}", 96 if u == 0 else 80)
                s.op("activation", hbs[:], hbs[:], AF.Sigmoid)
                s.mark(f"hbsig{u}")
                s.wait_tok(f"facv{u}")
                for tt in range(NTT):
                    nbm = NBM(tt)
                    W = nbm * 512
                    s.wait_tok(f"diag{u}_{tt}")
                    s.op("activation", mgs[:, 0:W], Ms[:, 0:W], AF.Sigmoid, bias=gbs[:, u:u + 1],
                         scale=1.0, accum_out=st_[:, 1:2])
                    s.mark(f"mg{u}_{tt}")
                    s.op("activation", mgs[:, 0:W], mgs[:, 0:W], AF.Square, accum_out=st_[:, 2:3])
                    s.mark(f"mg2{u}_{tt}")
                    for it in range(NBIS):
                        s.wait_tok(f"mid{u}_{tt}_{it}")
                        s.op("activation", sdump[:, 0:W], Ms[:, 0:W], AF.Sign,
                             bias=nmid[:, 0:1], scale=1.0, accum_out=scnt[:, 0:1])
                        s.op("activation", sdump[:, 0:W], Ms[:, 0:W], AF.Sign,
                             bias=nmid[:, 1:2], scale=1.0, accum_out=scnt[:, 1:2])
                        s.mark(f"scnt{u}_{tt}_{it}")

        @blk.vector
        def _(ve):
            v = Seq(p, ve, "v")
            for u in range(2):
                v.wait_tok(f"hbsig{u}")
                if u == 0:
                    v.op("tensor_scalar", vem[:], vem[:], 1e-6, None, ALU.max)
                    v.op("reciprocal", vem[:], vem[:])
                v.op("tensor_tensor", hbs[:], hbs[:], hbs[:], ALU.mult)
                v.op("tensor_scalar", hbs[:], hbs[:], vem[0:1, 0:1], None, ALU.mult)
                v.op("tensor_scalar", hbs[:], hbs[:], K_BASE1, None, ALU.mult)
                v.mark(f"facready{u}")
                v.mark(f"facv{u}")
                for tt in range(NTT):
                    v.wait_tok(f"qk{u}_{tt}")
                    v.op("tensor_copy", qIf[:, tt * 128:(tt + 1) * 128], ptq[:, 0:128])
                    v.op("tensor_copy", kIf[:, tt * 128:(tt + 1) * 128], ptq[:, 128:256])
                    v.mark(f"qkev{u}_{tt}")
                for tt in range(NTT):
                    nbm = NBM(tt)
                    W = nbm * 512
                    for sb_ in range(nbm):
                        bank = (pms, pms2)[sb_ % 2]
                        v.wait_tok(f"m{u}_{tt}_{sb_}")
                        v.op("tensor_copy", Ms[:, sb_ * 512:(sb_ + 1) * 512], bank[:])
                        v.mark(f"mev{u}_{tt}_{sb_}")
                    v.mark(f"mev{u}_{tt}")
                    v.op("tensor_reduce", st_[:, 0:1], Ms[:, 0:W], AX.X, ALU.max,
                         apply_absolute_value=True)
                    v.mark(f"amax{u}_{tt}")
                    v.op("tensor_tensor", Ms[:, (nbm - 1) * 512:nbm * 512],
                         Ms[:, (nbm - 1) * 512:nbm * 512], cmask4[:, tt % 4], ALU.add)
                    v.mark(f"diag{u}_{tt}")
                    v.wait_tok(f"mg2{u}_{tt}")
                    v.op("tensor_scalar", st_[:, 3:4], st_[:, 1:2], 1.0 / SS, None, ALU.mult)
                    v.op("tensor_scalar", st_[:, 4:5], st_[:, 3:4], st_[:, 3:4], None, ALU.mult)
                    v.op("tensor_scalar", st_[:, 3:4], st_[:, 2:3], 1.0 / SS, st_[:, 4:5],
                         ALU.mult, ALU.subtract)
                    v.op("tensor_scalar", st_[:, 5:6], wss[:, tt, u:u + 1], wss[:, tt, u:u + 1], None, ALU.mult)
                    v.op("tensor_tensor", st_[:, 3:4], st_[:, 3:4], st_[:, 5:6], ALU.mult)
                    v.wait_tok(f"facload{u}")
                    v.op("tensor_tensor", bis[:, 3, :], st_[:, 3:4].to_broadcast([128, 4]), facb[:], ALU.mult)
                    v.op("tensor_copy", kti[:], bis[:, 3, :])
                    v.op("tensor_copy", ktf[:], kti[:])
                    v.op("tensor_tensor", cnt[:], ktf[:], bis[:, 3, :], ALU.is_gt)
                    v.op("tensor_sub", ktf[:], ktf[:], cnt[:])
                    v.op("tensor_scalar", ktf[:], ktf[:], K_MIN1, None, ALU.max)
                    v.op("tensor_scalar", ktf[:], ktf[:], K_MAX1, None, ALU.min)
                    v.op("tensor_scalar", bis[:, 3, :], ktf[:], 4.0, None, ALU.subtract)
                    v.op("memset", Ms[:, 0:4], -1e30)
                    v.wait_tok(f"amax{u}_{tt}")
                    v.op("tensor_scalar", bis[:, 1, :], st_[:, 0:1].to_broadcast([128, 4]), 1e-3, None, ALU.add)
                    v.op("tensor_scalar", bis[:, 0, :], bis[:, 1, :], -1.0, None, ALU.mult)
                    for it in range(NBIS):
                        v.op("tensor_add", bis[:, 2, :], bis[:, 0, :], bis[:, 1, :])
                        v.op("tensor_scalar", bis[:, 2, :], bis[:, 2, :], 0.5, None, ALU.mult)
                        if it > 0:
                            v.wait_tok(f"scnt{u}_{tt}_{it - 1}")
                        v.op("tensor_scalar", nmid[:], bis[:, 2, 2:4], -1.0, None, ALU.mult)
                        v.mark(f"mid{u}_{tt}_{it}")
                        for h in range(2):
                            v.op("tensor_scalar", mgs[:, 0:W], Ms[:, 0:W], bis[:, 2, h:h + 1],
                                 None, ALU.is_gt, ALU.add, accum_out=cnt[:, h:h + 1])
                        v.wait_tok(f"scnt{u}_{tt}_{it}")
                        v.op("tensor_scalar", cnt[:, 2:4], scnt[:], float(W), None, ALU.add)
                        v.op("tensor_scalar", cnt[:, 2:4], cnt[:, 2:4], 0.5, None, ALU.mult)
                        v.op("tensor_tensor", cnt[:], cnt[:], bis[:, 3, :], ALU.is_ge)
                        v.op("tensor_tensor", st_[:, 8:12], bis[:, 2, :], bis[:, 0, :], ALU.subtract)
                        v.op("tensor_tensor", st_[:, 8:12], st_[:, 8:12], cnt[:], ALU.mult)
                        v.op("tensor_add", bis[:, 0, :], bis[:, 0, :], st_[:, 8:12])
                        v.op("tensor_scalar", cnt[:], cnt[:], -1.0, 1.0, ALU.mult, ALU.add)
                        v.op("tensor_tensor", st_[:, 8:12], bis[:, 2, :], bis[:, 1, :], ALU.subtract)
                        v.op("tensor_tensor", st_[:, 8:12], st_[:, 8:12], cnt[:], ALU.mult)
                        v.op("tensor_add", bis[:, 1, :], bis[:, 1, :], st_[:, 8:12])
                    if u == 0 and tt == 0:
                        p._thrb = [None, None]
                    if p._thrb[tt % 2] is not None:
                        v.wait_tok(p._thrb[tt % 2])
                    p._thrb[tt % 2] = f"thrst{u}_{tt}"
                    v.op("tensor_copy", thrt[:, tt % 2], bis[:, 0, :])
                    v.mark(f"thrd{u}_{tt}")
                    v.mark(f"bisdone{u}_{tt}")

        @blk.tensor
        def _(te):
            t = Seq(p, te, "t")
            t.wait_tok("identdone")
            for u in range(2):
                t.wait_tok(f"uload{u}")
                if u == 1:
                    t.wait_tok(f"bisdone0_{NTT - 1}")
                for tt in range(NTT):
                    t.wait_tok(f"qkev{u}_{tt - 1}")
                    inst = te.transpose(ptq[:, 0:128], qIs[:, tt, :], ident32[:])
                    inst.then_inc(p.sem("t"), 1); t.n += 1; p.counts["t"] = t.n
                    inst = te.transpose(ptq[:, 128:256], kIs[:, tt, :], ident32[:])
                    inst.then_inc(p.sem("t"), 1); t.n += 1; p.counts["t"] = t.n
                    t.mark(f"qk{u}_{tt}")
                t.wait_tok(f"qkev{u}_{NTT - 1}")
                if u == 0:
                    p._mbank = [None, None]
                for tt in range(NTT):
                    nbm = NBM(tt)
                    for sb_ in range(nbm):
                        bank = (pms, pms2)[sb_ % 2]
                        if p._mbank[sb_ % 2] is not None:
                            t.wait_tok(p._mbank[sb_ % 2])
                        inst = te.matmul(bank[:], qIf[:, tt * 128:(tt + 1) * 128],
                                         kIf[:, sb_ * 512:(sb_ + 1) * 512], start=True, stop=True)
                        inst.then_inc(p.sem("t"), 1); t.n += 1; p.counts["t"] = t.n
                        t.mark(f"m{u}_{tt}_{sb_}")
                        p._mbank[sb_ % 2] = f"mev{u}_{tt}_{sb_}"

        @blk.vector
        def _(ve):
            pass

        @blk.sync
        def _(sy):
            s = Seq(p, sy, "syf")
            for nm in ("sthr0", "sthr1"):
                if p.dmac.get(p.rsem(nm)):
                    s.wait(nm, p.dmac[p.rsem(nm)])


def build_sel1(SS):
    return build_twice(emit_sel1, SS)



FF1 = 4096


def emit_moe1(p, TL, presummed=False):
    NT = TL // 128
    NTS = TL // 512
    _lysem = "ly%d"
    nc = p.nc

    xn_i = p.din("xn1", [TL, D], BF16)
    eps_i = p.din("eps", [128, TL // 128, 2])
    w1_i = p.din("w1", [8, D, 2 * FF1])
    w2_i = p.din("w2", [8, FF1, D])
    x_i = p.din("x1", [TL, D])
    ya_i = p.din("ya", [D, TL])
    yb_i = None if presummed else p.din("yb", [D, TL])
    x2_o = p.dout("x2", [TL, D])

    hf = p.sb("hf", [128, DC, TL], BF16)
    w1a = p.sb("w1a", [128, 2, DC, 512], BF16)
    w1g = p.sb("w1g", [128, 2, DC, 512], BF16)
    w2s = p.sb("w2s", [128, 2, 4, D], BF16)
    gcol = p.sb("gcol", [128, NT, 8], F32)
    esl = p.sb("esl", [128, NT, 2], F32)
    ysb = p.sb("ysb", [128, NT, D], F32)
    hid16 = p.sb("hid16", [128, 2, 4, 512], BF16)
    sil = p.sb("sil", [128, 512], F32)
    xs_ = p.sb("xs_", [128, 2, D], F32)
    yab = p.sb("yab", [128, 2, 2, DC * 128], F32)
    g1t = p.sb("g1t", [128, 2, D], F32)
    s1t = p.sb("s1t", [128, 2, D], F32)
    ident32 = p.sb("ident32", [128, 128], F32)

    ph1 = p.ps("ph1", [128, 512], F32)
    ph2 = p.ps("ph2", [128, 512], F32)
    pg1 = p.ps("pg1", [128, 512], F32)
    pg2 = p.ps("pg2", [128, 512], F32)
    py1 = p.ps("py1", [128, 512], F32)
    py2 = p.ps("py2", [128, 512], F32)
    pt1 = p.ps("pt1", [128, 512], F32)
    pt2 = p.ps("pt2", [128, 512], F32)
    with nc.Block() as blk:
        @blk.sync
        def _(sy):
            for c in range(DC):
                p.dma(sy, "ldh", hf[:, c], xn_i[:, c * 128:(c + 1) * 128], transpose=True)
            p.dma(sy, "ldc", esl[:], eps_i)
            p.dtok("hload", "ldh")

        @blk.gpsimd
        def _(g):
            gg = Seq(p, g, "g")
            gg.op("memset", ident32[:], 1.0)
            gg.op("affine_select", ident32[:], ident32[:], [[-1, 128]], ALU.is_equal, 0.0,
                  base=0, channel_multiplier=1)
            gg.op("memset", ysb[:], 0.0)
            gg.mark("init")
            p.emit_consts(gg)
            for e in range(8):
                for q in range(8):
                    j = e * 8 + q
                    gg.wait_tok(f"wfree{j - 2}")
                    p.dma(g, f"lw{j % 2}", w1a[:, j % 2],
                          w1_i[e].rearrange("(c k) f -> k c f", k=128)[:, :, q * 512:(q + 1) * 512])
                    p.dma(g, f"lw{j % 2}", w1g[:, j % 2],
                          w1_i[e].rearrange("(c k) f -> k c f", k=128)[:, :, FF1 + q * 512:FF1 + (q + 1) * 512])
                    p.dma(g, f"lw{j % 2}", w2s[:, j % 2],
                          w2_i[e, q * 512:(q + 1) * 512].rearrange("(c k) f -> k c f", k=128))
                    p.dtok(f"wld{j}", f"lw{j % 2}")
            gg.wait_tok("yready")
            for i in range(NT + 2):
                if i < NT:
                    gg.wait_tok(f"x2d{i - 2}")
                    gg.wait_tok(f"ytpA{i - 1}")
                    p.dma(g, f"lx{i % 2}", xs_[:, i % 2], x_i.rearrange("(n p) d -> n p d", p=128)[i])
                    p.dma(g, f"ly{i % 2}", yab[:, i % 2, 0].rearrange("p (c t) -> p c t", t=128),
                          ya_i[:, i * 128:(i + 1) * 128].rearrange("(c p) t -> p c t", p=128))
                    if not presummed:
                        p.dma(g, f"ly{i % 2}", yab[:, i % 2, 1].rearrange("p (c t) -> p c t", t=128),
                              yb_i[:, i * 128:(i + 1) * 128].rearrange("(c p) t -> p c t", p=128))
                if i >= 2:
                    gg.wait_tok(f"x2v{i - 2}")
                    p.dma(g, f"sx{i % 2}", x2_o.rearrange("(n p) d -> n p d", p=128)[i - 2], g1t[:, (i - 2) % 2])
                    p.dtok(f"x2st{i - 2}", f"sx{i % 2}")

        @blk.tensor
        def _(te):
            t = Seq(p, te, "t")
            t.wait_tok("init")
            t.wait_tok("hload")
            for e in range(8):
                for q in range(8):
                    j = e * 8 + q
                    t.wait_rel("lw%d" % (j % 2), 48 * (j // 2 + 1))
                    for ts in range(NTS):
                        for fc in range(4):
                            gidx = ts * 4 + fc
                            ba = (ph1, ph2)[gidx % 2]
                            bg = (pg1, pg2)[gidx % 2]
                            if gidx >= 2:
                                t.wait_tok(f"hg{j}_{gidx - 2}")
                            elif j > 0:
                                t.wait_tok(f"hg{j - 1}_{NTS * 4 - 2 + gidx}")
                            for dc in range(DC):
                                te.matmul(ba[:], w1a[:, j % 2, dc, fc * 128:(fc + 1) * 128],
                                          hf[:, dc, ts * 512:(ts + 1) * 512],
                                          start=(dc == 0), stop=(dc == DC - 1))
                            for dc in range(DC):
                                inst = te.matmul(bg[:], w1g[:, j % 2, dc, fc * 128:(fc + 1) * 128],
                                                 hf[:, dc, ts * 512:(ts + 1) * 512],
                                                 start=(dc == 0), stop=(dc == DC - 1))
                            inst.then_inc(p.sem("t"), 1); t.n += 1; p.counts["t"] = t.n
                            t.mark(f"h{j}_{gidx}")
                        for sub in range(4):
                            i = ts * 4 + sub
                            for nn in range(2):
                                jj = i * 2 + nn
                                bank = (py1, py2)[jj % 2]
                                t.wait_tok(f"yev{j}_{jj - 2}")
                                t.wait_tok(f"hg{j}_{ts * 4 + 3}")
                                for fc in range(4):
                                    inst = te.matmul(bank[:], hid16[:, j % 2, fc, sub * 128:(sub + 1) * 128],
                                                     w2s[:, j % 2, fc, nn * 512:(nn + 1) * 512],
                                                     start=(fc == 0), stop=(fc == 3))
                                inst.then_inc(p.sem("t"), 1); t.n += 1; p.counts["t"] = t.n
                                t.mark(f"y{j}_{jj}")
                    t.mark(f"wfree{j}")
            for i in range(NT):
                t.wait_tok(f"ysum{i}")
                t.wait_rel(_lysem % (i % 2), (16 if presummed else 32) * (i // 2 + 1))
                t.wait_tok(f"ytpeA{i - 1}")
                for c in range(4):
                    inst = te.transpose(pt1[:, c * 128:(c + 1) * 128],
                                        yab[:, i % 2, 0, c * 128:(c + 1) * 128], ident32[:])
                inst.then_inc(p.sem("t"), 1); t.n += 1; p.counts["t"] = t.n
                t.mark(f"ytpA{i}")
                t.wait_tok(f"ytpeB{i - 1}")
                for c in range(4):
                    inst = te.transpose(pt2[:, c * 128:(c + 1) * 128],
                                        yab[:, i % 2, 0, (4 + c) * 128:(5 + c) * 128], ident32[:])
                inst.then_inc(p.sem("t"), 1); t.n += 1; p.counts["t"] = t.n
                t.mark(f"ytpB{i}")

        @blk.vector
        def _(ve):
            v = Seq(p, ve, "v")
            v.wait_tok("init")
            v.wait_rel("ldc", 16)
            for i in range(NT):
                for e in range(8):
                    v.op("tensor_scalar", gcol[:, i, e:e + 1], esl[:, i, 0:1], float(e), None,
                         ALU.is_equal)
                v.op("tensor_tensor", gcol[:, i, :], gcol[:, i, :],
                     esl[:, i, 1:2].to_broadcast([128, 8]), ALU.mult)
            v.mark("gateready")
            for e in range(8):
                for q in range(8):
                    j = e * 8 + q
                    for ts in range(NTS):
                        for fc in range(4):
                            gidx = ts * 4 + fc
                            ba = (ph1, ph2)[gidx % 2]
                            bg = (pg1, pg2)[gidx % 2]
                            v.wait_tok(f"h{j}_{gidx}")
                            v.wait_tok(f"sil{j}_{gidx}")
                            if ts > 0:
                                v.wait_tok(f"y{j}_{ts * 8 - 1}")
                            elif j >= 2:
                                v.wait_tok(f"y{j - 2}_{NTS * 8 - 1}")
                            v.op("tensor_tensor", hid16[:, j % 2, fc, :], sil[:], bg[:], ALU.mult)
                            v.mark(f"hg{j}_{gidx}")
                        for sub in range(4):
                            i = ts * 4 + sub
                            for nn in range(2):
                                jj = i * 2 + nn
                                bank = (py1, py2)[jj % 2]
                                v.wait_tok(f"y{j}_{jj}")
                                v.op("scalar_tensor_tensor", ysb[:, i, nn * 512:(nn + 1) * 512],
                                     bank[:], gcol[:, i, e:e + 1], ysb[:, i, nn * 512:(nn + 1) * 512],
                                     ALU.mult, ALU.add)
                                v.mark(f"yev{j}_{jj}")
            v.mark("yready")
            for i in range(NT):
                if presummed:
                    v.wait_rel(f"ly{i % 2}", 16 * (i // 2 + 1))
                else:
                    v.wait_rel(f"ly{i % 2}", 32 * (i // 2 + 1))
                    v.op("tensor_add", yab[:, i % 2, 0], yab[:, i % 2, 0], yab[:, i % 2, 1])
                v.mark(f"ysum{i}")
                v.wait_tok(f"ytpA{i}")
                v.op("tensor_copy", s1t[:, i % 2, 0:512], pt1[:])
                v.mark(f"ytpeA{i}")
                v.wait_tok(f"ytpB{i}")
                v.op("tensor_copy", s1t[:, i % 2, 512:1024], pt2[:])
                v.mark(f"ytpeB{i}")
                v.wait_tok(f"sil1{i}")
                v.wait_rel(f"lx{i % 2}", 16 * (i // 2 + 1))
                v.op("tensor_tensor", g1t[:, i % 2], g1t[:, i % 2], s1t[:, i % 2], ALU.mult)
                v.op("tensor_tensor", g1t[:, i % 2], g1t[:, i % 2], xs_[:, i % 2], ALU.add)
                v.mark(f"x2v{i}")
                v.mark(f"x2d{i}")

        @blk.scalar
        def _(sc):
            s = Seq(p, sc, "sc")
            s.wait_tok("consts")
            for e in range(8):
                for q in range(8):
                    j = e * 8 + q
                    for ts in range(NTS):
                        for fc in range(4):
                            gidx = ts * 4 + fc
                            ba = (ph1, ph2)[gidx % 2]
                            s.wait_tok(f"h{j}_{gidx}")
                            if gidx > 0:
                                s.wait_tok(f"hg{j}_{gidx - 1}")
                            elif j > 0:
                                s.wait_tok(f"hg{j - 1}_{NTS * 4 - 1}")
                            s.op("activation", sil[:], ba[:], AF.Silu)
                            s.mark(f"sil{j}_{gidx}")
            for i in range(NT):
                s.wait_tok("yready")
                s.wait_tok(f"x2st{i - 2}", may_miss=True)
                s.op("activation", g1t[:, i % 2], ysb[:, i], AF.Silu)
                s.mark(f"sil1{i}")

        @blk.sync
        def _(sy):
            s = Seq(p, sy, "syf")
            for nm in ("sx0", "sx1"):
                if p.dmac.get(p.rsem(nm)):
                    s.wait(nm, p.dmac[p.rsem(nm)])


def build_moe1(TL):
    return build_twice(emit_moe1, TL)



def emit_fused(p, TL):
    SX = 2 * TL
    nc = p.nc
    f = lambda n, shape, dt=F32: nc.dram_tensor(n, shape, dt, kind="Internal").ap()
    RG = [[0, 1], [2, 3], [4, 5], [6, 7]]

    ein = lambda n, shape, dt=F32: nc.dram_tensor(n, shape, dt, kind="ExternalInput").ap()
    x_in = ein("x", [TL, D])
    l0a_wq = ein("l0a_wq", [D, D]); l0a_wk = ein("l0a_wk", [D, D]); l0a_wv = ein("l0a_wv", [D, D])
    l0a_wo = ein("l0a_wo", [D, D]); l0a_bo = ein("l0a_bo", [1, D])
    l0a_g = ein("l0a_g", [1, D]); l0a_b = ein("l0a_b", [1, D])
    l0f_g = ein("l0f_g", [1, D]); l0f_b = ein("l0f_b", [1, D])
    l0f_wq = ein("l0f_wq", [D, D]); l0f_wk = ein("l0f_wk", [D, D]); l0f_wv = ein("l0f_wv", [D, D])
    l0f_qs = ein("l0f_qs", [1, D]); l0f_ks = ein("l0f_ks", [1, D])
    l0f_wo = ein("l0f_wo", [512, D])
    l0f_gw = ein("l0f_gw", [D, 8]); l0f_gb = ein("l0f_gb", [1, 8])
    l0f_w1 = ein("l0f_w1", [8, D, FF0]); l0f_b1 = ein("l0f_b1", [8, FF0])
    l0f_w2 = ein("l0f_w2", [8, FF0, D]); l0f_b2 = ein("l0f_b2", [8, D])
    wI_in = ein("wI", [D, 268]); rb_in = ein("rb", [1, 8])
    w5_in = ein("w5", [5, D, D])
    ch_in = ein("ch", [TL, 32]); sh_in = ein("sh", [TL, 32])
    hb_in = ein("hb", [2, 1, 4]); gbx_in = ein("gbx", [2, 1, 1]); vema_in = ein("vema", [1, 1])
    l1a_wo = ein("l1a_wo", [512, D])
    l1f_w1 = ein("l1f_w1", [8, D, 2 * FF1]); l1f_w2 = ein("l1f_w2", [8, FF1, D])

    xn0 = f("i_xn0", [TL, D]); eq = f("i_eq", [D, TL], BF16); ctx = f("i_ctx", [D, D + 1])
    ctxS = f("i_ctxS", [D, D + 1])
    qh = f("i_qh", [TL, D], BF16); kh = f("i_kh", [TL, D], BF16); vh = f("i_vh", [TL, D], BF16)
    qF = f("i_qF", [SX, 512], BF16); kF = f("i_kF", [SX, 512], BF16); vF = f("i_vF", [SX, 512], BF16)
    qhG = f("i_qhG", [SX, D], BF16); khG = f("i_khG", [SX, D], BF16); vhG = f("i_vhG", [SX, D], BF16)
    hoff = nc.dram_tensor("hoff", [1, 1], mybir.dt.int32, kind="ExternalInput").ap()
    y0T = f("i_y0T", [2, D, TL]); y0R = f("i_y0R", [D, TL])
    s0 = f("i_s0", [TL, D]); h2 = f("i_h2", [TL, D], BF16)
    x1 = f("i_x1", [TL, D])
    xn1b = f("i_xn1b", [TL, D], BF16)
    qI = f("i_qI", [TL, 128]); kI = f("i_kI", [TL, 128]); wsg = f("i_wsg", [TL, 4])
    eps = f("i_eps", [128, TL // 128, 2])
    rq = f("i_rq", [TL, D], BF16); rk = f("i_rk", [TL, D], BF16)
    vg = f("i_vg", [TL, D], BF16); gos = f("i_gos", [TL, D], BF16)
    qIG = f("i_qIG", [SX, 64]); kIG = f("i_kIG", [SX, 64]); wsG = f("i_wsG", [SX, 2])
    qIGf = f("i_qIGf", [SX, 128]); kIGf = f("i_kIGf", [SX, 128]); wsGf = f("i_wsGf", [SX, 4])
    rqF = f("i_rqF", [SX, 512], BF16); rkF = f("i_rkF", [SX, 512], BF16)
    vgF = f("i_vgF", [SX, 512], BF16); goF = f("i_goF", [SX, 512], BF16)
    rqG = f("i_rqG", [SX, D], BF16); rkG = f("i_rkG", [SX, D], BF16)
    vgG = f("i_vgG", [SX, D], BF16); goG = f("i_goG", [SX, D], BF16)
    thr = f("i_thr", [SX, 2, 4])
    y1T = f("i_y1T", [2, D, TL]); y1R = f("i_y1R", [D, TL])
    x2 = nc.dram_tensor("x2", [TL, D], F32, kind="ExternalOutput").ap()

    p.ctile()

    p._xn = [0]

    def xchg(ops, copies=()):
        tag = p._xn[0]; p._xn[0] += 1
        with nc.Block() as blk:
            @blk.gpsimd
            def _(g):
                s = p.sem(f"cc{tag}")
                n = 0
                for kind, alu, srcap, dstap in ops:
                    g.collective_compute(kind, alu, RG, [srcap], [dstap]).then_inc(s, 16)
                    n += 16
                g.wait_ge(s, n)
                if copies:
                    if p._hreg is None:
                        p._hreg = nc.values_load(hoff, engines=[mybir.EngineType.Pool],
                                                 min_val=0, max_val=1,
                                                 skip_runtime_bounds_check=True)
                    s2 = p.sem(f"cp{tag}")
                    n2 = 0
                    for dst, srcfull, w in copies:
                        g.dma_start(out=dst, in_=srcfull[:, bass.ts(p._hreg, w)]).then_inc(s2, 16)
                        n2 += 16
                    g.wait_ge(s2, n2)
        nc.all_engine_barrier()
    p._hreg = None

    def run(name, emitter, io, *args, **kw):
        with p.phase(name):
            p.io = io
            emitter(p, *args, **kw)
            p.io = {}
        nc.all_engine_barrier()

    run("A", emit_pre0a, {"x": x_in, "wq": l0a_wq, "wk": l0a_wk, "wv": l0a_wv,
                          "xn0": xn0, "eq": eq, "ctx": ctx}, TL)
    run("B", emit_pre0b, {"xn": xn0, "wq": l0f_wq, "wk": l0f_wk, "wv": l0f_wv,
                          "ln_g": l0f_g, "ln_b": l0f_b, "qsc": l0f_qs, "ksc": l0f_ks,
                          "qh": qh, "kh": kh, "vh": vh}, TL)
    r2 = lambda a: a.rearrange("t (r c) -> r t c", r=2)
    xchg([("AllReduce", ALU.add, ctx, ctxS),
          ("AllGather", ALU.bypass, qh, qhG),
          ("AllGather", ALU.bypass, kh, khG),
          ("AllGather", ALU.bypass, vh, vhG)],
         copies=[(qF, qhG, 512), (kF, khG, 512), (vF, vhG, 512)])
    run("C", emit_attn0, {"q": qF, "k": kF, "v": vF, "wo": l0f_wo, "yT": y0T},
        SX, 8, QKSCALE, False)
    xchg([("ReduceScatter", ALU.add, y0T, y0R)])
    run("DD", emit_post0, {"xn": xn0, "eq": eq, "ctxa": ctxS, "wo": l0a_wo, "bo": l0a_bo,
                           "gam": l0a_g, "bet": l0a_b, "ya": y0R, "lng": l0f_g, "lnb": l0f_b,
                           "s0": s0, "h2": h2}, TL, True)
    run("E", emit_moe0, {"h2": h2, "gw": l0f_gw, "gb": l0f_gb, "w1": l0f_w1, "b1": l0f_b1,
                         "w2": l0f_w2, "b2": l0f_b2, "lng": l0f_g, "lnb": l0f_b,
                         "x": x_in, "s0": s0, "x1": x1}, TL)
    run("F", emit_pre1, {"x1": x1, "wI": wI_in, "rb": rb_in, "w5": w5_in, "ch": ch_in,
                         "sh": sh_in, "xn1b": xn1b, "qI": qI, "kI": kI, "wsig": wsg,
                         "eps": eps, "rq": rq, "rk": rk, "vg": vg, "gos": gos}, TL)
    xchg([("AllGather", ALU.bypass, qI, qIGf),
          ("AllGather", ALU.bypass, kI, kIGf),
          ("AllGather", ALU.bypass, wsg, wsGf),
          ("AllGather", ALU.bypass, rq, rqG),
          ("AllGather", ALU.bypass, rk, rkG),
          ("AllGather", ALU.bypass, vg, vgG),
          ("AllGather", ALU.bypass, gos, goG)],
         copies=[(qIG, qIGf, 64), (kIG, kIGf, 64), (wsG, wsGf, 2),
                 (rqF, rqG, 512), (rkF, rkG, 512), (vgF, vgG, 512), (goF, goG, 512)])
    run("G", emit_sel1, {"qI": qIG.rearrange("t (u c) -> u t c", u=2),
                         "kI": kIG.rearrange("t (u c) -> u t c", u=2),
                         "ws": wsG, "hb": hb_in, "gbx": gbx_in, "vema": vema_in,
                         "thr": thr}, SX)
    run("H", emit_attn0, {"q": rqF, "k": rkF, "v": vgF, "go": goF, "wo": l1a_wo,
                          "qIs": qIG, "kIs": kIG, "thr": thr.rearrange("t u c -> t (u c)"),
                          "yT": y1T}, SX, 8, 0.125, True, True)
    xchg([("ReduceScatter", ALU.add, y1T, y1R)])
    run("I", emit_moe1, {"xn1": xn1b, "eps": eps, "w1": l1f_w1, "w2": l1f_w2,
                         "x1": x1, "ya": y1R, "x2": x2}, TL, True)


def build_fused(TL):
    return build_twice(emit_fused, TL)



_BUILT = {}


def _get(name, builder, *args):
    key = (name,) + args
    if key not in _BUILT:
        _BUILT[key] = builder(*args)
    return _BUILT[key]


def _rope_tables(SL=S):
    import math
    HD = 64
    MAXPOS, ORIG, BASE, SCALE = 262144, 8192, 10000.0, 32.0
    BETA_FAST, BETA_SLOW = 1.0, 32.0
    dim = HD
    ext = MAXPOS / ORIG
    scaled_base = BASE * ext ** (dim / (dim - 2))
    idx = (np.arange(0, dim, 2).astype(np.float32) / dim).astype(np.float32)
    inv_freq = (1.0 / scaled_base ** idx).astype(np.float32)
    freq_extra = (1.0 / BASE ** idx).astype(np.float32)
    wavelen = (2.0 * math.pi / freq_extra).astype(np.float32)
    ramp = np.clip((wavelen - BETA_FAST) / (BETA_SLOW - BETA_FAST), 0.0, 1.0).astype(np.float32)
    scale = (1.0 + (SCALE - 1.0) * ramp).astype(np.float32)
    t = np.arange(SL, dtype=np.float32)
    freqs = (t[:, None] / scale[None, :] * inv_freq[None, :]).astype(np.float32)
    emb = np.concatenate([freqs, freqs], -1)
    return (np.cos(emb)[:, ::2].astype(np.float32),
            np.sin(emb)[:, ::2].astype(np.float32))


def _run(nc, in_maps):
    res = run_bass_kernel_spmd(nc, in_maps, core_ids=list(range(NCORE)))
    return res.results


def _pcol(a):
    a = np.asarray(a).reshape(-1)
    return np.ascontiguousarray(a.reshape(-1, 128).T)


def kernel_fused(_TL=None, _runner=None, **inp):
    TL = _TL or (S // 2)
    SL = 2 * TL
    f32 = np.float32
    C = lambda a: np.ascontiguousarray(a, dtype=f32)
    x_full = np.asarray(inp["L0_attn_x"], dtype=f32)
    nc = _get("fused", build_fused, TL)
    ch_t, sh_t = _rope_tables(SL)
    wI = C(np.concatenate([inp["L1_attn_W_Iq"], inp["L1_attn_W_Ik"],
                           inp["L1_attn_W_Iw"], inp["L1_ffn_router_w"]], axis=1))
    w5 = C(np.stack([inp["L1_attn_W_q"], inp["L1_attn_W_k"], inp["L1_attn_W_v"],
                     inp["L1_attn_W_gv"], inp["L1_attn_W_go"]], 0))
    hb = np.asarray(inp["L1_attn_head_bias"], dtype=f32)
    gbias = np.asarray(inp["L1_attn_gate_bias"], dtype=f32)
    woF = np.asarray(inp["L0_ffn_Wo"], dtype=f32)
    woL1 = np.asarray(inp["L1_attn_W_o"], dtype=f32)
    in_maps = []
    for c in range(NCORE):
        b, th = c // 2, c % 2
        hh = th
        i0, i1 = 2 * hh, 2 * hh + 1
        sl = slice(th * TL, (th + 1) * TL)
        im = {
            "x": C(x_full[b, sl]),
            "l0a_wq": C(inp["L0_attn_Wq"]), "l0a_wk": C(inp["L0_attn_Wk"]),
            "l0a_wv": C(inp["L0_attn_Wv"]), "l0a_wo": C(inp["L0_attn_Wo"]),
            "l0a_bo": C(inp["L0_attn_bo"].reshape(1, D)),
            "l0a_g": C(inp["L0_attn_gamma"].reshape(1, D)),
            "l0a_b": C(inp["L0_attn_beta"].reshape(1, D)),
            "l0f_g": C(inp["L0_ffn_ln_g"].reshape(1, D)),
            "l0f_b": C(inp["L0_ffn_ln_b"].reshape(1, D)),
            "l0f_wq": C(inp["L0_ffn_Wq"]), "l0f_wk": C(inp["L0_ffn_Wk"]),
            "l0f_wv": C(inp["L0_ffn_Wv"]),
            "l0f_qs": C(np.asarray(inp["L0_ffn_q_scale"]).reshape(1, D)),
            "l0f_ks": C(np.asarray(inp["L0_ffn_k_scale"]).reshape(1, D)),
            "l0f_wo": C(woF[hh * 512:(hh + 1) * 512, :]),
            "l0f_gw": C(inp["L0_ffn_gate_w"]), "l0f_gb": C(inp["L0_ffn_gate_b"].reshape(1, 8)),
            "l0f_w1": C(inp["L0_ffn_w1"]), "l0f_b1": C(inp["L0_ffn_b1"]),
            "l0f_w2": C(inp["L0_ffn_w2"]), "l0f_b2": C(inp["L0_ffn_b2"]),
            "wI": wI, "rb": C(inp["L1_ffn_router_b"].reshape(1, 8)), "w5": w5,
            "ch": C(ch_t[sl]), "sh": C(sh_t[sl]),
            "hb": C(np.stack([hb[4 * i0:4 * i0 + 4].reshape(1, 4),
                              hb[4 * i1:4 * i1 + 4].reshape(1, 4)])),
            "gbx": C(np.stack([gbias[i0].reshape(1, 1), gbias[i1].reshape(1, 1)])),
            "vema": C(np.asarray(inp["L1_attn_variance_ema"]).reshape(1, 1)),
            "l1a_wo": C(woL1[hh * 512:(hh + 1) * 512, :]),
            "l1f_w1": C(inp["L1_ffn_w1"]), "l1f_w2": C(inp["L1_ffn_w2"]),
            "hoff": np.array([[hh]], dtype=np.int32),
        }
        in_maps.append(im)
    res = _runner(nc, in_maps) if _runner else _run(nc, in_maps)
    out = np.zeros((B, SL, D), f32)
    for c in range(NCORE):
        b, th = c // 2, c % 2
        out[b, th * TL:(th + 1) * TL] = res[c]["x2"]
    return out


FUSED = False


def kernel(**inp):
    if FUSED:
        return kernel_fused(**inp)
    return kernel_unfused(**inp)


def kernel_unfused(**inp):
    TL = S // 2
    f32 = np.float32
    C = lambda a: np.ascontiguousarray(a, dtype=f32)
    x_full = np.asarray(inp["L0_attn_x"], dtype=f32)
    cores = [(c // 2, c % 2) for c in range(NCORE)]
    sl = lambda th: slice(th * TL, (th + 1) * TL)

    nc = _get("pre0a", build_pre0a, TL)
    im = [{"x": C(x_full[b, sl(th)]),
           "wq": C(inp["L0_attn_Wq"]), "wk": C(inp["L0_attn_Wk"]), "wv": C(inp["L0_attn_Wv"])}
          for b, th in cores]
    r1 = _run(nc, im)

    nc = _get("pre0b", build_pre0b, TL)
    qsc = C(np.asarray(inp["L0_ffn_q_scale"]).reshape(1, D))
    ksc = C(np.asarray(inp["L0_ffn_k_scale"]).reshape(1, D))
    im = [{"xn": r1[c]["xn0"],
           "wq": C(inp["L0_ffn_Wq"]), "wk": C(inp["L0_ffn_Wk"]), "wv": C(inp["L0_ffn_Wv"]),
           "ln_g": C(inp["L0_ffn_ln_g"].reshape(1, D)), "ln_b": C(inp["L0_ffn_ln_b"].reshape(1, D)),
           "qsc": qsc, "ksc": ksc}
          for c in range(NCORE)]
    r2 = _run(nc, im)

    nc = _get("attn0", build_attn0, S, 8, QKSCALE, False)
    qf = [np.concatenate([r2[2 * b]["qh"], r2[2 * b + 1]["qh"]], 0) for b in range(B)]
    kf = [np.concatenate([r2[2 * b]["kh"], r2[2 * b + 1]["kh"]], 0) for b in range(B)]
    vf = [np.concatenate([r2[2 * b]["vh"], r2[2 * b + 1]["vh"]], 0) for b in range(B)]
    woF = np.asarray(inp["L0_ffn_Wo"], dtype=f32)
    im = [{"q": np.ascontiguousarray(qf[b][:, hh * 512:(hh + 1) * 512]),
           "k": np.ascontiguousarray(kf[b][:, hh * 512:(hh + 1) * 512]),
           "v": np.ascontiguousarray(vf[b][:, hh * 512:(hh + 1) * 512]),
           "wo": C(woF[hh * 512:(hh + 1) * 512, :])}
          for b, hh in cores]
    r3 = _run(nc, im)

    nc = _get("post0", build_post0, TL)
    im = [{"xn": r1[c]["xn0"], "eq": r1[c]["eq"],
           "ctxa": r1[2 * b]["ctx"], "ctxb": r1[2 * b + 1]["ctx"],
           "wo": C(inp["L0_attn_Wo"]), "bo": C(inp["L0_attn_bo"].reshape(1, D)),
           "gam": C(inp["L0_attn_gamma"].reshape(1, D)), "bet": C(inp["L0_attn_beta"].reshape(1, D)),
           "ya": np.ascontiguousarray(r3[2 * b]["yT"][th]),
           "yb": np.ascontiguousarray(r3[2 * b + 1]["yT"][th]),
           "lng": C(inp["L0_ffn_ln_g"].reshape(1, D)), "lnb": C(inp["L0_ffn_ln_b"].reshape(1, D))}
          for c, (b, th) in enumerate(cores)]
    r4 = _run(nc, im)

    nc = _get("moe0", build_moe0, TL)
    im = [{"h2": r4[c]["h2"], "gw": C(inp["L0_ffn_gate_w"]), "gb": C(inp["L0_ffn_gate_b"].reshape(1, 8)),
           "w1": C(inp["L0_ffn_w1"]), "b1": C(inp["L0_ffn_b1"]),
           "w2": C(inp["L0_ffn_w2"]), "b2": C(inp["L0_ffn_b2"]),
           "lng": C(inp["L0_ffn_ln_g"].reshape(1, D)), "lnb": C(inp["L0_ffn_ln_b"].reshape(1, D)),
           "x": C(x_full[b, sl(th)]), "s0": r4[c]["s0"]}
          for c, (b, th) in enumerate(cores)]
    r5 = _run(nc, im)

    nc = _get("pre1", build_pre1, TL)
    wI = C(np.concatenate([inp["L1_attn_W_Iq"], inp["L1_attn_W_Ik"],
                           inp["L1_attn_W_Iw"], inp["L1_ffn_router_w"]], axis=1))
    w5 = C(np.stack([inp["L1_attn_W_q"], inp["L1_attn_W_k"], inp["L1_attn_W_v"],
                     inp["L1_attn_W_gv"], inp["L1_attn_W_go"]], 0))
    ch_t, sh_t = _rope_tables()
    im = [{"x1": r5[c]["x1"], "wI": wI, "rb": C(inp["L1_ffn_router_b"].reshape(1, 8)),
           "w5": w5, "ch": C(ch_t[sl(th)]), "sh": C(sh_t[sl(th)])}
          for c, (b, th) in enumerate(cores)]
    r6 = _run(nc, im)

    nc = _get("sel1", build_sel1, S)
    qI_full = [np.concatenate([r6[2 * b]["qI"], r6[2 * b + 1]["qI"]], 0) for b in range(B)]
    kI_full = [np.concatenate([r6[2 * b]["kI"], r6[2 * b + 1]["kI"]], 0) for b in range(B)]
    ws_full = [np.concatenate([r6[2 * b]["wsig"], r6[2 * b + 1]["wsig"]], 0) for b in range(B)]
    hb = np.asarray(inp["L1_attn_head_bias"], dtype=f32)
    gbias = np.asarray(inp["L1_attn_gate_bias"], dtype=f32)
    vema = np.asarray(inp["L1_attn_variance_ema"], dtype=f32).reshape(1, 1)
    im = []
    for c in range(NCORE):
        us = [(c * 2 + k) // 4 for k in range(2)], [(c * 2 + k) % 4 for k in range(2)]
        qIu = np.stack([qI_full[us[0][k]][:, us[1][k] * 32:(us[1][k] + 1) * 32] for k in range(2)])
        kIu = np.stack([kI_full[us[0][k]][:, us[1][k] * 32:(us[1][k] + 1) * 32] for k in range(2)])
        wsu = np.stack([ws_full[us[0][k]][:, us[1][k]] for k in range(2)], axis=1)
        hbu = np.stack([hb[4 * us[1][k]:4 * us[1][k] + 4].reshape(1, 4) for k in range(2)])
        gbu = np.stack([gbias[us[1][k]].reshape(1, 1) for k in range(2)])
        im.append({"qI": C(qIu), "kI": C(kIu), "ws": C(wsu), "hb": C(hbu),
                   "gbx": C(gbu), "vema": vema})
    r7 = _run(nc, im)

    thr_full = [np.zeros((S, 16), f32) for _ in range(B)]
    for c in range(NCORE):
        for k in range(2):
            u = c * 2 + k
            bu, iu = u // 4, u % 4
            thr_full[bu][:, 4 * iu:4 * iu + 4] = r7[c]["thr"][:, k, :]

    nc = _get("attn1", build_attn1, S, 8)
    rq = [np.concatenate([r6[2 * b]["rq"], r6[2 * b + 1]["rq"]], 0) for b in range(B)]
    rk = [np.concatenate([r6[2 * b]["rk"], r6[2 * b + 1]["rk"]], 0) for b in range(B)]
    vg = [np.concatenate([r6[2 * b]["vg"], r6[2 * b + 1]["vg"]], 0) for b in range(B)]
    gos = [np.concatenate([r6[2 * b]["gos"], r6[2 * b + 1]["gos"]], 0) for b in range(B)]
    woL1 = np.asarray(inp["L1_attn_W_o"], dtype=f32)
    im = [{"q": np.ascontiguousarray(rq[b][:, hh * 512:(hh + 1) * 512]),
           "k": np.ascontiguousarray(rk[b][:, hh * 512:(hh + 1) * 512]),
           "v": np.ascontiguousarray(vg[b][:, hh * 512:(hh + 1) * 512]),
           "go": np.ascontiguousarray(gos[b][:, hh * 512:(hh + 1) * 512]),
           "wo": C(woL1[hh * 512:(hh + 1) * 512, :]),
           "qIs": np.ascontiguousarray(qI_full[b][:, hh * 64:(hh + 1) * 64]),
           "kIs": np.ascontiguousarray(kI_full[b][:, hh * 64:(hh + 1) * 64]),
           "thr": np.ascontiguousarray(thr_full[b][:, hh * 8:(hh + 1) * 8])}
          for b, hh in cores]
    r8 = _run(nc, im)

    nc = _get("moe1", build_moe1, TL)
    im = [{"xn1": r6[c]["xn1b"], "eps": r6[c]["eps"],
           "w1": C(inp["L1_ffn_w1"]), "w2": C(inp["L1_ffn_w2"]),
           "x1": r5[c]["x1"],
           "ya": np.ascontiguousarray(r8[2 * b]["yT"][th]),
           "yb": np.ascontiguousarray(r8[2 * b + 1]["yT"][th])}
          for c, (b, th) in enumerate(cores)]
    r9 = _run(nc, im)

    out = np.zeros((B, S, D), f32)
    for c, (b, th) in enumerate(cores):
        out[b, sl(th)] = r9[c]["x2"]
    return out


if __name__ == "__main__":
    nc = build_moe0(512)
    print("moe0 ok:", nc.n_instructions(), "instructions")
    nc = build_pre0a(512)
    print("pre0a ok:", nc.n_instructions(), "instructions")
    nc = build_pre0b(512)
    print("pre0b ok:", nc.n_instructions(), "instructions")
    nc = build_attn0(512)
    print("attn0 ok:", nc.n_instructions(), "instructions")
    nc = build_post0(512)
    print("post0 ok:", nc.n_instructions(), "instructions")


# version 15
# speedup vs baseline: 1.0685x; 1.0685x over previous
import numpy as np
import ml_dtypes
from contextlib import ExitStack

import concourse.bass as bass
import concourse.mybir as mybir
from concourse.bass_utils import run_bass_kernel_spmd

F32 = mybir.dt.float32
BF16 = mybir.dt.bfloat16
AF = mybir.ActivationFunctionType
ALU = mybir.AluOpType
AX = mybir.AxisListType

B = 4
S = 4096
D = 1024
DC = D // 128
NCORE = 8
LN_EPS = 1e-5
BF = ml_dtypes.bfloat16



class Seq:

    def __init__(self, prog, eng, name):
        self.p = prog
        self.e = eng
        self.name = name
        self.sem = prog.sem(name)
        self.n = prog.counts.get(name, 0)

    def op(self, fname, *a, dep=True, **kw):
        if dep and self.n:
            self.e.wait_ge(self.sem, self.n)
        inst = getattr(self.e, fname)(*a, **kw)
        inst.then_inc(self.sem, 1)
        self.n += 1
        self.p.counts[self.name] = self.n
        return inst

    def mark(self, tokname):
        self.p.tokens[self.p.prefix + tokname] = (self.name, self.n)

    def wait_tok(self, tokname, may_miss=False):
        import re
        m = re.search(r"-?\d+$", tokname)
        if m and int(m.group()) < 0:
            return
        if self.p.prev_tokens is None:
            return
        tokname = self.p.prefix + tokname
        if may_miss and tokname not in self.p.prev_tokens:
            return
        semname, cnt = self.p.prev_tokens[tokname]
        if cnt > 0:
            self.e.wait_ge(self.p.sem(semname), cnt)

    def wait(self, semname, val):
        semname = self.p.rsem(semname)
        if val > 0:
            self.e.wait_ge(self.p.sem(semname), val)

    def wait_rel(self, semname, val):
        semname = self.p.rsem(semname)
        base = self.p.pbase.get(semname, 0)
        if base + val > 0:
            self.e.wait_ge(self.p.sem(semname), base + val)

    def dma(self, semname, out, in_, **kw):
        return self.p.dma(self.e, semname, out, in_, **kw)


class Prog:
    MAXC = 16

    def __init__(self, name, prev_tokens=None, prev_consts=None):
        self.nc = bass.Bass("TRN2", target_bir_lowering=False)
        self.es = ExitStack()
        self.sems = {}
        self.dmac = {}
        self.counts = {}
        self.tokens = {}
        self.prev_tokens = prev_tokens
        self.name = name
        self.const_vals = list(prev_consts) if prev_consts else []
        self._ctile = None
        self.prefix = ""
        self.io = {}
        self.pbase = {}
        self.cur_es = self.es

    def sem(self, name):
        name = self.rsem(name) if hasattr(self, "dmac") else name
        if name not in self.sems:
            self.sems[name] = self.es.enter_context(self.nc.semaphore(name))
            self.dmac.setdefault(name, 0)
        return self.sems[name]

    def dma(self, eng, semname, out, in_, **kw):
        if "GpSimd" in type(eng).__name__:
            semname = self.prefix + semname
        s = self.sem(semname)
        eng.dma_start(out=out, in_=in_, **kw).then_inc(s, 16)
        self.dmac[semname] += 16
        return self.dmac[semname]

    def rsem(self, name):
        if self.prefix + name in self.dmac:
            return self.prefix + name
        return name

    def dtok(self, tokname, semname):
        semname = self.rsem(semname)
        self.tokens[self.prefix + tokname] = (semname, self.dmac[semname])

    def phase(self, name):
        from contextlib import contextmanager

        @contextmanager
        def _cm():
            old_prefix, old_es, old_pbase = self.prefix, self.cur_es, self.pbase
            self.prefix = name + "."
            self.pbase = dict(self.dmac)
            self.cur_es = ExitStack()
            try:
                yield
            finally:
                self.cur_es.close()
                self.prefix, self.cur_es, self.pbase = old_prefix, old_es, old_pbase
        return _cm()

    def ctile(self):
        if self._ctile is None:
            self._ctile = self.es.enter_context(
                self.nc.sbuf_tensor("_consts", [128, self.MAXC], F32))
        return self._ctile

    def cc(self, value):
        value = float(value)
        if value not in self.const_vals:
            self.const_vals.append(value)
        assert len(self.const_vals) <= self.MAXC
        return self.ctile()[:, self.const_vals.index(value):self.const_vals.index(value) + 1]

    def emit_consts(self, gseq):
        t = self.ctile()
        vals = self.const_vals + [0.0] * (self.MAXC - len(self.const_vals))
        for j, v in enumerate(vals):
            gseq.op("memset", t[:, j:j + 1], v)
        gseq.mark("consts")

    def sb(self, name, shape, dtype):
        return self.cur_es.enter_context(self.nc.sbuf_tensor(self.prefix + name, shape, dtype))

    def ps(self, name, shape, dtype=F32):
        return self.cur_es.enter_context(self.nc.psum_tensor(self.prefix + name, shape, dtype))

    def din(self, name, shape, dtype=F32):
        if name in self.io:
            return self.io[name]
        return self.nc.dram_tensor(self.prefix + name, shape, dtype, kind="ExternalInput").ap()

    def dout(self, name, shape, dtype=F32):
        if name in self.io:
            return self.io[name]
        return self.nc.dram_tensor(self.prefix + name, shape, dtype, kind="ExternalOutput").ap()

    def close(self):
        self.es.close()
        return self.nc


def build_twice(fn, *args, **kw):
    p1 = Prog("pass1")
    fn(p1, *args, **kw)
    toks = dict(p1.tokens)
    p1.close()
    p2 = Prog("pass2", prev_tokens=toks, prev_consts=p1.const_vals)
    fn(p2, *args, **kw)
    assert p1.counts == p2.counts, (p1.counts, p2.counts)
    p2.close()
    return p2.nc


CQ = 8.0
CK = 8.0


def emit_pre0a(p, TL):
    NT = TL // 128
    nc = p.nc

    x = p.din("x", [TL, D])
    wq = p.din("wq", [D, D]); wk = p.din("wk", [D, D]); wv = p.din("wv", [D, D])
    xn0_o = p.dout("xn0", [TL, D])
    eq_o = p.dout("eq", [D, TL], BF16)
    ctx_o = p.dout("ctx", [D, D + 1])

    xt = p.sb("xt", [128, 2, D], F32)
    xnt = p.sb("xnt", [128, 2, D], F32)
    xnt16 = p.sb("xnt16", [128, 2, D], BF16)
    sqd = p.sb("sqd", [128, D], F32)
    stats = p.sb("stats", [128, 8], F32)
    xnf = p.sb("xnf", [128, DC, TL], BF16)
    ekv = p.sb("ekv", [128, NT, 2 * D], BF16)
    w3 = p.sb("w3", [128, 3, DC, D], BF16)
    ident = p.sb("ident", [128, 128], BF16)
    onesc = p.sb("onesc", [128, 1], BF16)
    eqv = p.sb("eqv", [128, 2, 512], BF16)
    ctxv = p.sb("ctxv", [128, 2, 513], F32)

    pa = p.ps("pa", [128, 512], F32)
    pb = p.ps("pb", [128, 512], F32)
    pt = p.ps("pt", [128, 512], BF16)
    pd = p.ps("pd", [128, 16], F32)

    with nc.Block() as blk:
        @blk.sync
        def _(sy):
            for i in range(NT):
                s = Seq(p, sy, "sy")
                s.wait_tok(f"xtfree{i}", may_miss=True)
                p.dma(sy, f"ld{i % 2}", xt[:, i % 2], x.rearrange("(n p) d -> n p d", p=128)[i])
                p.dtok(f"xld{i}", f"ld{i % 2}")

        @blk.gpsimd
        def _(g):
            gg = Seq(p, g, "g")
            for wi, w in enumerate((wq, wk, wv)):
                for c in range(DC):
                    p.dma(g, "ldw", w3[:, wi, c], w[c * 128:(c + 1) * 128, :])
            p.dtok("wload", "ldw")
            gg.op("memset", ident[:], 1.0)
            gg.op("affine_select", ident[:], ident[:], [[-1, 128]], ALU.is_equal, 0.0,
                  base=0, channel_multiplier=1)
            gg.op("memset", onesc[:], 1.0)
            gg.mark("identdone")
            p.emit_consts(gg)

        @blk.scalar
        def _(sc):
            s = Seq(p, sc, "sc")
            for i in range(NT):
                s.wait_rel(f"ld{i % 2}", 16 * (i // 2 + 1))
                s.wait_tok(f"statfreev{i}", may_miss=True)
                s.op("activation", sqd[:], xt[:, i % 2], AF.Square, accum_out=stats[:, 1:2])
                s.mark(f"ssq{i}")
                s.wait_tok(f"var{i}")
                s.wait_tok("consts")
                s.op("activation", stats[:, 3:4], stats[:, 4:5], AF.Ln, bias=p.cc(LN_EPS), scale=1.0)
                s.op("activation", stats[:, 3:4], stats[:, 3:4], AF.Exp, bias=0.0, scale=-0.5)
                s.mark(f"rstd{i}")

        @blk.vector
        def _(ve):
            v = Seq(p, ve, "v")
            for i in range(NT):
                v.wait_rel(f"ld{i % 2}", 16 * (i // 2 + 1))
                v.wait_tok(f"ssq{i}")
                v.op("reduce_sum", stats[:, 0:1], xt[:, i % 2], axis=AX.X)
                v.op("tensor_scalar", stats[:, 2:3], stats[:, 0:1], 1.0 / D, None, ALU.mult)
                v.op("tensor_scalar", stats[:, 5:6], stats[:, 2:3], stats[:, 2:3], None, ALU.mult)
                v.op("tensor_scalar", stats[:, 4:5], stats[:, 1:2], 1.0 / D, stats[:, 5:6],
                     ALU.mult, ALU.subtract)
                v.mark(f"var{i}")
                v.mark(f"statfreev{i + 1}")
                v.wait_tok(f"rstd{i}")
                v.wait_tok(f"xnstored{i - 2}")
                v.op("tensor_scalar", xnt[:, i % 2], xt[:, i % 2], stats[:, 2:3], stats[:, 3:4],
                     ALU.subtract, ALU.mult)
                v.mark(f"xtfree{i + 2}")
                v.wait_tok(f"tpb{i - 2}")
                v.op("tensor_copy", xnt16[:, i % 2], xnt[:, i % 2])
                v.mark(f"xn16{i}")
                v.wait_tok(f"tpa{i}")
                v.op("tensor_copy", xnf[:, 0:4, i * 128:(i + 1) * 128], pt[:].rearrange("p (c t) -> p c t", t=128))
                v.mark(f"tpaev{i}")
                v.wait_tok(f"tpb{i}")
                v.op("tensor_copy", xnf[:, 4:8, i * 128:(i + 1) * 128], pt[:].rearrange("p (c t) -> p c t", t=128))
                v.mark(f"tpbev{i}")

        @blk.tensor
        def _(te):
            t = Seq(p, te, "t")
            t.wait_tok("identdone")
            for i in range(NT):
                t.wait_tok(f"xn16{i}")
                t.wait_tok(f"tpbev{i - 1}")
                for c in range(4):
                    inst = te.transpose(pt[:, c * 128:(c + 1) * 128], xnt16[:, i % 2, c * 128:(c + 1) * 128], ident[:])
                inst.then_inc(p.sem("t"), 1); t.n += 1; p.counts["t"] = t.n
                t.mark(f"tpa{i}")
                t.wait_tok(f"tpaev{i}")
                for c in range(4):
                    inst = te.transpose(pt[:, c * 128:(c + 1) * 128], xnt16[:, i % 2, (4 + c) * 128:(5 + c) * 128], ident[:])
                inst.then_inc(p.sem("t"), 1); t.n += 1; p.counts["t"] = t.n
                t.mark(f"tpb{i}")

        @blk.gpsimd
        def _(g):
            gg = Seq(p, g, "g2")
            for i in range(NT):
                gg.wait_tok(f"xn16{i}")
                p.dma(g, f"sxn{i % 2}", xn0_o.rearrange("(n p) d -> n p d", p=128)[i], xnt[:, i % 2])
                p.dtok(f"xnstored{i}", f"sxn{i % 2}")

    with nc.Block() as blk2:
        @blk2.tensor
        def _(te):
            t = Seq(p, te, "t2")
            t.wait_tok("wload")
            t.wait_tok(f"tpbev{NT - 1}")
            gi = 0
            for i in range(NT):
                for proj in (1, 2):
                    for nn in range(2):
                        bank = (pa, pb)[gi % 2]
                        t.wait_tok(f"kvev{gi - 2}")
                        for dc in range(DC):
                            inst = te.matmul(bank[:], xnf[:, dc, i * 128:(i + 1) * 128],
                                             w3[:, proj, dc, nn * 512:(nn + 1) * 512],
                                             start=(dc == 0), stop=(dc == DC - 1))
                        inst.then_inc(p.sem("t2"), 1); t.n += 1; p.counts["t2"] = t.n
                        t.mark(f"kv{gi}")
                        gi += 1
            for do in range(DC):
                for ts in range(TL // 512):
                    bank = (pa, pb)[gi % 2]
                    t.wait_tok(f"kvev{gi - 2}")
                    for dc in range(DC):
                        inst = te.matmul(bank[:], w3[:, 0, dc, do * 128:(do + 1) * 128],
                                         xnf[:, dc, ts * 512:(ts + 1) * 512],
                                         start=(dc == 0), stop=(dc == DC - 1))
                    inst.then_inc(p.sem("t2"), 1); t.n += 1; p.counts["t2"] = t.n
                    t.mark(f"kv{gi}")
                    gi += 1
            ci = 0
            for dc in range(DC):
                for nn in range(2):
                    bank = (pa, pb)[gi % 2]
                    t.wait_tok(f"kvev{gi - 2}")
                    t.wait_tok("ekvready")
                    t.wait_tok("ekready")
                    for i in range(NT):
                        inst = te.matmul(bank[:], ekv[:, i, dc * 128:(dc + 1) * 128],
                                         ekv[:, i, D + nn * 512:D + (nn + 1) * 512],
                                         start=(i == 0), stop=(i == NT - 1))
                    inst.then_inc(p.sem("t2"), 1); t.n += 1; p.counts["t2"] = t.n
                    t.mark(f"kv{gi}")
                    gi += 1
                t.wait_tok(f"denev{ci - 1}")
                for i in range(NT):
                    inst = te.matmul(pd[:, 0:1], ekv[:, i, dc * 128:(dc + 1) * 128], onesc[:],
                                     start=(i == 0), stop=(i == NT - 1))
                inst.then_inc(p.sem("t2"), 1); t.n += 1; p.counts["t2"] = t.n
                t.mark(f"den{ci}")
                ci += 1

        @blk2.scalar
        def _(sc):
            s = Seq(p, sc, "sc2")
            s.wait_tok("consts")
            gi = 0
            for i in range(NT):
                for proj in (1, 2):
                    for nn in range(2):
                        if proj == 1:
                            bank = (pa, pb)[gi % 2]
                            s.wait_tok(f"kv{gi}")
                            s.op("activation", ekv[:, i, nn * 512:(nn + 1) * 512], bank[:],
                                 AF.Exp, bias=p.cc(-CK), scale=1.0)
                            s.mark(f"kvev{gi}")
                        gi += 1
            s.mark("ekready")
            for do in range(DC):
                for ts in range(TL // 512):
                    bank = (pa, pb)[gi % 2]
                    s.wait_tok(f"kv{gi}")
                    s.wait_tok(f"eqst{gi - 2}", may_miss=True)
                    s.op("activation", eqv[:, gi % 2], bank[:], AF.Exp, bias=p.cc(-CQ), scale=1.0)
                    s.mark(f"kvev{gi}")
                    s.mark(f"eqv{gi}")
                    gi += 1

        @blk2.vector
        def _(ve):
            v = Seq(p, ve, "v2")
            gi = 0
            for i in range(NT):
                for proj in (1, 2):
                    for nn in range(2):
                        if proj == 2:
                            bank = (pa, pb)[gi % 2]
                            v.wait_tok(f"kv{gi}")
                            v.op("tensor_copy", ekv[:, i, D + nn * 512:D + (nn + 1) * 512], bank[:])
                            v.mark(f"kvev{gi}")
                        gi += 1
            v.mark("ekvready")
            for do in range(DC):
                for ts in range(TL // 512):
                    gi += 1
            ci = 0
            for dc in range(DC):
                for nn in range(2):
                    bank = (pa, pb)[gi % 2]
                    v.wait_tok(f"kv{gi}")
                    v.wait_tok(f"ctxst{(dc * 2 + nn) - 2}", may_miss=True)
                    v.op("tensor_copy", ctxv[:, (dc * 2 + nn) % 2, 0:512], bank[:])
                    v.mark(f"kvev{gi}")
                    v.mark(f"ctxv{dc * 2 + nn}")
                    gi += 1
                v.wait_tok(f"den{ci}")
                v.op("tensor_copy", ctxv[:, (dc * 2 + 1) % 2, 512:513], pd[:, 0:1])
                v.mark(f"denev{ci}")
                ci += 1

        @blk2.gpsimd
        def _(g):
            gi = NT * 4
            for do in range(DC):
                for ts in range(TL // 512):
                    gg = Seq(p, g, "g3")
                    gg.wait_tok(f"eqv{gi}")
                    p.dma(g, f"seq{gi % 2}", eq_o[do * 128:(do + 1) * 128, ts * 512:(ts + 1) * 512],
                          eqv[:, gi % 2])
                    p.dtok(f"eqst{gi}", f"seq{gi % 2}")
                    gi += 1
            for dc in range(DC):
                for nn in range(2):
                    gg = Seq(p, g, "g3")
                    gg.wait_tok(f"ctxv{dc * 2 + nn}")
                    if nn == 1:
                        gg.wait_tok(f"denev{dc}")
                    ap = ctx_o[dc * 128:(dc + 1) * 128, nn * 512:nn * 512 + (513 if nn == 1 else 512)]
                    src = ctxv[:, (dc * 2 + nn) % 2, 0:(513 if nn == 1 else 512)]
                    p.dma(g, f"sctx{(dc * 2 + nn) % 2}", ap, src)
                    p.dtok(f"ctxst{dc * 2 + nn}", f"sctx{(dc * 2 + nn) % 2}")

        @blk2.sync
        def _(sy):
            s = Seq(p, sy, "syf")
            for nm in ("sxn0", "sxn1", "seq0", "seq1", "sctx0", "sctx1"):
                if p.dmac.get(p.rsem(nm)):
                    s.wait(nm, p.dmac[p.rsem(nm)])


def build_pre0a(TL):
    return build_twice(emit_pre0a, TL)



L2_EPS = 1e-12


def emit_pre0b(p, TL):
    NT = TL // 128
    nc = p.nc

    xn = p.din("xn", [TL, D])
    wq = p.din("wq", [D, D]); wk = p.din("wk", [D, D]); wv = p.din("wv", [D, D])
    ln_g = p.din("ln_g", [1, D]); ln_b = p.din("ln_b", [1, D])
    qsc = p.din("qsc", [1, D]); ksc = p.din("ksc", [1, D])
    qh_o = p.dout("qh", [TL, D], BF16)
    kh_o = p.dout("kh", [TL, D], BF16)
    vh_o = p.dout("vh", [TL, D], BF16)

    xt = p.sb("xt", [128, 2, D], F32)
    ht16 = p.sb("ht16", [128, 2, D], BF16)
    sqd = p.sb("sqd", [128, 2048], F32)
    stats = p.sb("stats", [128, 8], F32)
    hf = p.sb("hf", [128, DC, TL], BF16)
    w3 = p.sb("w3", [128, 3, DC, D], BF16)
    ident = p.sb("ident", [128, 128], BF16)
    gamb = p.sb("gamb", [128, 2, D], F32)
    qks = p.sb("qks", [128, 2 * D], F32)
    qkt = p.sb("qkt", [128, 2, 2 * D], F32)
    vt16 = p.sb("vt16", [128, 2, D], BF16)
    qk16 = p.sb("qk16", [128, 2, 2 * D], BF16)
    ssq32 = p.sb("ssq32", [128, 2, 32], F32)

    pa = p.ps("pa", [128, 512], F32)
    pb = p.ps("pb", [128, 512], F32)
    pt = p.ps("pt", [128, 512], BF16)

    with nc.Block() as blk:
        @blk.sync
        def _(sy):
            s = Seq(p, sy, "sy")
            p.dma(sy, "ldc", gamb[:, 0], ln_g.to_broadcast([128, D]))
            p.dma(sy, "ldc", gamb[:, 1], ln_b.to_broadcast([128, D]))
            p.dma(sy, "ldc", qks[:, 0:D], qsc.to_broadcast([128, D]))
            p.dma(sy, "ldc", qks[:, D:2 * D], ksc.to_broadcast([128, D]))
            for i in range(NT):
                s.wait_tok(f"xtfree{i}", may_miss=True)
                p.dma(sy, f"ld{i % 2}", xt[:, i % 2], xn.rearrange("(n p) d -> n p d", p=128)[i])

        @blk.gpsimd
        def _(g):
            gg = Seq(p, g, "g")
            for wi, w in enumerate((wq, wk, wv)):
                for c in range(DC):
                    p.dma(g, "ldw", w3[:, wi, c], w[c * 128:(c + 1) * 128, :])
            p.dtok("wload", "ldw")
            gg.op("memset", ident[:], 1.0)
            gg.op("affine_select", ident[:], ident[:], [[-1, 128]], ALU.is_equal, 0.0,
                  base=0, channel_multiplier=1)
            gg.mark("identdone")
            p.emit_consts(gg)

        @blk.scalar
        def _(sc):
            s = Seq(p, sc, "sc")
            s.wait_tok("consts")
            for i in range(NT):
                s.wait_rel(f"ld{i % 2}", 16 * (i // 2 + 1))
                s.wait_tok(f"statfreev{i}", may_miss=True)
                s.op("activation", sqd[:, 0:D], xt[:, i % 2], AF.Square, accum_out=stats[:, 1:2])
                s.mark(f"ssq{i}")
                s.wait_tok(f"var{i}")
                s.op("activation", stats[:, 3:4], stats[:, 4:5], AF.Ln, bias=p.cc(LN_EPS), scale=1.0)
                s.op("activation", stats[:, 3:4], stats[:, 3:4], AF.Exp, bias=0.0, scale=-0.5)
                s.mark(f"rstd{i}")

        @blk.vector
        def _(ve):
            v = Seq(p, ve, "v")
            v.wait_rel("ldc", 64)
            for i in range(NT):
                v.wait_rel(f"ld{i % 2}", 16 * (i // 2 + 1))
                v.wait_tok(f"ssq{i}")
                v.op("reduce_sum", stats[:, 0:1], xt[:, i % 2], axis=AX.X)
                v.op("tensor_scalar", stats[:, 2:3], stats[:, 0:1], 1.0 / D, None, ALU.mult)
                v.op("tensor_scalar", stats[:, 5:6], stats[:, 2:3], stats[:, 2:3], None, ALU.mult)
                v.op("tensor_scalar", stats[:, 4:5], stats[:, 1:2], 1.0 / D, stats[:, 5:6],
                     ALU.mult, ALU.subtract)
                v.mark(f"var{i}")
                v.mark(f"statfreev{i + 1}")
                v.wait_tok(f"rstd{i}")
                v.op("tensor_scalar", sqd[:, D:2 * D], xt[:, i % 2], stats[:, 2:3], stats[:, 3:4],
                     ALU.subtract, ALU.mult)
                v.mark(f"xtfree{i + 2}")
                v.op("tensor_tensor", sqd[:, D:2 * D], sqd[:, D:2 * D], gamb[:, 0], ALU.mult)
                v.wait_tok(f"tpb{i - 2}")
                v.op("tensor_tensor", ht16[:, i % 2], sqd[:, D:2 * D], gamb[:, 1], ALU.add)
                v.mark(f"h16{i}")
                v.wait_tok(f"tpa{i}")
                v.op("tensor_copy", hf[:, 0:4, i * 128:(i + 1) * 128],
                     pt[:].rearrange("p (c t) -> p c t", t=128))
                v.mark(f"tpaev{i}")
                v.wait_tok(f"tpb{i}")
                v.op("tensor_copy", hf[:, 4:8, i * 128:(i + 1) * 128],
                     pt[:].rearrange("p (c t) -> p c t", t=128))
                v.mark(f"tpbev{i}")

        @blk.tensor
        def _(te):
            t = Seq(p, te, "t")
            t.wait_tok("identdone")
            for i in range(NT):
                t.wait_tok(f"h16{i}")
                t.wait_tok(f"tpbev{i - 1}")
                for c in range(4):
                    inst = te.transpose(pt[:, c * 128:(c + 1) * 128],
                                        ht16[:, i % 2, c * 128:(c + 1) * 128], ident[:])
                inst.then_inc(p.sem("t"), 1); t.n += 1; p.counts["t"] = t.n
                t.mark(f"tpa{i}")
                t.wait_tok(f"tpaev{i}")
                for c in range(4):
                    inst = te.transpose(pt[:, c * 128:(c + 1) * 128],
                                        ht16[:, i % 2, (4 + c) * 128:(5 + c) * 128], ident[:])
                inst.then_inc(p.sem("t"), 1); t.n += 1; p.counts["t"] = t.n
                t.mark(f"tpb{i}")

    with nc.Block() as blk2:
        @blk2.tensor
        def _(te):
            t = Seq(p, te, "t2")
            t.wait_tok("wload")
            t.wait_tok(f"tpbev{NT - 1}")
            gi = 0
            for i in range(NT):
                for proj in range(3):
                    for nn in range(2):
                        bank = (pa, pb)[gi % 2]
                        t.wait_tok(f"pev{gi - 2}")
                        for dc in range(DC):
                            inst = te.matmul(bank[:], hf[:, dc, i * 128:(i + 1) * 128],
                                             w3[:, proj, dc, nn * 512:(nn + 1) * 512],
                                             start=(dc == 0), stop=(dc == DC - 1))
                        inst.then_inc(p.sem("t2"), 1); t.n += 1; p.counts["t2"] = t.n
                        t.mark(f"p{gi}")
                        gi += 1

        @blk2.vector
        def _(ve):
            v = Seq(p, ve, "v2")
            gi = 0
            for i in range(NT):
                for proj in range(3):
                    for nn in range(2):
                        bank = (pa, pb)[gi % 2]
                        v.wait_tok(f"p{gi}")
                        if proj < 2:
                            v.wait_tok(f"qkst{i - 2}", may_miss=True)
                            v.op("tensor_copy", qkt[:, i % 2, proj * D + nn * 512:proj * D + (nn + 1) * 512],
                                 bank[:])
                        else:
                            v.wait_tok(f"vst{i - 2}", may_miss=True)
                            v.op("tensor_copy", vt16[:, i % 2, nn * 512:(nn + 1) * 512], bank[:])
                            if nn == 1:
                                v.mark(f"v16{i}")
                        v.mark(f"pev{gi}")
                        gi += 1
                v.wait_tok(f"sq{i}")
                v.op("tensor_reduce", ssq32[:, i % 2], sqd[:].rearrange("p (g w) -> p g w", w=64),
                     AX.X, ALU.add)
                v.mark(f"ssqred{i}")
                v.wait_tok(f"rln{i}")
                v.op("tensor_tensor", qkt[:, i % 2].rearrange("p (g w) -> p g w", w=64),
                     qkt[:, i % 2].rearrange("p (g w) -> p g w", w=64),
                     ssq32[:, i % 2].to_broadcast([128, 32, 64]), ALU.mult)
                v.op("tensor_tensor", qk16[:, i % 2], qkt[:, i % 2], qks[:], ALU.mult)
                v.mark(f"qk16{i}")

        @blk2.scalar
        def _(sc):
            s = Seq(p, sc, "sc2")
            for i in range(NT):
                s.wait_tok(f"pev{i * 6 + 3}")
                s.op("activation", sqd[:], qkt[:, i % 2], AF.Square)
                s.mark(f"sq{i}")
                s.wait_tok(f"ssqred{i}")
                s.op("activation", ssq32[:, i % 2], ssq32[:, i % 2], AF.Ln, bias=p.cc(L2_EPS), scale=1.0)
                s.op("activation", ssq32[:, i % 2], ssq32[:, i % 2], AF.Exp, bias=0.0, scale=-0.5)
                s.mark(f"rln{i}")

        @blk2.gpsimd
        def _(g):
            gg = Seq(p, g, "g3")
            for i in range(NT):
                gg.wait_tok(f"qk16{i}")
                p.dma(g, f"sq{i % 2}", qh_o.rearrange("(n p) d -> n p d", p=128)[i], qk16[:, i % 2, 0:D])
                p.dma(g, f"sq{i % 2}", kh_o.rearrange("(n p) d -> n p d", p=128)[i], qk16[:, i % 2, D:2 * D])
                p.dtok(f"qkst{i}", f"sq{i % 2}")
                gg.wait_tok(f"v16{i}")
                p.dma(g, f"sv{i % 2}", vh_o.rearrange("(n p) d -> n p d", p=128)[i], vt16[:, i % 2])
                p.dtok(f"vst{i}", f"sv{i % 2}")

        @blk2.sync
        def _(sy):
            s = Seq(p, sy, "syf")
            for nm in ("sq0", "sq1", "sv0", "sv1"):
                if p.dmac.get(p.rsem(nm)):
                    s.wait(nm, p.dmac[p.rsem(nm)])


def build_pre0b(TL):
    return build_twice(emit_pre0b, TL)



QKSCALE = 10.0


def emit_attn0(p, SS, NH=8, scale=QKSCALE, gated=False, sel=False):
    NQT = SS // 128
    NPAIR = NH // 2
    nc = p.nc

    q_i = p.din("q", [SS, NH * 64], BF16)
    k_i = p.din("k", [SS, NH * 64], BF16)
    v_i = p.din("v", [SS, NH * 64], BF16)
    wo_i = p.din("wo", [NH * 64, D])
    if gated:
        go_i = p.din("go", [SS, NH * 64], BF16)
    if sel:
        qI_i = p.din("qIs", [SS, 64])
        kI_i = p.din("kIs", [SS, 64])
        thr_i = p.din("thr", [SS, NH])
    yT_o = p.dout("yT", [2, D, SS // 2])

    qf = p.sb("qf", [128, 2, NPAIR, 128], BF16)
    kf = p.sb("kf", [128, NPAIR, SS], BF16)
    vt = p.sb("vt", [128, SS // 128, NH * 64], BF16)
    gof = p.sb("gof", [128, NPAIR, SS], BF16) if gated else None
    wos = p.sb("wos", [64, NH, D], BF16)
    ssb = p.sb("ssb", [128, SS], F32)
    p16 = p.sb("p16", [128, SS], BF16)
    pT = p.sb("pT", [128, SS // 128, 128], BF16)
    racc = p.sb("racc", [128, 2, 8], F32)
    rst = p.sb("rst", [128, 4], F32)
    poS = p.sb("poS", [64, 128], BF16)
    yev = p.sb("yev", [128, 2, DC * 128], F32)
    ident = p.sb("ident", [128, 128], BF16)
    cmask = p.sb("cmask", [128, 4, 512], F32)
    if sel:
        qIt = p.sb("qIt", [128, 2, 64], F32)
        kIt = p.sb("kIt", [128, 2, 64], F32)
        qIf = p.sb("qIf", [64, SS], F32)
        kIf = p.sb("kIf", [64, SS], F32)
        thrs = p.sb("thrs", [128, SS // 128, NH], F32)
        madd = p.sb("madd", [128, 512], F32)
        ident32 = p.sb("ident32", [128, 128], F32)

    ps1 = p.ps("ps1", [128, 512], F32)
    ps2 = p.ps("ps2", [128, 512], F32)
    ptt = p.ps("ptt", [128, 1024], BF16)
    po = p.ps("po", [64, 128], F32)
    py = p.ps("py", [128, DC * 128], F32)
    if sel:
        pm = p.ps("pm", [128, 512], F32)
        ptq = p.ps("ptq", [64, 512], F32)

    NKB = lambda qt: qt // 4 + 1

    with nc.Block() as blk:
        @blk.sync
        def _(sy):
            for pc in range(NPAIR):
                p.dma(sy, "ldq", kf[:, pc], k_i[:, pc * 128:(pc + 1) * 128], transpose=True)
                if gated:
                    p.dma(sy, "ldq", gof[:, pc], go_i[:, pc * 128:(pc + 1) * 128], transpose=True)
            p.dma(sy, "ldq", vt[:], v_i.rearrange("(c p) n -> p c n", p=128))
            if sel:
                p.dma(sy, "ldq", thrs[:], thr_i.rearrange("(n p) h -> p n h", p=128))
            p.dtok("qkvload", "ldq")
            if sel:
                for tt in range(SS // 128):
                    s = Seq(p, sy, "sy")
                    s.wait_tok(f"qkI{tt - 2}")
                    p.dma(sy, f"lqi{tt % 2}", qIt[:, tt % 2], qI_i.rearrange("(n p) c -> n p c", p=128)[tt])
                    p.dma(sy, f"lqi{tt % 2}", kIt[:, tt % 2], kI_i.rearrange("(n p) c -> n p c", p=128)[tt])
            for qt in range(NQT):
                s = Seq(p, sy, "sy")
                s.wait_tok(f"qdone{qt - 2}")
                for pc in range(NPAIR):
                    p.dma(sy, f"lqt{qt % 2}", qf[:, qt % 2, pc],
                          q_i[qt * 128:(qt + 1) * 128, pc * 128:(pc + 1) * 128], transpose=True)

        @blk.gpsimd
        def _(g):
            gg = Seq(p, g, "g")
            for h in range(NH):
                p.dma(g, "ldw", wos[:, h], wo_i[h * 64:(h + 1) * 64, :])
            p.dtok("wload", "ldw")
            gg.op("memset", ident[:], 1.0)
            gg.op("affine_select", ident[:], ident[:], [[-1, 128]], ALU.is_equal, 0.0,
                  base=0, channel_multiplier=1)
            if sel:
                gg.op("memset", ident32[:], 1.0)
                gg.op("affine_select", ident32[:], ident32[:], [[-1, 128]], ALU.is_equal, 0.0,
                      base=0, channel_multiplier=1)
            gg.op("memset", cmask[:], 0.0)
            for m_ in range(4):
                gg.op("affine_select", cmask[:, m_], cmask[:, m_], [[-1, 512]], ALU.is_ge, -1e30,
                      base=128 * m_, channel_multiplier=1)
            gg.mark("identdone")
            p.emit_consts(gg)
            for qt in range(NQT):
                gg.wait_tok(f"yev{qt}")
                _r = (qt * 128) // (SS // 2)
                _off = qt * 128 - _r * (SS // 2)
                p.dma(g, f"sy{qt % 2}", yT_o[_r].rearrange("(c q) s -> q c s", q=128)[:, :, _off:_off + 128],
                      yev[:, qt % 2].rearrange("p (c t) -> p c t", t=128))
                p.dtok(f"yst{qt}", f"sy{qt % 2}")

        @blk.tensor
        def _(te):
            t = Seq(p, te, "t")
            t.wait_tok("identdone")
            t.wait_tok("qkvload")
            t.wait_tok("wload")
            if sel:
                for tt in range(SS // 128):
                    t.wait_rel(f"lqi{tt % 2}", 32 * (tt // 2 + 1))
                    t.wait_tok(f"qkIev{tt - 1}")
                    inst = te.transpose(ptq[:, 0:128], qIt[:, tt % 2, :], ident32[:])
                    inst.then_inc(p.sem("t"), 1); t.n += 1; p.counts["t"] = t.n
                    inst = te.transpose(ptq[:, 128:256], kIt[:, tt % 2, :], ident32[:])
                    inst.then_inc(p.sem("t"), 1); t.n += 1; p.counts["t"] = t.n
                    t.mark(f"qkI{tt}")
                t.wait_tok(f"qkIev{SS // 128 - 1}")
            p._pmlast = None
            for qt in range(NQT):
                nb = NKB(qt)
                for h in range(NH):
                    pc, ho = h // 2, (h % 2) * 64
                    for kb in range(nb):
                        bank = (ps1, ps2)[kb % 2]
                        t.wait_tok(f"sev{qt}_{h}_{kb - 2}")
                        if h == 0 and kb == 0:
                            t.wait_rel(f"lqt{qt % 2}", 16 * NPAIR * (qt // 2 + 1))
                        inst = te.matmul(bank[:], qf[ho:ho + 64, qt % 2, pc, :],
                                         kf[ho:ho + 64, pc, kb * 512:(kb + 1) * 512],
                                         start=True, stop=True)
                        inst.then_inc(p.sem("t"), 1); t.n += 1; p.counts["t"] = t.n
                        t.mark(f"s{qt}_{h}_{kb}")
                        if h == NH - 1 and kb == nb - 1:
                            t.mark(f"qdone{qt}")
                        if sel:
                            il = h // 4
                            if p._pmlast is not None:
                                t.wait_tok(p._pmlast)
                            inst = te.matmul(pm[:], qIf[il * 32:(il + 1) * 32, qt * 128:(qt + 1) * 128],
                                             kIf[il * 32:(il + 1) * 32, kb * 512:(kb + 1) * 512],
                                             start=True, stop=True)
                            inst.then_inc(p.sem("t"), 1); t.n += 1; p.counts["t"] = t.n
                            t.mark(f"mp{qt}_{h}_{kb}")
                            p._pmlast = f"mpev{qt}_{h}_{kb}"
                    nch = (qt + 1)
                    t.wait_tok(f"pnorm{qt}_{h}")
                    for grp in range((nch + 7) // 8):
                        c0, c1 = grp * 8, min(grp * 8 + 8, nch)
                        t.wait_tok(f"ptev{qt}_{h}_{grp - 1}")
                        for c in range(c0, c1):
                            inst = te.transpose(ptt[:, (c - c0) * 128:(c - c0 + 1) * 128],
                                                p16[:, c * 128:(c + 1) * 128], ident[:])
                        inst.then_inc(p.sem("t"), 1); t.n += 1; p.counts["t"] = t.n
                        t.mark(f"pt{qt}_{h}_{grp}")
                    t.wait_tok(f"ptev{qt}_{h}_{(nch + 7) // 8 - 1}")
                    t.wait_tok(f"poev{qt}_{h - 1}")
                    for c in range(nch):
                        inst = te.matmul(po[:], vt[:, c, h * 64:(h + 1) * 64], pT[:, c, :],
                                         start=(c == 0), stop=(c == nch - 1))
                    inst.then_inc(p.sem("t"), 1); t.n += 1; p.counts["t"] = t.n
                    t.mark(f"po{qt}_{h}")
                    t.wait_tok(f"poS{qt}_{h}")
                    if h == 0:
                        t.wait_tok(f"yev{qt - 1}")
                    for dco in range(DC):
                        inst = te.matmul(py[:, dco * 128:(dco + 1) * 128], wos[:, h, dco * 128:(dco + 1) * 128],
                                         poS[:], start=(h == 0 and dco % 4 == 0), stop=(h == NH - 1),
                                         skip_group_check=True)
                    inst.then_inc(p.sem("t"), 1); t.n += 1; p.counts["t"] = t.n
                    t.mark(f"y{qt}_{h}")

        @blk.vector
        def _(ve):
            v = Seq(p, ve, "v")
            if sel:
                for tt in range(SS // 128):
                    v.wait_tok(f"qkI{tt}")
                    v.op("tensor_copy", qIf[:, tt * 128:(tt + 1) * 128], ptq[0:64, 0:128])
                    v.op("tensor_copy", kIf[:, tt * 128:(tt + 1) * 128], ptq[0:64, 128:256])
                    v.mark(f"qkIev{tt}")
            for qt in range(NQT):
                nb = NKB(qt)
                dkb = qt // 4
                for h in range(NH):
                    prev = f"{qt}_{h - 1}" if h > 0 else (f"{qt - 1}_{NH - 1}" if qt > 0 else None)
                    if prev is not None:
                        v.wait_tok(f"expd{prev}")
                    for kb in range(nb):
                        bank = (ps1, ps2)[kb % 2]
                        v.wait_tok(f"s{qt}_{h}_{kb}")
                        if sel:
                            v.wait_tok(f"mp{qt}_{h}_{kb}")
                            v.op("tensor_scalar", madd[:], pm[:], thrs[:, qt, h:h + 1],
                                 None, ALU.is_gt)
                            v.mark(f"mpev{qt}_{h}_{kb}")
                            if kb == 0:
                                v.op("memset", madd[:, 0:4], 1.0)
                            v.op("tensor_scalar", madd[:], madd[:], 1.0, 1e30, ALU.subtract, ALU.mult)
                            if kb == dkb:
                                v.op("tensor_tensor", madd[:], madd[:], cmask[:, qt % 4], ALU.add)
                            v.op("tensor_tensor", ssb[:, kb * 512:(kb + 1) * 512], bank[:],
                                 madd[:], ALU.add)
                            v.mark(f"sev{qt}_{h}_{kb}")
                            v.op("reduce_max", racc[:, 0, kb:kb + 1], ssb[:, kb * 512:(kb + 1) * 512],
                                 axis=AX.X)
                        elif kb == dkb:
                            v.op("tensor_tensor", ssb[:, kb * 512:(kb + 1) * 512], bank[:],
                                 cmask[:, qt % 4], ALU.add)
                            v.mark(f"sev{qt}_{h}_{kb}")
                            v.op("reduce_max", racc[:, 0, kb:kb + 1], ssb[:, kb * 512:(kb + 1) * 512],
                                 axis=AX.X)
                        else:
                            v.op("tensor_scalar", ssb[:, kb * 512:(kb + 1) * 512], bank[:],
                                 0.0, None, ALU.bypass, ALU.max, accum_out=racc[:, 0, kb:kb + 1])
                            v.mark(f"sev{qt}_{h}_{kb}")
                    v.mark(f"sev{qt}_{h}_x")
                    v.op("reduce_max", rst[:, 0:1], racc[:, 0, 0:nb], axis=AX.X)
                    v.op("tensor_scalar", rst[:, 1:2], rst[:, 0:1], -scale, None, ALU.mult)
                    v.mark(f"bias{qt}_{h}")
                    v.wait_tok(f"expd{qt}_{h}")
                    v.op("reduce_sum", rst[:, 2:3], racc[:, 1, 0:nb], axis=AX.X)
                    v.op("reciprocal", rst[:, 3:4], rst[:, 2:3])
                    v.op("tensor_scalar", p16[:, 0:nb * 512], p16[:, 0:nb * 512], rst[:, 3:4],
                         None, ALU.mult)
                    v.mark(f"pnorm{qt}_{h}")
                    if prev is not None:
                        v.wait_tok(f"po{prev}")
                    nch = qt + 1
                    for grp in range((nch + 7) // 8):
                        c0, c1 = grp * 8, min(grp * 8 + 8, nch)
                        v.wait_tok(f"pt{qt}_{h}_{grp}")
                        v.op("tensor_copy", pT[:, c0:c1, :],
                             ptt[:, 0:(c1 - c0) * 128].rearrange("p (c t) -> p c t", t=128))
                        v.mark(f"ptev{qt}_{h}_{grp}")
                    v.wait_tok(f"po{qt}_{h}")
                    if prev is not None:
                        v.wait_tok(f"y{prev}")
                    pc, ho = h // 2, (h % 2) * 64
                    if gated:
                        v.op("tensor_tensor", poS[:], po[:], gof[ho:ho + 64, pc, qt * 128:(qt + 1) * 128],
                             ALU.mult)
                    else:
                        v.op("tensor_copy", poS[:], po[:])
                    v.mark(f"poS{qt}_{h}")
                    v.mark(f"poev{qt}_{h}")
                v.wait_tok(f"y{qt}_{NH - 1}")
                v.wait_tok(f"yst{qt - 2}", may_miss=True)
                v.op("tensor_copy", yev[:, qt % 2], py[:])
                v.mark(f"yev{qt}")

        @blk.scalar
        def _(sc):
            s = Seq(p, sc, "sc")
            s.wait_tok("consts")
            for qt in range(NQT):
                nb = NKB(qt)
                for h in range(NH):
                    prevs = f"{qt}_{h - 1}" if h > 0 else (f"{qt - 1}_{NH - 1}" if qt > 0 else None)
                    if prevs is not None:
                        pq = qt if h > 0 else qt - 1
                        s.wait_tok(f"pt{prevs}_{(pq + 1 + 7) // 8 - 1}")
                    s.wait_tok(f"bias{qt}_{h}")
                    for kb in range(nb):
                        s.op("activation", p16[:, kb * 512:(kb + 1) * 512],
                             ssb[:, kb * 512:(kb + 1) * 512], AF.Exp, bias=rst[:, 1:2],
                             scale=scale, accum_out=racc[:, 1, kb:kb + 1])
                    s.mark(f"expd{qt}_{h}")

        @blk.sync
        def _(sy):
            s = Seq(p, sy, "syf")
            for nm in ("sy0", "sy1"):
                if p.dmac.get(p.rsem(nm)):
                    s.wait(nm, p.dmac[p.rsem(nm)])


def build_attn0(SS, NH=8, scale=QKSCALE, gated=False):
    return build_twice(emit_attn0, SS, NH, scale, gated)


def build_attn1(SS, NH=8):
    return build_twice(emit_attn0, SS, NH, 0.125, True, True)



def emit_post0(p, TL, presummed=False):
    NT = TL // 128
    _lysem = "lya%d"
    nc = p.nc

    xn = p.din("xn", [TL, D])
    eq_i = p.din("eq", [D, TL], BF16)
    ctxa = p.din("ctxa", [D, D + 1])
    ctxb = None if presummed else p.din("ctxb", [D, D + 1])
    wo_i = p.din("wo", [D, D])
    bo_i = p.din("bo", [1, D])
    gam_i = p.din("gam", [1, D]); bet_i = p.din("bet", [1, D])
    ya_i = p.din("ya", [D, TL])
    yb_i = None if presummed else p.din("yb", [D, TL])
    lng_i = p.din("lng", [1, D]); lnb_i = p.din("lnb", [1, D])
    s0_o = p.dout("s0", [TL, D])
    h2_o = p.dout("h2", [TL, D], BF16)

    eqf = p.sb("eqf", [128, DC, TL], BF16)
    cnb = p.sb("cnb", [128, DC, D], BF16)
    cnt_ = p.sb("cnt_", [128, DC, D + 1], F32)
    araw = p.sb("araw", [128, DC, TL], BF16)
    woL = p.sb("woL", [128, DC, D], BF16)
    cons = p.sb("cons", [128, 5, D], F32)
    rseq = p.sb("rseq", [128, NT], F32)
    rr = p.sb("rr", [128, NT], F32)
    onesb = p.sb("onesb", [128, 1], BF16)
    xt = p.sb("xt", [128, 2, D], F32)
    o0 = p.sb("o0", [128, 2, D], F32)
    yab = p.sb("yab", [128, 2, 2, DC * 128], F32)
    yft16 = p.sb("yft16", [128, 2, D], F32)
    h2t = p.sb("h2t", [128, 2, D], F32)
    h16 = p.sb("h16", [128, 2, D], BF16)
    sqd = p.sb("sqd", [128, D], F32)
    stats = p.sb("stats", [128, 8], F32)
    ident32 = p.sb("ident32", [128, 128], F32)

    pa = p.ps("pa", [128, 512], F32)
    pb = p.ps("pb", [128, 512], F32)
    pr = p.ps("pr", [128, 16], F32)
    pt1 = p.ps("pt1", [128, 512], F32)
    pt2 = p.ps("pt2", [128, 512], F32)

    with nc.Block() as blk:
        @blk.sync
        def _(sy):
            p.dma(sy, "ldc", cons[:, 0], bo_i.to_broadcast([128, D]))
            p.dma(sy, "ldc", cons[:, 1], gam_i.to_broadcast([128, D]))
            p.dma(sy, "ldc", cons[:, 2], bet_i.to_broadcast([128, D]))
            p.dma(sy, "ldc", cons[:, 3], lng_i.to_broadcast([128, D]))
            p.dma(sy, "ldc", cons[:, 4], lnb_i.to_broadcast([128, D]))
            p.dma(sy, "ldq", eqf[:], eq_i.rearrange("(c p) t -> p c t", p=128))
            p.dma(sy, "ldq", cnt_[:], ctxa.rearrange("(c p) e -> p c e", p=128))
            p.dtok("ld0", "ldq")
            for i in range(NT):
                s = Seq(p, sy, "sy")
                s.wait_tok(f"xtfree{i}", may_miss=True)
                p.dma(sy, f"ldx{i % 2}", xt[:, i % 2], xn.rearrange("(n p) d -> n p d", p=128)[i])

        @blk.gpsimd
        def _(g):
            gg = Seq(p, g, "g")
            for c in range(DC):
                p.dma(g, "ldw", woL[:, c], wo_i[c * 128:(c + 1) * 128, :])
            p.dtok("wload", "ldw")
            gg.op("memset", ident32[:], 1.0)
            gg.op("affine_select", ident32[:], ident32[:], [[-1, 128]], ALU.is_equal, 0.0,
                  base=0, channel_multiplier=1)
            gg.op("memset", onesb[:], 1.0)
            gg.mark("identdone")
            p.emit_consts(gg)
            if not presummed:
                g.wait_ge(p.sem("ldq"), p.pbase.get(p.rsem("ldq"), 0) + 32)
                p.dma(g, "ldq2", cnt_[:], ctxb.rearrange("(c p) e -> p c e", p=128),
                      accum_op=ALU.add)
            for i in range(NT):
                gg.wait_tok(f"s0d{i}")
                p.dma(g, f"ss{i % 2}", s0_o.rearrange("(n p) d -> n p d", p=128)[i], o0[:, i % 2])
                p.dtok(f"s0st{i}", f"ss{i % 2}")
            for i in range(NT + 2):
                if i < NT:
                    gg.wait_tok(f"yabfree{i}", may_miss=True)
                    gg.wait_tok(f"ytpA{i - 1}")
                    p.dma(g, f"lya{i % 2}", yab[:, i % 2, 0].rearrange("p (c t) -> p c t", t=128),
                          ya_i[:, i * 128:(i + 1) * 128].rearrange("(c p) t -> p c t", p=128))
                    if not presummed:
                        p.dma(g, f"lya{i % 2}", yab[:, i % 2, 1].rearrange("p (c t) -> p c t", t=128),
                              yb_i[:, i * 128:(i + 1) * 128].rearrange("(c p) t -> p c t", p=128))
                if i >= 2:
                    gg.wait_tok(f"h16d{i - 2}")
                    p.dma(g, f"sh{i % 2}", h2_o.rearrange("(n p) d -> n p d", p=128)[i - 2], h16[:, (i - 2) % 2])
                    p.dtok(f"h2st{i - 2}", f"sh{i % 2}")

        @blk.vector
        def _(ve):
            v = Seq(p, ve, "v")
            v.wait("ldq", p.dmac.get(p.rsem("ldq"), 0))
            if not presummed:
                v.wait_rel("ldq2", 16)
            v.wait("ldc", p.dmac.get(p.rsem("ldc"), 0))
            for dc in range(DC):
                v.op("reciprocal", stats[:, 0:1], cnt_[:, dc, D:D + 1])
                v.op("tensor_scalar", cnb[:, dc], cnt_[:, dc, 0:D], stats[:, 0:1], None, ALU.mult)
            v.mark("cnready")
            gi = 0
            for ec in range(DC):
                for ts in range(TL // 512):
                    bank = (pa, pb)[gi % 2]
                    v.wait_tok(f"ar{gi}")
                    v.op("tensor_copy", araw[:, ec, ts * 512:(ts + 1) * 512], bank[:])
                    v.mark(f"arev{gi}")
                    gi += 1
            v.wait_tok("rseqd")
            v.op("tensor_copy", rseq[:], pr[:, 0:NT])
            v.op("reciprocal", rr[:], rseq[:])
            v.op("tensor_scalar", rr[:], rr[:], 1.0 / 32.0, None, ALU.mult)
            v.mark("rrready")
            for i in range(NT):
                v.wait_tok(f"s0st{i - 2}", may_miss=True)
                for nn in range(2):
                    bank = (pa, pb)[(i * 2 + nn) % 2]
                    v.wait_tok(f"o0_{i}_{nn}")
                    v.op("tensor_scalar", o0[:, i % 2, nn * 512:(nn + 1) * 512], bank[:],
                         rr[:, i:i + 1], None, ALU.mult)
                    v.mark(f"o0ev{i * 2 + nn}")
                v.wait_rel(f"ldx{i % 2}", 16 * (i // 2 + 1))
                v.op("tensor_add", o0[:, i % 2], o0[:, i % 2], cons[:, 0])
                v.op("tensor_add", o0[:, i % 2], o0[:, i % 2], xt[:, i % 2])
                v.mark(f"xtfree{i + 2}")
                v.mark(f"r0{i}")
                v.wait_tok(f"ssq{i}")
                v.op("reduce_sum", stats[:, 0:1], o0[:, i % 2], axis=AX.X)
                v.op("tensor_scalar", stats[:, 2:3], stats[:, 0:1], 1.0 / D, None, ALU.mult)
                v.op("tensor_scalar", stats[:, 5:6], stats[:, 2:3], stats[:, 2:3], None, ALU.mult)
                v.op("tensor_scalar", stats[:, 4:5], stats[:, 1:2], 1.0 / D, stats[:, 5:6],
                     ALU.mult, ALU.subtract)
                v.mark(f"var{i}")
                v.wait_tok(f"rstd{i}")
                v.op("tensor_scalar", o0[:, i % 2], o0[:, i % 2], stats[:, 2:3], stats[:, 3:4],
                     ALU.subtract, ALU.mult)
                v.op("tensor_tensor", o0[:, i % 2], o0[:, i % 2], cons[:, 1], ALU.mult)
                v.op("tensor_tensor", o0[:, i % 2], o0[:, i % 2], cons[:, 2], ALU.add)
                v.mark(f"s0d{i}")
            for i in range(NT):
                if presummed:
                    v.wait_rel(f"lya{i % 2}", 16 * (i // 2 + 1))
                else:
                    v.wait_rel(f"lya{i % 2}", 32 * (i // 2 + 1))
                    v.op("tensor_add", yab[:, i % 2, 0], yab[:, i % 2, 0], yab[:, i % 2, 1])
                v.mark(f"ysum{i}")
                v.wait_tok(f"ytpA{i}")
                v.op("tensor_copy", yft16[:, i % 2, 0:512], pt1[:])
                v.mark(f"ytpeA{i}")
                v.wait_tok(f"ytpB{i}")
                v.op("tensor_copy", yft16[:, i % 2, 512:1024], pt2[:])
                v.mark(f"yabfree{i + 2}")
                v.mark(f"ytpeB{i}")
                v.wait_tok(f"hssq{i}")
                v.op("reduce_sum", stats[:, 0:1], yft16[:, i % 2], axis=AX.X)
                v.op("tensor_scalar", stats[:, 2:3], stats[:, 0:1], 1.0 / D, None, ALU.mult)
                v.op("tensor_scalar", stats[:, 5:6], stats[:, 2:3], stats[:, 2:3], None, ALU.mult)
                v.op("tensor_scalar", stats[:, 4:5], stats[:, 1:2], 1.0 / D, stats[:, 5:6],
                     ALU.mult, ALU.subtract)
                v.mark(f"hvar{i}")
                v.wait_tok(f"hrstd{i}")
                v.op("tensor_scalar", h2t[:, i % 2], yft16[:, i % 2], stats[:, 2:3], stats[:, 3:4],
                     ALU.subtract, ALU.mult)
                v.op("tensor_tensor", h2t[:, i % 2], h2t[:, i % 2], cons[:, 3], ALU.mult)
                v.wait_tok(f"h2st{i - 2}", may_miss=True)
                v.op("tensor_tensor", h16[:, i % 2], h2t[:, i % 2], cons[:, 4], ALU.add)
                v.mark(f"h16d{i}")

        @blk.tensor
        def _(te):
            t = Seq(p, te, "t")
            t.wait_tok("identdone")
            t.wait_tok("cnready")
            gi = 0
            for ec in range(DC):
                for ts in range(TL // 512):
                    t.wait_tok(f"arev{gi - 2}")
                    bank = (pa, pb)[gi % 2]
                    for dc in range(DC):
                        inst = te.matmul(bank[:], cnb[:, dc, ec * 128:(ec + 1) * 128],
                                         eqf[:, dc, ts * 512:(ts + 1) * 512],
                                         start=(dc == 0), stop=(dc == DC - 1))
                    inst.then_inc(p.sem("t"), 1); t.n += 1; p.counts["t"] = t.n
                    t.mark(f"ar{gi}")
                    gi += 1
            for i in range(NT):
                for dc in range(DC):
                    inst = te.matmul(pr[:, i % 16:i % 16 + 1], eqf[:, dc, i * 128:(i + 1) * 128],
                                     onesb[:], start=(dc == 0), stop=(dc == DC - 1),
                                     skip_group_check=True)
                inst.then_inc(p.sem("t"), 1); t.n += 1; p.counts["t"] = t.n
            t.mark("rseqd")
            t.wait_tok("wload")
            t.wait_tok(f"arev{DC * (TL // 512) - 1}")
            for i in range(NT):
                for nn in range(2):
                    gi2 = i * 2 + nn
                    t.wait_tok(f"o0ev{gi2 - 2}")
                    bank = (pa, pb)[gi2 % 2]
                    for ec in range(DC):
                        inst = te.matmul(bank[:], araw[:, ec, i * 128:(i + 1) * 128],
                                         woL[:, ec, nn * 512:(nn + 1) * 512],
                                         start=(ec == 0), stop=(ec == DC - 1))
                    inst.then_inc(p.sem("t"), 1); t.n += 1; p.counts["t"] = t.n
                    t.mark(f"o0_{i}_{nn}")
            t.wait_tok("identdone")
            for i in range(NT):
                t.wait_tok(f"ysum{i}")
                t.wait_rel(_lysem % (i % 2), (16 if presummed else 32) * (i // 2 + 1))
                t.wait_tok(f"ytpeA{i - 1}")
                for c in range(4):
                    inst = te.transpose(pt1[:, c * 128:(c + 1) * 128],
                                        yab[:, i % 2, 0, c * 128:(c + 1) * 128], ident32[:])
                inst.then_inc(p.sem("t"), 1); t.n += 1; p.counts["t"] = t.n
                t.mark(f"ytpA{i}")
                t.wait_tok(f"ytpeB{i - 1}")
                for c in range(4):
                    inst = te.transpose(pt2[:, c * 128:(c + 1) * 128],
                                        yab[:, i % 2, 0, (4 + c) * 128:(5 + c) * 128], ident32[:])
                inst.then_inc(p.sem("t"), 1); t.n += 1; p.counts["t"] = t.n
                t.mark(f"ytpB{i}")

        @blk.scalar
        def _(sc):
            s = Seq(p, sc, "sc")
            s.wait_tok("consts")
            for i in range(NT):
                s.wait_tok(f"r0{i}")
                s.op("activation", sqd[:], o0[:, i % 2], AF.Square, accum_out=stats[:, 1:2])
                s.mark(f"ssq{i}")
                s.wait_tok(f"var{i}")
                s.op("activation", stats[:, 3:4], stats[:, 4:5], AF.Ln, bias=p.cc(LN_EPS), scale=1.0)
                s.op("activation", stats[:, 3:4], stats[:, 3:4], AF.Exp, bias=0.0, scale=-0.5)
                s.mark(f"rstd{i}")
            for i in range(NT):
                s.wait_tok(f"ytpeB{i}")
                s.op("activation", sqd[:], yft16[:, i % 2], AF.Square, accum_out=stats[:, 1:2])
                s.mark(f"hssq{i}")
                s.wait_tok(f"hvar{i}")
                s.op("activation", stats[:, 3:4], stats[:, 4:5], AF.Ln, bias=p.cc(LN_EPS), scale=1.0)
                s.op("activation", stats[:, 3:4], stats[:, 3:4], AF.Exp, bias=0.0, scale=-0.5)
                s.mark(f"hrstd{i}")

        @blk.sync
        def _(sy):
            s = Seq(p, sy, "syf")
            for nm in ("ss0", "ss1", "sh0", "sh1"):
                if p.dmac.get(p.rsem(nm)):
                    s.wait(nm, p.dmac[p.rsem(nm)])


def build_post0(TL):
    return build_twice(emit_post0, TL)



GC1 = 1.5957691216
GC2 = 0.07135480787
FF0 = 4096


def emit_moe0(p, TL):
    NT = TL // 128
    NTS = TL // 512
    nc = p.nc

    h2_i = p.din("h2", [TL, D], BF16)
    gw_i = p.din("gw", [D, 8])
    gb_i = p.din("gb", [1, 8])
    w1_i = p.din("w1", [8, D, FF0])
    b1_i = p.din("b1", [8, FF0])
    w2_i = p.din("w2", [8, FF0, D])
    b2_i = p.din("b2", [8, D])
    lng_i = p.din("lng", [1, D]); lnb_i = p.din("lnb", [1, D])
    x_i = p.din("x", [TL, D])
    s0_i = p.din("s0", [TL, D])
    x1_o = p.dout("x1", [TL, D])

    h2f = p.sb("h2f", [128, DC, TL], BF16)
    w1s = p.sb("w1s", [128, 2, DC, 512], BF16)
    w2s = p.sb("w2s", [128, 2, 4, D], BF16)
    b1q = p.sb("b1q", [1, 2, 512], BF16)
    b2s = p.sb("b2s", [8, D], BF16)
    gw16 = p.sb("gw16", [128, DC, 8], BF16)
    gbb = p.sb("gbb", [128, 8], F32)
    gate = p.sb("gate", [128, NT, 8], F32)
    gateT = p.sb("gateT", [8, NT, 128], BF16)
    ysb = p.sb("ysb", [128, NT, D], F32)
    hid16 = p.sb("hid16", [128, 2, 4, 512], BF16)
    sq1 = p.sb("sq1", [128, 512], F32)
    sq2 = p.sb("sq2", [128, 512], F32)
    onesr = p.sb("onesr", [1, 512], BF16)
    cons = p.sb("cons", [128, 2, D], F32)
    stats = p.sb("stats", [128, 8], F32)
    sqd = p.sb("sqd", [128, D], F32)
    xs0 = p.sb("xs0", [128, 2, 2, D], F32)
    g0t = p.sb("g0t", [128, 2, D], F32)
    ident32 = p.sb("ident32", [128, 128], F32)

    ph1 = p.ps("ph1", [128, 512], F32)
    ph2 = p.ps("ph2", [128, 512], F32)
    py1 = p.ps("py1", [128, 512], F32)
    py2 = p.ps("py2", [128, 512], F32)
    pg = p.ps("pg", [128, 128], F32)

    with nc.Block() as blk:
        @blk.sync
        def _(sy):
            for c in range(DC):
                p.dma(sy, "ldh", h2f[:, c], h2_i[:, c * 128:(c + 1) * 128], transpose=True)
            p.dma(sy, "ldc", gbb[:], gb_i.to_broadcast([128, 8]))
            p.dma(sy, "ldc", cons[:, 0], lng_i.to_broadcast([128, D]))
            p.dma(sy, "ldc", cons[:, 1], lnb_i.to_broadcast([128, D]))
            p.dtok("hload", "ldh")

        @blk.gpsimd
        def _(g):
            gg = Seq(p, g, "g")
            for c in range(DC):
                p.dma(g, "ldg", gw16[:, c], gw_i[c * 128:(c + 1) * 128, :])
            p.dma(g, "ldg", b2s[:], b2_i)
            p.dtok("gload", "ldg")
            gg.op("memset", onesr[:], 1.0)
            gg.op("memset", ident32[:], 1.0)
            gg.op("affine_select", ident32[:], ident32[:], [[-1, 128]], ALU.is_equal, 0.0,
                  base=0, channel_multiplier=1)
            gg.op("memset", ysb[:], 0.0)
            gg.mark("init")
            p.emit_consts(gg)
            for e in range(8):
                for q in range(8):
                    j = (e * 8 + q)
                    gg.wait_tok(f"wfree{j - 2}")
                    p.dma(g, f"lw{j % 2}", w1s[:, j % 2],
                          w1_i[e].rearrange("(c k) f -> k c f", k=128)[:, :, q * 512:(q + 1) * 512])
                    p.dma(g, f"lw{j % 2}", w2s[:, j % 2],
                          w2_i[e, q * 512:(q + 1) * 512].rearrange("(c k) f -> k c f", k=128))
                    p.dma(g, f"lw{j % 2}", b1q[:, j % 2], b1_i[e:e + 1, q * 512:(q + 1) * 512])
                    p.dtok(f"wld{j}", f"lw{j % 2}")
            gg.wait_tok("yready")
            for i in range(NT + 2):
                if i < NT:
                    gg.wait_tok(f"x1d{i - 2}")
                    p.dma(g, f"lx{i % 2}", xs0[:, i % 2, 0], x_i.rearrange("(n p) d -> n p d", p=128)[i])
                    p.dma(g, f"lx{i % 2}", xs0[:, i % 2, 1], s0_i.rearrange("(n p) d -> n p d", p=128)[i])
                if i >= 2:
                    gg.wait_tok(f"x1d{i - 2}")
                    p.dma(g, f"sx{i % 2}", x1_o.rearrange("(n p) d -> n p d", p=128)[i - 2], g0t[:, (i - 2) % 2])
                    p.dtok(f"x1st{i - 2}", f"sx{i % 2}")

        @blk.tensor
        def _(te):
            t = Seq(p, te, "t")
            t.wait_tok("init")
            t.wait_tok("hload")
            t.wait_tok("gload")
            for i in range(NT):
                t.wait_tok(f"gTev{i - 1}")
                for dc in range(DC):
                    inst = te.matmul(pg[:, 0:8], h2f[:, dc, i * 128:(i + 1) * 128],
                                     gw16[:, dc], start=(dc == 0), stop=(dc == DC - 1),
                                     skip_group_check=True)
                inst.then_inc(p.sem("t"), 1); t.n += 1; p.counts["t"] = t.n
                t.mark(f"glog{i}")
                t.wait_tok(f"gsm{i}")
                inst = te.transpose(pg[0:8, 0:128], gate[:, i], ident32[:])
                inst.then_inc(p.sem("t"), 1); t.n += 1; p.counts["t"] = t.n
                t.mark(f"gT{i}")
            for i in range(NT):
                t.wait_tok(f"gTev{i}")
                for nn in range(2):
                    jj = i * 2 + nn
                    t.wait_tok(f"b2ev{jj - 2}")
                    bank = (py1, py2)[jj % 2]
                    inst = te.matmul(bank[:], gateT[:, i, :], b2s[:, nn * 512:(nn + 1) * 512],
                                     start=True, stop=True)
                    inst.then_inc(p.sem("t"), 1); t.n += 1; p.counts["t"] = t.n
                    t.mark(f"b2_{i}_{nn}")
            t.wait_tok(f"b2ev{2 * NT - 1}")
            for e in range(8):
                for q in range(8):
                    j = e * 8 + q
                    t.wait_rel("lw%d" % (j % 2), 48 * (j // 2 + 1))
                    for ts in range(NTS):
                        for fc in range(4):
                            gidx = (ts * 4 + fc)
                            bank = (ph1, ph2)[gidx % 2]
                            if gidx >= 2:
                                t.wait_tok(f"hg{j}_{gidx - 2}")
                            elif j > 0:
                                t.wait_tok(f"hg{j - 1}_{NTS * 4 - 2 + gidx}")
                            for dc in range(DC):
                                te.matmul(bank[:], w1s[:, j % 2, dc, fc * 128:(fc + 1) * 128],
                                          h2f[:, dc, ts * 512:(ts + 1) * 512],
                                          start=(dc == 0), stop=False)
                            inst = te.matmul(bank[:], b1q[0:1, j % 2, fc * 128:(fc + 1) * 128],
                                             onesr[:], start=False, stop=True)
                            inst.then_inc(p.sem("t"), 1); t.n += 1; p.counts["t"] = t.n
                            t.mark(f"h{j}_{gidx}")
                        for sub in range(4):
                            i = ts * 4 + sub
                            for nn in range(2):
                                jj = (i * 2 + nn)
                                bank = (py1, py2)[jj % 2]
                                t.wait_tok(f"yev{j}_{jj - 2}")
                                t.wait_tok(f"hg{j}_{ts * 4 + 3}")
                                for fc in range(4):
                                    inst = te.matmul(bank[:], hid16[:, j % 2, fc, sub * 128:(sub + 1) * 128],
                                                     w2s[:, j % 2, fc, nn * 512:(nn + 1) * 512],
                                                     start=(fc == 0), stop=(fc == 3))
                                inst.then_inc(p.sem("t"), 1); t.n += 1; p.counts["t"] = t.n
                                t.mark(f"y{j}_{jj}")
                    t.mark(f"wfree{j}")

        @blk.vector
        def _(ve):
            v = Seq(p, ve, "v")
            v.wait_tok("init")
            v.wait_rel("ldc", 48)
            for i in range(NT):
                v.wait_tok(f"glog{i}")
                v.op("tensor_tensor", gate[:, i], pg[:, 0:8], gbb[:], ALU.add)
                v.mark(f"gev{i}")
                v.op("reduce_max", stats[:, 0:1], gate[:, i], axis=AX.X)
                v.op("tensor_scalar", stats[:, 1:2], stats[:, 0:1], -1.0, None, ALU.mult)
                v.mark(f"gmax{i}")
                v.wait_tok(f"gexp{i}")
                v.op("reciprocal", stats[:, 2:3], stats[:, 3:4])
                v.op("tensor_scalar", gate[:, i], gate[:, i], stats[:, 2:3], None, ALU.mult)
                v.mark(f"gsm{i}")
                v.wait_tok(f"gT{i}")
                v.op("tensor_copy", gateT[:, i, :], pg[0:8, 0:128])
                v.mark(f"gTev{i}")
            for i in range(NT):
                for nn in range(2):
                    jj = i * 2 + nn
                    bank = (py1, py2)[jj % 2]
                    v.wait_tok(f"b2_{i}_{nn}")
                    v.op("tensor_add", ysb[:, i, nn * 512:(nn + 1) * 512], bank[:],
                         ysb[:, i, nn * 512:(nn + 1) * 512])
                    v.mark(f"b2ev{jj}")
            for e in range(8):
                for q in range(8):
                    j = e * 8 + q
                    for ts in range(NTS):
                        for fc in range(4):
                            gidx = ts * 4 + fc
                            bank = (ph1, ph2)[gidx % 2]
                            v.wait_tok(f"h{j}_{gidx}")
                            v.wait_tok(f"sqf{j}_{gidx}")
                            v.op("tensor_scalar", sq1[:], sq1[:], GC2, GC1, ALU.mult, ALU.add)
                            v.op("tensor_tensor", sq1[:], sq1[:], bank[:], ALU.mult)
                            v.mark(f"t3{j}_{gidx}")
                            v.wait_tok(f"sig{j}_{gidx}")
                            if ts > 0:
                                v.wait_tok(f"y{j}_{ts * 8 - 1}")
                            elif j >= 2:
                                v.wait_tok(f"y{j - 2}_{NTS * 8 - 1}")
                            v.op("tensor_tensor", hid16[:, j % 2, fc, :], sq2[:], bank[:], ALU.mult)
                            v.mark(f"hg{j}_{gidx}")
                        for sub in range(4):
                            i = ts * 4 + sub
                            for nn in range(2):
                                jj = i * 2 + nn
                                bank = (py1, py2)[jj % 2]
                                v.wait_tok(f"y{j}_{jj}")
                                v.op("scalar_tensor_tensor", ysb[:, i, nn * 512:(nn + 1) * 512],
                                     bank[:], gate[:, i, e:e + 1], ysb[:, i, nn * 512:(nn + 1) * 512],
                                     ALU.mult, ALU.add)
                                v.mark(f"yev{j}_{jj}")
            v.mark("yready")
            for i in range(NT):
                v.wait_tok(f"essq{i}")
                v.op("reduce_sum", stats[:, 0:1], ysb[:, i], axis=AX.X)
                v.op("tensor_scalar", stats[:, 2:3], stats[:, 0:1], 1.0 / D, None, ALU.mult)
                v.op("tensor_scalar", stats[:, 5:6], stats[:, 2:3], stats[:, 2:3], None, ALU.mult)
                v.op("tensor_scalar", stats[:, 4:5], stats[:, 1:2], 1.0 / D, stats[:, 5:6],
                     ALU.mult, ALU.subtract)
                v.mark(f"evar{i}")
                v.wait_tok(f"erstd{i}")
                v.wait_tok(f"x1st{i - 2}", may_miss=True)
                v.op("tensor_scalar", g0t[:, i % 2], ysb[:, i], stats[:, 2:3], stats[:, 3:4],
                     ALU.subtract, ALU.mult)
                v.op("tensor_tensor", g0t[:, i % 2], g0t[:, i % 2], cons[:, 0], ALU.mult)
                v.op("tensor_tensor", g0t[:, i % 2], g0t[:, i % 2], cons[:, 1], ALU.add)
                v.mark(f"g0pre{i}")
                v.wait_tok(f"silu{i}")
                v.wait_rel(f"lx{i % 2}", 32 * (i // 2 + 1))
                v.op("tensor_tensor", g0t[:, i % 2], g0t[:, i % 2], xs0[:, i % 2, 1], ALU.mult)
                v.op("tensor_tensor", g0t[:, i % 2], g0t[:, i % 2], xs0[:, i % 2, 0], ALU.add)
                v.mark(f"x1d{i}")

        @blk.scalar
        def _(sc):
            s = Seq(p, sc, "sc")
            s.wait_tok("consts")
            for i in range(NT):
                s.wait_tok(f"gmax{i}")
                s.op("activation", gate[:, i], gate[:, i], AF.Exp, bias=stats[:, 1:2],
                     scale=1.0, accum_out=stats[:, 3:4])
                s.mark(f"gexp{i}")
            for e in range(8):
                for q in range(8):
                    j = e * 8 + q
                    for ts in range(NTS):
                        for fc in range(4):
                            gidx = ts * 4 + fc
                            bank = (ph1, ph2)[gidx % 2]
                            s.wait_tok(f"h{j}_{gidx}")
                            s.op("activation", sq1[:], bank[:], AF.Square)
                            s.mark(f"sqf{j}_{gidx}")
                            s.wait_tok(f"t3{j}_{gidx}")
                            s.op("activation", sq2[:], sq1[:], AF.Sigmoid)
                            s.mark(f"sig{j}_{gidx}")
                            s.mark(f"hev{j}_{gidx}")
            s.wait_tok("yready")
            for i in range(NT):
                s.wait_tok(f"evar{i - 1}")
                s.op("activation", sqd[:], ysb[:, i], AF.Square, accum_out=stats[:, 1:2])
                s.mark(f"essq{i}")
                s.wait_tok(f"evar{i}")
                s.op("activation", stats[:, 3:4], stats[:, 4:5], AF.Ln, bias=p.cc(LN_EPS), scale=1.0)
                s.op("activation", stats[:, 3:4], stats[:, 3:4], AF.Exp, bias=0.0, scale=-0.5)
                s.mark(f"erstd{i}")
                s.wait_tok(f"g0pre{i}")
                s.op("activation", g0t[:, i % 2], g0t[:, i % 2], AF.Silu)
                s.mark(f"silu{i}")

        @blk.sync
        def _(sy):
            s = Seq(p, sy, "syf")
            for nm in ("sx0", "sx1"):
                if p.dmac.get(p.rsem(nm)):
                    s.wait(nm, p.dmac[p.rsem(nm)])


def build_moe0(TL):
    return build_twice(emit_moe0, TL)



def emit_pre1(p, TL):
    NT = TL // 128
    nc = p.nc

    x_i = p.din("x1", [TL, D])
    wI_i = p.din("wI", [D, 268])
    rb_i = p.din("rb", [1, 8])
    w5_i = p.din("w5", [5, D, D])
    ch_i = p.din("ch", [TL, 32])
    sh_i = p.din("sh", [TL, 32])
    xn1b_o = p.dout("xn1b", [TL, D], BF16)
    qI_o = p.dout("qI", [TL, 128])
    kI_o = p.dout("kI", [TL, 128])
    wsig_o = p.dout("wsig", [TL, 4])
    eps_o = p.dout("eps", [128, TL // 128, 2])
    rq_o = p.dout("rq", [TL, D], BF16)
    rk_o = p.dout("rk", [TL, D], BF16)
    vg_o = p.dout("vg", [TL, D], BF16)
    gos_o = p.dout("gos", [TL, D], BF16)

    xt = p.sb("xt", [128, 2, D], F32)
    xnt = p.sb("xnt", [128, 2, D], F32)
    xn16 = p.sb("xn16", [128, 2, D], BF16)
    xf32 = p.sb("xf32", [128, 2, DC, 128], F32)
    xf16 = p.sb("xf16", [128, 2, DC, 128], BF16)
    wI = p.sb("wIs", [128, DC, 268], F32)
    w5 = p.sb("w5s", [128, 5, DC, D], BF16)
    rbb = p.sb("rbb", [128, 8], F32)
    chs = p.sb("chs", [128, NT, 64], F32)
    sel = p.sb("sel", [128, 2, 268], F32)
    selo = p.sb("selo", [128, 2, 144], F32)
    eidx = p.sb("eidx", [128, 8], mybir.dt.uint32)
    emax = p.sb("emax", [128, 8], F32)
    small = p.sb("small", [128, 2, 8], F32)
    qkf = p.sb("qkf", [128, 2, 2048], F32)
    rop = p.sb("rop", [128, 2048], F32)
    rqk16 = p.sb("rqk16", [128, 2, 2048], BF16)
    vv = p.sb("vv", [128, D], F32)
    gg_ = p.sb("gg_", [128, D], F32)
    vg16 = p.sb("vg16", [128, 2, D], BF16)
    go16 = p.sb("go16", [128, 2, D], BF16)
    sqd = p.sb("sqd", [128, D], F32)
    stats = p.sb("stats", [128, 8], F32)
    ident32 = p.sb("ident32", [128, 128], F32)

    pa = p.ps("pa", [128, 512], F32)
    pb = p.ps("pb", [128, 512], F32)
    pt1 = p.ps("pt1", [128, 512], F32)
    pt2 = p.ps("pt2", [128, 512], F32)
    psl = p.ps("psl", [128, 268], F32)

    with nc.Block() as blk:
        @blk.sync
        def _(sy):
            p.dma(sy, "ldc", rbb[:], rb_i.to_broadcast([128, 8]))
            p.dma(sy, "ldc", chs[:].rearrange("p n c -> p n c")[:, :, 0:32],
                  ch_i.rearrange("(n p) c -> p n c", p=128))
            p.dma(sy, "ldc", chs[:, :, 32:64], sh_i.rearrange("(n p) c -> p n c", p=128))
            for c in range(DC):
                p.dma(sy, "ldw32", wI[:, c], wI_i[c * 128:(c + 1) * 128, :])
            p.dtok("wIload", "ldw32")
            for i in range(NT):
                s = Seq(p, sy, "sy")
                s.wait_tok(f"xtfree{i}", may_miss=True)
                p.dma(sy, f"ld{i % 2}", xt[:, i % 2], x_i.rearrange("(n p) d -> n p d", p=128)[i])

        @blk.gpsimd
        def _(g):
            gg = Seq(p, g, "g")
            for w in range(5):
                for c in range(DC):
                    p.dma(g, "ldw", w5[:, w, c], w5_i[w, c * 128:(c + 1) * 128, :])
            p.dtok("wload", "ldw")
            gg.op("memset", ident32[:], 1.0)
            gg.op("affine_select", ident32[:], ident32[:], [[-1, 128]], ALU.is_equal, 0.0,
                  base=0, channel_multiplier=1)
            gg.mark("identdone")
            p.emit_consts(gg)
            for i in range(NT):
                gg.wait_tok(f"xn16d{i}")
                p.dma(g, f"s0{i % 2}", xn1b_o.rearrange("(n p) d -> n p d", p=128)[i], xn16[:, i % 2])
                p.dtok(f"xnst{i}", f"s0{i % 2}")
                gg.wait_tok(f"seld{i}")
                p.dma(g, f"s1{i % 2}", qI_o.rearrange("(n p) d -> n p d", p=128)[i], sel[:, i % 2, 0:128])
                p.dma(g, f"s1{i % 2}", kI_o.rearrange("(n p) d -> n p d", p=128)[i], sel[:, i % 2, 128:256])
                p.dma(g, f"s1{i % 2}", wsig_o.rearrange("(n p) d -> n p d", p=128)[i], sel[:, i % 2, 256:260])
                p.dtok(f"selst{i}", f"s1{i % 2}")
                gg.wait_tok(f"routd{i}")
                p.dma(g, f"s2{i % 2}", eps_o[:, i], small[:, i % 2, 0:2])
                p.dtok(f"routst{i}", f"s2{i % 2}")
                gg.wait_tok(f"roped{i}")
                p.dma(g, f"s3{i % 2}", rq_o.rearrange("(n p) d -> n p d", p=128)[i], rqk16[:, i % 2, 0:D])
                p.dma(g, f"s3{i % 2}", rk_o.rearrange("(n p) d -> n p d", p=128)[i], rqk16[:, i % 2, D:2 * D])
                p.dtok(f"ropest{i}", f"s3{i % 2}")
                gg.wait_tok(f"vgd{i}")
                p.dma(g, f"s4{i % 2}", vg_o.rearrange("(n p) d -> n p d", p=128)[i], vg16[:, i % 2])
                gg.wait_tok(f"god{i}")
                p.dma(g, f"s4{i % 2}", gos_o.rearrange("(n p) d -> n p d", p=128)[i], go16[:, i % 2])
                p.dtok(f"vgst{i}", f"s4{i % 2}")

        @blk.scalar
        def _(sc):
            s = Seq(p, sc, "sc")
            s.wait_tok("consts")
            for i in range(NT):
                s.wait_rel(f"ld{i % 2}", 16 * (i // 2 + 1))
                s.wait_tok(f"statfreev{i}", may_miss=True)
                s.op("activation", sqd[:], xt[:, i % 2], AF.Square, accum_out=stats[:, 1:2])
                s.mark(f"ssq{i}")
                s.wait_tok(f"var{i}")
                s.op("activation", stats[:, 3:4], stats[:, 4:5], AF.Ln, bias=p.cc(LN_EPS), scale=1.0)
                s.op("activation", stats[:, 3:4], stats[:, 3:4], AF.Exp, bias=0.0, scale=-0.5)
                s.mark(f"rstd{i}")
                s.wait_tok(f"selmm{i}")
                s.wait_tok(f"selst{i - 2}", may_miss=True)
                s.op("activation", sel[:, i % 2, 256:260], psl[:, 256:260], AF.Sigmoid)
                s.mark(f"wsigd{i}")
                s.wait_tok(f"rmax{i}")
                s.op("activation", emax[:], psl[:, 260:268], AF.Exp, bias=stats[:, 6:7],
                     scale=1.0, accum_out=stats[:, 7:8])
                s.mark(f"rexp{i}")
                s.wait_tok(f"gvev{i}")
                s.op("activation", gg_[:], gg_[:], AF.Sigmoid)
                s.mark(f"gvsig{i}")
                s.wait_tok(f"goev{i}")
                s.wait_tok(f"vgst{i - 2}", may_miss=True)
                s.op("activation", go16[:, i % 2], sqd[:], AF.Sigmoid)
                s.mark(f"god{i}")

        @blk.vector
        def _(ve):
            v = Seq(p, ve, "v")
            v.wait_rel("ldc", 48)
            for i in range(NT):
                v.wait_rel(f"ld{i % 2}", 16 * (i // 2 + 1))
                v.wait_tok(f"ssq{i}")
                v.op("reduce_sum", stats[:, 0:1], xt[:, i % 2], axis=AX.X)
                v.op("tensor_scalar", stats[:, 2:3], stats[:, 0:1], 1.0 / D, None, ALU.mult)
                v.op("tensor_scalar", stats[:, 5:6], stats[:, 2:3], stats[:, 2:3], None, ALU.mult)
                v.op("tensor_scalar", stats[:, 4:5], stats[:, 1:2], 1.0 / D, stats[:, 5:6],
                     ALU.mult, ALU.subtract)
                v.mark(f"var{i}")
                v.mark(f"statfreev{i + 1}")
                v.wait_tok(f"rstd{i}")
                v.wait_tok(f"xnuse{i - 2}", may_miss=True)
                v.op("tensor_scalar", xnt[:, i % 2], xt[:, i % 2], stats[:, 2:3], stats[:, 3:4],
                     ALU.subtract, ALU.mult)
                v.mark(f"xtfree{i + 2}")
                v.mark(f"xn1d{i}")
                v.wait_tok(f"xnst{i - 2}", may_miss=True)
                v.op("tensor_copy", xn16[:, i % 2], xnt[:, i % 2])
                v.mark(f"xn16d{i}")
                for w2_ in range(2):
                    v.wait_tok(f"tp{i}_{w2_}")
                    v.op("tensor_copy", xf32[:, i % 2, 4 * w2_:4 * w2_ + 4, :],
                         (pt1, pt2)[w2_][:].rearrange("p (c t) -> p c t", t=128))
                v.mark(f"tpev{i}")
                v.op("tensor_copy", xf16[:, i % 2], xf32[:, i % 2])
                v.mark(f"xf16d{i}")
                v.wait_tok(f"selmm{i}")
                v.wait_tok(f"selst{i - 2}", may_miss=True)
                v.op("tensor_copy", sel[:, i % 2, 0:256], psl[:, 0:256])
                v.mark(f"qkId{i}")
                v.op("tensor_tensor", psl[:, 260:268], psl[:, 260:268], rbb[:], ALU.add)
                v.op("reduce_max", stats[:, 6:7], psl[:, 260:268], axis=AX.X)
                v.op("tensor_scalar", stats[:, 6:7], stats[:, 6:7], -1.0, None, ALU.mult)
                v.mark(f"rmax{i}")
                v.wait_tok(f"rexp{i}")
                v.wait_tok(f"routst{i - 2}", may_miss=True)
                v.op("reciprocal", small[:, i % 2, 1:2], stats[:, 7:8])
                v.op("max", emax[:], psl[:, 260:268])
                v.op("max_index", eidx[:], emax[:], psl[:, 260:268])
                v.op("tensor_copy", small[:, i % 2, 0:1], eidx[:, 0:1])
                v.mark(f"routd{i}")
                v.mark(f"seld{i}")
                for pj in range(5):
                    for nn in range(2):
                        gi = (i * 10) + pj * 2 + nn
                        bank = (pa, pb)[gi % 2]
                        v.wait_tok(f"pj{gi}")
                        if pj < 2:
                            v.wait_tok(f"ropest{i - 2}" if (pj == 0 and nn == 0) else "zz-1", may_miss=True)
                            v.op("tensor_copy", qkf[:, i % 2, pj * 1024 + nn * 512:pj * 1024 + (nn + 1) * 512],
                                 bank[:])
                        elif pj == 2:
                            v.op("tensor_copy", vv[:, nn * 512:(nn + 1) * 512], bank[:])
                        elif pj == 3:
                            v.op("tensor_copy", gg_[:, nn * 512:(nn + 1) * 512], bank[:])
                            if nn == 1:
                                v.mark(f"gvev{i}")
                        else:
                            v.op("tensor_copy", sqd[:, nn * 512:(nn + 1) * 512], bank[:])
                            if nn == 1:
                                v.mark(f"goev{i}")
                        v.mark(f"pjev{gi}")
                v.wait_tok(f"gvsig{i}")
                v.wait_tok(f"vgst{i - 2}", may_miss=True)
                v.op("tensor_tensor", vg16[:, i % 2], vv[:], gg_[:], ALU.mult)
                v.mark(f"vgd{i}")
                ch3 = chs[:, i, 0:32].to_broadcast([128, 32, 16]).rearrange("p w h -> p h w")
                sh3 = chs[:, i, 32:64].to_broadcast([128, 32, 16]).rearrange("p w h -> p h w")
                for pj in range(2):
                    qk = qkf[:, i % 2, pj * 1024:(pj + 1) * 1024]
                    qe = qk.rearrange("p (h w) -> p h w", w=64)[:, :, 0:64:2]
                    qo = qk.rearrange("p (h w) -> p h w", w=64)[:, :, 1:64:2]
                    r3 = rop[:, 0:1024].rearrange("p (h w) -> p h w", w=64)
                    o16 = rqk16[:, i % 2, pj * 1024:(pj + 1) * 1024].rearrange("p (h w) -> p h w", w=64)
                    v.op("tensor_tensor", r3[:, :, 0:32], qe, ch3, ALU.mult)
                    v.op("tensor_tensor", r3[:, :, 32:64], qo, sh3, ALU.mult)
                    v.op("tensor_tensor", o16[:, :, 0:32], r3[:, :, 0:32], r3[:, :, 32:64], ALU.subtract)
                    v.op("tensor_tensor", r3[:, :, 0:32], qe, sh3, ALU.mult)
                    v.op("tensor_tensor", r3[:, :, 32:64], qo, ch3, ALU.mult)
                    v.op("tensor_tensor", o16[:, :, 32:64], r3[:, :, 0:32], r3[:, :, 32:64], ALU.add)
                v.mark(f"roped{i}")
                v.mark(f"xnuse{i}")

        @blk.tensor
        def _(te):
            t = Seq(p, te, "t")
            t.wait_tok("identdone")
            t.wait_tok("wIload")
            t.wait_tok("wload")
            for i in range(NT):
                t.wait_tok(f"xn1d{i}")
                t.wait_tok(f"tpev{i - 1}")
                for w2_ in range(2):
                    for c in range(4):
                        inst = te.transpose((pt1, pt2)[w2_][:, c * 128:(c + 1) * 128],
                                            xnt[:, i % 2, (4 * w2_ + c) * 128:(4 * w2_ + c + 1) * 128],
                                            ident32[:])
                    inst.then_inc(p.sem("t"), 1); t.n += 1; p.counts["t"] = t.n
                    t.mark(f"tp{i}_{w2_}")
                t.wait_tok(f"tpev{i}")
                t.wait_tok(f"routd{i - 1}")
                for dc in range(DC):
                    inst = te.matmul(psl[:], xf32[:, i % 2, dc, :], wI[:, dc],
                                     start=(dc == 0), stop=(dc == DC - 1))
                inst.then_inc(p.sem("t"), 1); t.n += 1; p.counts["t"] = t.n
                t.mark(f"selmm{i}")
                t.wait_tok(f"xf16d{i}")
                for pj in range(5):
                    for nn in range(2):
                        gi = (i * 10) + pj * 2 + nn
                        bank = (pa, pb)[gi % 2]
                        t.wait_tok(f"pjev{gi - 2}")
                        for dc in range(DC):
                            inst = te.matmul(bank[:], xf16[:, i % 2, dc, :],
                                             w5[:, pj, dc, nn * 512:(nn + 1) * 512],
                                             start=(dc == 0), stop=(dc == DC - 1))
                        inst.then_inc(p.sem("t"), 1); t.n += 1; p.counts["t"] = t.n
                        t.mark(f"pj{gi}")

        @blk.sync
        def _(sy):
            s = Seq(p, sy, "syf")
            for nm in ("s00", "s01", "s10", "s11", "s20", "s21", "s30", "s31", "s40", "s41"):
                if p.dmac.get(p.rsem(nm)):
                    s.wait(nm, p.dmac[p.rsem(nm)])


def build_pre1(TL):
    return build_twice(emit_pre1, TL)



NBIS = 24
K_BASE1, K_MIN1, K_MAX1 = 512.0, 32.0, 1024.0


def emit_sel1(p, SS):
    NTT = SS // 128
    nc = p.nc

    qI_i = p.din("qI", [2, SS, 32])
    kI_i = p.din("kI", [2, SS, 32])
    ws_i = p.din("ws", [SS, 2])
    hb_i = p.din("hb", [2, 1, 4])
    gbx_i = p.din("gbx", [2, 1, 1])
    vema_i = p.din("vema", [1, 1])
    thr_o = p.dout("thr", [SS, 2, 4])
    facd = p.nc.dram_tensor("facd", [2, 1, 4], F32, kind="Internal").ap()

    qIs = p.sb("qIs", [128, NTT, 32], F32)
    kIs = p.sb("kIs", [128, NTT, 32], F32)
    wss = p.sb("wss", [128, NTT, 2], F32)
    qIf = p.sb("qIf", [32, SS], F32)
    kIf = p.sb("kIf", [32, SS], F32)
    hbs = p.sb("hbs", [1, 4], F32)
    gbs = p.sb("gbs", [128, 2], F32)
    vem = p.sb("vem", [1, 1], F32)
    facb = p.sb("facb", [128, 4], F32)
    Ms = p.sb("Ms", [128, SS], F32)
    mgs = p.sb("mgs", [128, SS], F32)
    st_ = p.sb("st_", [128, 16], F32)
    bis = p.sb("bis", [128, 4, 4], F32)
    cnt = p.sb("cnt", [128, 4], F32)
    kti = p.sb("kti", [128, 4], mybir.dt.int32)
    gem = p.sb("gem", [128, 4], mybir.dt.int32)
    ktf = p.sb("ktf", [128, 4], F32)
    thrt = p.sb("thrt", [128, 2, 4], F32)
    ident32 = p.sb("ident32", [128, 128], F32)
    cmask4 = p.sb("cmask4", [128, 4, 512], F32)
    sdump = p.sb("sdump", [128, SS], F32)
    nmid = p.sb("nmid", [128, 2], F32)
    scnt = p.sb("scnt", [128, 2], F32)

    pms = p.ps("pms", [128, 512], F32)
    pms2 = p.ps("pms2", [128, 512], F32)
    ptq = p.ps("ptq", [32, 512], F32)

    NBM = lambda tt: (tt * 128) // 512 + 1

    with nc.Block() as blk:
        @blk.sync
        def _(sy):
            pass

        @blk.gpsimd
        def _(g):
            gg = Seq(p, g, "g")
            gg.op("memset", ident32[:], 1.0)
            gg.op("affine_select", ident32[:], ident32[:], [[-1, 128]], ALU.is_equal, 0.0,
                  base=0, channel_multiplier=1)
            gg.op("memset", cmask4[:], 0.0)
            for m_ in range(4):
                gg.op("affine_select", cmask4[:, m_], cmask4[:, m_], [[-1, 512]], ALU.is_ge, -1e30,
                      base=128 * m_, channel_multiplier=1)
            gg.mark("identdone")
            p.emit_consts(gg)
            for u in range(2):
                p.dma(g, f"ldu{u % 2}", qIs[:], qI_i[u].rearrange("(n p) c -> p n c", p=128))
                p.dma(g, f"ldu{u % 2}", kIs[:], kI_i[u].rearrange("(n p) c -> p n c", p=128))
                p.dma(g, f"ldu{u % 2}", wss[:], ws_i.rearrange("(n p) c -> p n c", p=128))
                p.dma(g, f"ldu{u % 2}", hbs[:], hb_i[u])
                p.dma(g, f"ldu{u % 2}", gbs[:, u:u + 1], gbx_i[u].to_broadcast([128, 1]))
                if u == 0:
                    p.dma(g, f"ldu{u % 2}", vem[:], vema_i)
                p.dtok(f"uload{u}", f"ldu{u % 2}")
                gg.wait_tok(f"facready{u}")
                p.dma(g, f"facs{u}", facd[u], hbs[:])
                gg.wait_rel(f"facs{u}", 16)
                p.dma(g, f"facl{u}", facb[:], facd[u].to_broadcast([128, 4]))
                p.dtok(f"facload{u}", f"facl{u}")
                for tt in range(NTT):
                    gg.wait_tok(f"thrd{u}_{tt}")
                    p.dma(g, f"sthr{tt % 2}", thr_o[tt * 128:(tt + 1) * 128, u, :], thrt[:, tt % 2])
                    p.dtok(f"thrst{u}_{tt}", f"sthr{tt % 2}")

        @blk.scalar
        def _(sc):
            s = Seq(p, sc, "sc")
            s.wait_tok("consts")
            for u in range(2):
                s.wait_rel(f"ldu{u % 2}", 96 if u == 0 else 80)
                s.op("activation", hbs[:], hbs[:], AF.Sigmoid)
                s.mark(f"hbsig{u}")
                s.wait_tok(f"facv{u}")
                for tt in range(NTT):
                    nbm = NBM(tt)
                    W = nbm * 512
                    s.wait_tok(f"diag{u}_{tt}")
                    s.op("activation", mgs[:, 0:W], Ms[:, 0:W], AF.Sigmoid, bias=gbs[:, u:u + 1],
                         scale=1.0, accum_out=st_[:, 1:2])
                    s.mark(f"mg{u}_{tt}")
                    s.op("activation", mgs[:, 0:W], mgs[:, 0:W], AF.Square, accum_out=st_[:, 2:3])
                    s.mark(f"mg2{u}_{tt}")
                    for it in range(NBIS):
                        s.wait_tok(f"mid{u}_{tt}_{it}")
                        s.op("activation", sdump[:, 0:W], Ms[:, 0:W], AF.Sign,
                             bias=nmid[:, 0:1], scale=1.0, accum_out=scnt[:, 0:1])
                        s.op("activation", sdump[:, 0:W], Ms[:, 0:W], AF.Sign,
                             bias=nmid[:, 1:2], scale=1.0, accum_out=scnt[:, 1:2])
                        s.mark(f"scnt{u}_{tt}_{it}")

        @blk.vector
        def _(ve):
            v = Seq(p, ve, "v")
            for u in range(2):
                v.wait_tok(f"hbsig{u}")
                if u == 0:
                    v.op("tensor_scalar", vem[:], vem[:], 1e-6, None, ALU.max)
                    v.op("reciprocal", vem[:], vem[:])
                v.op("tensor_tensor", hbs[:], hbs[:], hbs[:], ALU.mult)
                v.op("tensor_scalar", hbs[:], hbs[:], vem[0:1, 0:1], None, ALU.mult)
                v.op("tensor_scalar", hbs[:], hbs[:], K_BASE1, None, ALU.mult)
                v.mark(f"facready{u}")
                v.mark(f"facv{u}")
                for tt in range(NTT):
                    v.wait_tok(f"qk{u}_{tt}")
                    v.op("tensor_copy", qIf[:, tt * 128:(tt + 1) * 128], ptq[:, 0:128])
                    v.op("tensor_copy", kIf[:, tt * 128:(tt + 1) * 128], ptq[:, 128:256])
                    v.mark(f"qkev{u}_{tt}")
                for tt in range(NTT):
                    nbm = NBM(tt)
                    W = nbm * 512
                    for sb_ in range(nbm):
                        bank = (pms, pms2)[sb_ % 2]
                        v.wait_tok(f"m{u}_{tt}_{sb_}")
                        v.op("tensor_copy", Ms[:, sb_ * 512:(sb_ + 1) * 512], bank[:])
                        v.mark(f"mev{u}_{tt}_{sb_}")
                    v.mark(f"mev{u}_{tt}")
                    v.op("tensor_reduce", st_[:, 0:1], Ms[:, 0:W], AX.X, ALU.max,
                         apply_absolute_value=True)
                    v.mark(f"amax{u}_{tt}")
                    v.op("tensor_tensor", Ms[:, (nbm - 1) * 512:nbm * 512],
                         Ms[:, (nbm - 1) * 512:nbm * 512], cmask4[:, tt % 4], ALU.add)
                    v.mark(f"diag{u}_{tt}")
                    v.wait_tok(f"mg2{u}_{tt}")
                    v.op("tensor_scalar", st_[:, 3:4], st_[:, 1:2], 1.0 / SS, None, ALU.mult)
                    v.op("tensor_scalar", st_[:, 4:5], st_[:, 3:4], st_[:, 3:4], None, ALU.mult)
                    v.op("tensor_scalar", st_[:, 3:4], st_[:, 2:3], 1.0 / SS, st_[:, 4:5],
                         ALU.mult, ALU.subtract)
                    v.op("tensor_scalar", st_[:, 5:6], wss[:, tt, u:u + 1], wss[:, tt, u:u + 1], None, ALU.mult)
                    v.op("tensor_tensor", st_[:, 3:4], st_[:, 3:4], st_[:, 5:6], ALU.mult)
                    v.wait_tok(f"facload{u}")
                    v.op("tensor_tensor", bis[:, 3, :], st_[:, 3:4].to_broadcast([128, 4]), facb[:], ALU.mult)
                    v.op("tensor_copy", kti[:], bis[:, 3, :])
                    v.op("tensor_copy", ktf[:], kti[:])
                    v.op("tensor_tensor", cnt[:], ktf[:], bis[:, 3, :], ALU.is_gt)
                    v.op("tensor_sub", ktf[:], ktf[:], cnt[:])
                    v.op("tensor_scalar", ktf[:], ktf[:], K_MIN1, None, ALU.max)
                    v.op("tensor_scalar", ktf[:], ktf[:], K_MAX1, None, ALU.min)
                    v.op("tensor_scalar", bis[:, 3, :], ktf[:], 4.0, None, ALU.subtract)
                    v.op("memset", Ms[:, 0:4], -1e30)
                    v.wait_tok(f"amax{u}_{tt}")
                    v.op("tensor_scalar", bis[:, 1, :], st_[:, 0:1].to_broadcast([128, 4]), 1e-3, None, ALU.add)
                    v.op("tensor_scalar", bis[:, 0, :], bis[:, 1, :], -1.0, None, ALU.mult)
                    for it in range(NBIS):
                        v.op("tensor_add", bis[:, 2, :], bis[:, 0, :], bis[:, 1, :])
                        v.op("tensor_scalar", bis[:, 2, :], bis[:, 2, :], 0.5, None, ALU.mult)
                        if it > 0:
                            v.wait_tok(f"scnt{u}_{tt}_{it - 1}")
                        v.op("tensor_scalar", nmid[:], bis[:, 2, 2:4], -1.0, None, ALU.mult)
                        v.mark(f"mid{u}_{tt}_{it}")
                        for h in range(2):
                            v.op("tensor_scalar", mgs[:, 0:W], Ms[:, 0:W], bis[:, 2, h:h + 1],
                                 None, ALU.is_gt, ALU.add, accum_out=cnt[:, h:h + 1])
                        v.wait_tok(f"scnt{u}_{tt}_{it}")
                        v.op("tensor_scalar", cnt[:, 2:4], scnt[:], float(W), 0.5, ALU.add, ALU.mult)
                        v.op("tensor_tensor", gem[:], cnt[:], bis[:, 3, :], ALU.is_ge)
                        v.op("select", bis[:, 0, :], gem[:], bis[:, 2, :], bis[:, 0, :], add_drain=True)
                        v.op("tensor_tensor", gem[:], cnt[:], bis[:, 3, :], ALU.is_lt)
                        v.op("select", bis[:, 1, :], gem[:], bis[:, 2, :], bis[:, 1, :], add_drain=True)
                    if u == 0 and tt == 0:
                        p._thrb = [None, None]
                    if p._thrb[tt % 2] is not None:
                        v.wait_tok(p._thrb[tt % 2])
                    p._thrb[tt % 2] = f"thrst{u}_{tt}"
                    v.op("tensor_copy", thrt[:, tt % 2], bis[:, 0, :])
                    v.mark(f"thrd{u}_{tt}")
                    v.mark(f"bisdone{u}_{tt}")

        @blk.tensor
        def _(te):
            t = Seq(p, te, "t")
            t.wait_tok("identdone")
            for u in range(2):
                t.wait_tok(f"uload{u}")
                if u == 1:
                    t.wait_tok(f"bisdone0_{NTT - 1}")
                for tt in range(NTT):
                    t.wait_tok(f"qkev{u}_{tt - 1}")
                    inst = te.transpose(ptq[:, 0:128], qIs[:, tt, :], ident32[:])
                    inst.then_inc(p.sem("t"), 1); t.n += 1; p.counts["t"] = t.n
                    inst = te.transpose(ptq[:, 128:256], kIs[:, tt, :], ident32[:])
                    inst.then_inc(p.sem("t"), 1); t.n += 1; p.counts["t"] = t.n
                    t.mark(f"qk{u}_{tt}")
                t.wait_tok(f"qkev{u}_{NTT - 1}")
                if u == 0:
                    p._mbank = [None, None]
                for tt in range(NTT):
                    nbm = NBM(tt)
                    for sb_ in range(nbm):
                        bank = (pms, pms2)[sb_ % 2]
                        if p._mbank[sb_ % 2] is not None:
                            t.wait_tok(p._mbank[sb_ % 2])
                        inst = te.matmul(bank[:], qIf[:, tt * 128:(tt + 1) * 128],
                                         kIf[:, sb_ * 512:(sb_ + 1) * 512], start=True, stop=True)
                        inst.then_inc(p.sem("t"), 1); t.n += 1; p.counts["t"] = t.n
                        t.mark(f"m{u}_{tt}_{sb_}")
                        p._mbank[sb_ % 2] = f"mev{u}_{tt}_{sb_}"

        @blk.vector
        def _(ve):
            pass

        @blk.sync
        def _(sy):
            s = Seq(p, sy, "syf")
            for nm in ("sthr0", "sthr1"):
                if p.dmac.get(p.rsem(nm)):
                    s.wait(nm, p.dmac[p.rsem(nm)])


def build_sel1(SS):
    return build_twice(emit_sel1, SS)



FF1 = 4096


def emit_moe1(p, TL, presummed=False):
    NT = TL // 128
    NTS = TL // 512
    _lysem = "ly%d"
    nc = p.nc

    xn_i = p.din("xn1", [TL, D], BF16)
    eps_i = p.din("eps", [128, TL // 128, 2])
    w1_i = p.din("w1", [8, D, 2 * FF1])
    w2_i = p.din("w2", [8, FF1, D])
    x_i = p.din("x1", [TL, D])
    ya_i = p.din("ya", [D, TL])
    yb_i = None if presummed else p.din("yb", [D, TL])
    x2_o = p.dout("x2", [TL, D])

    hf = p.sb("hf", [128, DC, TL], BF16)
    w1a = p.sb("w1a", [128, 2, DC, 512], BF16)
    w1g = p.sb("w1g", [128, 2, DC, 512], BF16)
    w2s = p.sb("w2s", [128, 2, 4, D], BF16)
    gcol = p.sb("gcol", [128, NT, 8], F32)
    esl = p.sb("esl", [128, NT, 2], F32)
    ysb = p.sb("ysb", [128, NT, D], F32)
    hid16 = p.sb("hid16", [128, 2, 4, 512], BF16)
    sil = p.sb("sil", [128, 512], F32)
    xs_ = p.sb("xs_", [128, 2, D], F32)
    yab = p.sb("yab", [128, 2, 2, DC * 128], F32)
    g1t = p.sb("g1t", [128, 2, D], F32)
    s1t = p.sb("s1t", [128, 2, D], F32)
    ident32 = p.sb("ident32", [128, 128], F32)

    ph1 = p.ps("ph1", [128, 512], F32)
    ph2 = p.ps("ph2", [128, 512], F32)
    pg1 = p.ps("pg1", [128, 512], F32)
    pg2 = p.ps("pg2", [128, 512], F32)
    py1 = p.ps("py1", [128, 512], F32)
    py2 = p.ps("py2", [128, 512], F32)
    pt1 = p.ps("pt1", [128, 512], F32)
    pt2 = p.ps("pt2", [128, 512], F32)
    with nc.Block() as blk:
        @blk.sync
        def _(sy):
            for c in range(DC):
                p.dma(sy, "ldh", hf[:, c], xn_i[:, c * 128:(c + 1) * 128], transpose=True)
            p.dma(sy, "ldc", esl[:], eps_i)
            p.dtok("hload", "ldh")

        @blk.gpsimd
        def _(g):
            gg = Seq(p, g, "g")
            gg.op("memset", ident32[:], 1.0)
            gg.op("affine_select", ident32[:], ident32[:], [[-1, 128]], ALU.is_equal, 0.0,
                  base=0, channel_multiplier=1)
            gg.op("memset", ysb[:], 0.0)
            gg.mark("init")
            p.emit_consts(gg)
            for e in range(8):
                for q in range(8):
                    j = e * 8 + q
                    gg.wait_tok(f"wfree{j - 2}")
                    p.dma(g, f"lw{j % 2}", w1a[:, j % 2],
                          w1_i[e].rearrange("(c k) f -> k c f", k=128)[:, :, q * 512:(q + 1) * 512])
                    p.dma(g, f"lw{j % 2}", w1g[:, j % 2],
                          w1_i[e].rearrange("(c k) f -> k c f", k=128)[:, :, FF1 + q * 512:FF1 + (q + 1) * 512])
                    p.dma(g, f"lw{j % 2}", w2s[:, j % 2],
                          w2_i[e, q * 512:(q + 1) * 512].rearrange("(c k) f -> k c f", k=128))
                    p.dtok(f"wld{j}", f"lw{j % 2}")
            gg.wait_tok("yready")
            for i in range(NT + 2):
                if i < NT:
                    gg.wait_tok(f"x2d{i - 2}")
                    gg.wait_tok(f"ytpA{i - 1}")
                    p.dma(g, f"lx{i % 2}", xs_[:, i % 2], x_i.rearrange("(n p) d -> n p d", p=128)[i])
                    p.dma(g, f"ly{i % 2}", yab[:, i % 2, 0].rearrange("p (c t) -> p c t", t=128),
                          ya_i[:, i * 128:(i + 1) * 128].rearrange("(c p) t -> p c t", p=128))
                    if not presummed:
                        p.dma(g, f"ly{i % 2}", yab[:, i % 2, 1].rearrange("p (c t) -> p c t", t=128),
                              yb_i[:, i * 128:(i + 1) * 128].rearrange("(c p) t -> p c t", p=128))
                if i >= 2:
                    gg.wait_tok(f"x2v{i - 2}")
                    p.dma(g, f"sx{i % 2}", x2_o.rearrange("(n p) d -> n p d", p=128)[i - 2], g1t[:, (i - 2) % 2])
                    p.dtok(f"x2st{i - 2}", f"sx{i % 2}")

        @blk.tensor
        def _(te):
            t = Seq(p, te, "t")
            t.wait_tok("init")
            t.wait_tok("hload")
            for e in range(8):
                for q in range(8):
                    j = e * 8 + q
                    t.wait_rel("lw%d" % (j % 2), 48 * (j // 2 + 1))
                    for ts in range(NTS):
                        for fc in range(4):
                            gidx = ts * 4 + fc
                            ba = (ph1, ph2)[gidx % 2]
                            bg = (pg1, pg2)[gidx % 2]
                            if gidx >= 2:
                                t.wait_tok(f"hg{j}_{gidx - 2}")
                            elif j > 0:
                                t.wait_tok(f"hg{j - 1}_{NTS * 4 - 2 + gidx}")
                            for dc in range(DC):
                                te.matmul(ba[:], w1a[:, j % 2, dc, fc * 128:(fc + 1) * 128],
                                          hf[:, dc, ts * 512:(ts + 1) * 512],
                                          start=(dc == 0), stop=(dc == DC - 1))
                            for dc in range(DC):
                                inst = te.matmul(bg[:], w1g[:, j % 2, dc, fc * 128:(fc + 1) * 128],
                                                 hf[:, dc, ts * 512:(ts + 1) * 512],
                                                 start=(dc == 0), stop=(dc == DC - 1))
                            inst.then_inc(p.sem("t"), 1); t.n += 1; p.counts["t"] = t.n
                            t.mark(f"h{j}_{gidx}")
                        for sub in range(4):
                            i = ts * 4 + sub
                            for nn in range(2):
                                jj = i * 2 + nn
                                bank = (py1, py2)[jj % 2]
                                t.wait_tok(f"yev{j}_{jj - 2}")
                                t.wait_tok(f"hg{j}_{ts * 4 + 3}")
                                for fc in range(4):
                                    inst = te.matmul(bank[:], hid16[:, j % 2, fc, sub * 128:(sub + 1) * 128],
                                                     w2s[:, j % 2, fc, nn * 512:(nn + 1) * 512],
                                                     start=(fc == 0), stop=(fc == 3))
                                inst.then_inc(p.sem("t"), 1); t.n += 1; p.counts["t"] = t.n
                                t.mark(f"y{j}_{jj}")
                    t.mark(f"wfree{j}")
            for i in range(NT):
                t.wait_tok(f"ysum{i}")
                t.wait_rel(_lysem % (i % 2), (16 if presummed else 32) * (i // 2 + 1))
                t.wait_tok(f"ytpeA{i - 1}")
                for c in range(4):
                    inst = te.transpose(pt1[:, c * 128:(c + 1) * 128],
                                        yab[:, i % 2, 0, c * 128:(c + 1) * 128], ident32[:])
                inst.then_inc(p.sem("t"), 1); t.n += 1; p.counts["t"] = t.n
                t.mark(f"ytpA{i}")
                t.wait_tok(f"ytpeB{i - 1}")
                for c in range(4):
                    inst = te.transpose(pt2[:, c * 128:(c + 1) * 128],
                                        yab[:, i % 2, 0, (4 + c) * 128:(5 + c) * 128], ident32[:])
                inst.then_inc(p.sem("t"), 1); t.n += 1; p.counts["t"] = t.n
                t.mark(f"ytpB{i}")

        @blk.vector
        def _(ve):
            v = Seq(p, ve, "v")
            v.wait_tok("init")
            v.wait_rel("ldc", 16)
            for i in range(NT):
                for e in range(8):
                    v.op("tensor_scalar", gcol[:, i, e:e + 1], esl[:, i, 0:1], float(e), None,
                         ALU.is_equal)
                v.op("tensor_tensor", gcol[:, i, :], gcol[:, i, :],
                     esl[:, i, 1:2].to_broadcast([128, 8]), ALU.mult)
            v.mark("gateready")
            for e in range(8):
                for q in range(8):
                    j = e * 8 + q
                    for ts in range(NTS):
                        for fc in range(4):
                            gidx = ts * 4 + fc
                            ba = (ph1, ph2)[gidx % 2]
                            bg = (pg1, pg2)[gidx % 2]
                            v.wait_tok(f"h{j}_{gidx}")
                            v.wait_tok(f"sil{j}_{gidx}")
                            if ts > 0:
                                v.wait_tok(f"y{j}_{ts * 8 - 1}")
                            elif j >= 2:
                                v.wait_tok(f"y{j - 2}_{NTS * 8 - 1}")
                            v.op("tensor_tensor", hid16[:, j % 2, fc, :], sil[:], bg[:], ALU.mult)
                            v.mark(f"hg{j}_{gidx}")
                        for sub in range(4):
                            i = ts * 4 + sub
                            for nn in range(2):
                                jj = i * 2 + nn
                                bank = (py1, py2)[jj % 2]
                                v.wait_tok(f"y{j}_{jj}")
                                v.op("scalar_tensor_tensor", ysb[:, i, nn * 512:(nn + 1) * 512],
                                     bank[:], gcol[:, i, e:e + 1], ysb[:, i, nn * 512:(nn + 1) * 512],
                                     ALU.mult, ALU.add)
                                v.mark(f"yev{j}_{jj}")
            v.mark("yready")
            for i in range(NT):
                if presummed:
                    v.wait_rel(f"ly{i % 2}", 16 * (i // 2 + 1))
                else:
                    v.wait_rel(f"ly{i % 2}", 32 * (i // 2 + 1))
                    v.op("tensor_add", yab[:, i % 2, 0], yab[:, i % 2, 0], yab[:, i % 2, 1])
                v.mark(f"ysum{i}")
                v.wait_tok(f"ytpA{i}")
                v.op("tensor_copy", s1t[:, i % 2, 0:512], pt1[:])
                v.mark(f"ytpeA{i}")
                v.wait_tok(f"ytpB{i}")
                v.op("tensor_copy", s1t[:, i % 2, 512:1024], pt2[:])
                v.mark(f"ytpeB{i}")
                v.wait_tok(f"sil1{i}")
                v.wait_rel(f"lx{i % 2}", 16 * (i // 2 + 1))
                v.op("tensor_tensor", g1t[:, i % 2], g1t[:, i % 2], s1t[:, i % 2], ALU.mult)
                v.op("tensor_tensor", g1t[:, i % 2], g1t[:, i % 2], xs_[:, i % 2], ALU.add)
                v.mark(f"x2v{i}")
                v.mark(f"x2d{i}")

        @blk.scalar
        def _(sc):
            s = Seq(p, sc, "sc")
            s.wait_tok("consts")
            for e in range(8):
                for q in range(8):
                    j = e * 8 + q
                    for ts in range(NTS):
                        for fc in range(4):
                            gidx = ts * 4 + fc
                            ba = (ph1, ph2)[gidx % 2]
                            s.wait_tok(f"h{j}_{gidx}")
                            if gidx > 0:
                                s.wait_tok(f"hg{j}_{gidx - 1}")
                            elif j > 0:
                                s.wait_tok(f"hg{j - 1}_{NTS * 4 - 1}")
                            s.op("activation", sil[:], ba[:], AF.Silu)
                            s.mark(f"sil{j}_{gidx}")
            for i in range(NT):
                s.wait_tok("yready")
                s.wait_tok(f"x2st{i - 2}", may_miss=True)
                s.op("activation", g1t[:, i % 2], ysb[:, i], AF.Silu)
                s.mark(f"sil1{i}")

        @blk.sync
        def _(sy):
            s = Seq(p, sy, "syf")
            for nm in ("sx0", "sx1"):
                if p.dmac.get(p.rsem(nm)):
                    s.wait(nm, p.dmac[p.rsem(nm)])


def build_moe1(TL):
    return build_twice(emit_moe1, TL)



def emit_fused(p, TL):
    SX = 2 * TL
    nc = p.nc
    f = lambda n, shape, dt=F32: nc.dram_tensor(n, shape, dt, kind="Internal").ap()
    RG = [[0, 1], [2, 3], [4, 5], [6, 7]]

    ein = lambda n, shape, dt=F32: nc.dram_tensor(n, shape, dt, kind="ExternalInput").ap()
    x_in = ein("x", [TL, D])
    l0a_wq = ein("l0a_wq", [D, D]); l0a_wk = ein("l0a_wk", [D, D]); l0a_wv = ein("l0a_wv", [D, D])
    l0a_wo = ein("l0a_wo", [D, D]); l0a_bo = ein("l0a_bo", [1, D])
    l0a_g = ein("l0a_g", [1, D]); l0a_b = ein("l0a_b", [1, D])
    l0f_g = ein("l0f_g", [1, D]); l0f_b = ein("l0f_b", [1, D])
    l0f_wq = ein("l0f_wq", [D, D]); l0f_wk = ein("l0f_wk", [D, D]); l0f_wv = ein("l0f_wv", [D, D])
    l0f_qs = ein("l0f_qs", [1, D]); l0f_ks = ein("l0f_ks", [1, D])
    l0f_wo = ein("l0f_wo", [512, D])
    l0f_gw = ein("l0f_gw", [D, 8]); l0f_gb = ein("l0f_gb", [1, 8])
    l0f_w1 = ein("l0f_w1", [8, D, FF0]); l0f_b1 = ein("l0f_b1", [8, FF0])
    l0f_w2 = ein("l0f_w2", [8, FF0, D]); l0f_b2 = ein("l0f_b2", [8, D])
    wI_in = ein("wI", [D, 268]); rb_in = ein("rb", [1, 8])
    w5_in = ein("w5", [5, D, D])
    ch_in = ein("ch", [TL, 32]); sh_in = ein("sh", [TL, 32])
    hb_in = ein("hb", [2, 1, 4]); gbx_in = ein("gbx", [2, 1, 1]); vema_in = ein("vema", [1, 1])
    l1a_wo = ein("l1a_wo", [512, D])
    l1f_w1 = ein("l1f_w1", [8, D, 2 * FF1]); l1f_w2 = ein("l1f_w2", [8, FF1, D])

    xn0 = f("i_xn0", [TL, D]); eq = f("i_eq", [D, TL], BF16); ctx = f("i_ctx", [D, D + 1])
    ctxS = f("i_ctxS", [D, D + 1])
    qh = f("i_qh", [TL, D], BF16); kh = f("i_kh", [TL, D], BF16); vh = f("i_vh", [TL, D], BF16)
    qF = f("i_qF", [SX, 512], BF16); kF = f("i_kF", [SX, 512], BF16); vF = f("i_vF", [SX, 512], BF16)
    qhG = f("i_qhG", [SX, D], BF16); khG = f("i_khG", [SX, D], BF16); vhG = f("i_vhG", [SX, D], BF16)
    hoff = nc.dram_tensor("hoff", [1, 1], mybir.dt.int32, kind="ExternalInput").ap()
    y0T = f("i_y0T", [2, D, TL]); y0R = f("i_y0R", [D, TL])
    s0 = f("i_s0", [TL, D]); h2 = f("i_h2", [TL, D], BF16)
    x1 = f("i_x1", [TL, D])
    xn1b = f("i_xn1b", [TL, D], BF16)
    qI = f("i_qI", [TL, 128]); kI = f("i_kI", [TL, 128]); wsg = f("i_wsg", [TL, 4])
    eps = f("i_eps", [128, TL // 128, 2])
    rq = f("i_rq", [TL, D], BF16); rk = f("i_rk", [TL, D], BF16)
    vg = f("i_vg", [TL, D], BF16); gos = f("i_gos", [TL, D], BF16)
    qIG = f("i_qIG", [SX, 64]); kIG = f("i_kIG", [SX, 64]); wsG = f("i_wsG", [SX, 2])
    qIGf = f("i_qIGf", [SX, 128]); kIGf = f("i_kIGf", [SX, 128]); wsGf = f("i_wsGf", [SX, 4])
    rqF = f("i_rqF", [SX, 512], BF16); rkF = f("i_rkF", [SX, 512], BF16)
    vgF = f("i_vgF", [SX, 512], BF16); goF = f("i_goF", [SX, 512], BF16)
    rqG = f("i_rqG", [SX, D], BF16); rkG = f("i_rkG", [SX, D], BF16)
    vgG = f("i_vgG", [SX, D], BF16); goG = f("i_goG", [SX, D], BF16)
    thr = f("i_thr", [SX, 2, 4])
    y1T = f("i_y1T", [2, D, TL]); y1R = f("i_y1R", [D, TL])
    x2 = nc.dram_tensor("x2", [TL, D], F32, kind="ExternalOutput").ap()

    p.ctile()

    p._xn = [0]

    def xchg(ops, copies=()):
        tag = p._xn[0]; p._xn[0] += 1
        with nc.Block() as blk:
            @blk.gpsimd
            def _(g):
                s = p.sem(f"cc{tag}")
                n = 0
                for kind, alu, srcap, dstap in ops:
                    g.collective_compute(kind, alu, RG, [srcap], [dstap]).then_inc(s, 16)
                    n += 16
                g.wait_ge(s, n)
                if copies:
                    if p._hreg is None:
                        p._hreg = nc.values_load(hoff, engines=[mybir.EngineType.Pool],
                                                 min_val=0, max_val=1,
                                                 skip_runtime_bounds_check=True)
                    s2 = p.sem(f"cp{tag}")
                    n2 = 0
                    for dst, srcfull, w in copies:
                        g.dma_start(out=dst, in_=srcfull[:, bass.ts(p._hreg, w)]).then_inc(s2, 16)
                        n2 += 16
                    g.wait_ge(s2, n2)
        nc.all_engine_barrier()
    p._hreg = None

    def run(name, emitter, io, *args, **kw):
        with p.phase(name):
            p.io = io
            emitter(p, *args, **kw)
            p.io = {}
        nc.all_engine_barrier()

    run("A", emit_pre0a, {"x": x_in, "wq": l0a_wq, "wk": l0a_wk, "wv": l0a_wv,
                          "xn0": xn0, "eq": eq, "ctx": ctx}, TL)
    run("B", emit_pre0b, {"xn": xn0, "wq": l0f_wq, "wk": l0f_wk, "wv": l0f_wv,
                          "ln_g": l0f_g, "ln_b": l0f_b, "qsc": l0f_qs, "ksc": l0f_ks,
                          "qh": qh, "kh": kh, "vh": vh}, TL)
    r2 = lambda a: a.rearrange("t (r c) -> r t c", r=2)
    xchg([("AllReduce", ALU.add, ctx, ctxS),
          ("AllGather", ALU.bypass, qh, qhG),
          ("AllGather", ALU.bypass, kh, khG),
          ("AllGather", ALU.bypass, vh, vhG)],
         copies=[(qF, qhG, 512), (kF, khG, 512), (vF, vhG, 512)])
    run("C", emit_attn0, {"q": qF, "k": kF, "v": vF, "wo": l0f_wo, "yT": y0T},
        SX, 8, QKSCALE, False)
    xchg([("ReduceScatter", ALU.add, y0T, y0R)])
    run("DD", emit_post0, {"xn": xn0, "eq": eq, "ctxa": ctxS, "wo": l0a_wo, "bo": l0a_bo,
                           "gam": l0a_g, "bet": l0a_b, "ya": y0R, "lng": l0f_g, "lnb": l0f_b,
                           "s0": s0, "h2": h2}, TL, True)
    run("E", emit_moe0, {"h2": h2, "gw": l0f_gw, "gb": l0f_gb, "w1": l0f_w1, "b1": l0f_b1,
                         "w2": l0f_w2, "b2": l0f_b2, "lng": l0f_g, "lnb": l0f_b,
                         "x": x_in, "s0": s0, "x1": x1}, TL)
    run("F", emit_pre1, {"x1": x1, "wI": wI_in, "rb": rb_in, "w5": w5_in, "ch": ch_in,
                         "sh": sh_in, "xn1b": xn1b, "qI": qI, "kI": kI, "wsig": wsg,
                         "eps": eps, "rq": rq, "rk": rk, "vg": vg, "gos": gos}, TL)
    xchg([("AllGather", ALU.bypass, qI, qIGf),
          ("AllGather", ALU.bypass, kI, kIGf),
          ("AllGather", ALU.bypass, wsg, wsGf),
          ("AllGather", ALU.bypass, rq, rqG),
          ("AllGather", ALU.bypass, rk, rkG),
          ("AllGather", ALU.bypass, vg, vgG),
          ("AllGather", ALU.bypass, gos, goG)],
         copies=[(qIG, qIGf, 64), (kIG, kIGf, 64), (wsG, wsGf, 2),
                 (rqF, rqG, 512), (rkF, rkG, 512), (vgF, vgG, 512), (goF, goG, 512)])
    run("G", emit_sel1, {"qI": qIG.rearrange("t (u c) -> u t c", u=2),
                         "kI": kIG.rearrange("t (u c) -> u t c", u=2),
                         "ws": wsG, "hb": hb_in, "gbx": gbx_in, "vema": vema_in,
                         "thr": thr}, SX)
    run("H", emit_attn0, {"q": rqF, "k": rkF, "v": vgF, "go": goF, "wo": l1a_wo,
                          "qIs": qIG, "kIs": kIG, "thr": thr.rearrange("t u c -> t (u c)"),
                          "yT": y1T}, SX, 8, 0.125, True, True)
    xchg([("ReduceScatter", ALU.add, y1T, y1R)])
    run("I", emit_moe1, {"xn1": xn1b, "eps": eps, "w1": l1f_w1, "w2": l1f_w2,
                         "x1": x1, "ya": y1R, "x2": x2}, TL, True)


def build_fused(TL):
    return build_twice(emit_fused, TL)



_BUILT = {}


def _get(name, builder, *args):
    key = (name,) + args
    if key not in _BUILT:
        _BUILT[key] = builder(*args)
    return _BUILT[key]


def _rope_tables(SL=S):
    import math
    HD = 64
    MAXPOS, ORIG, BASE, SCALE = 262144, 8192, 10000.0, 32.0
    BETA_FAST, BETA_SLOW = 1.0, 32.0
    dim = HD
    ext = MAXPOS / ORIG
    scaled_base = BASE * ext ** (dim / (dim - 2))
    idx = (np.arange(0, dim, 2).astype(np.float32) / dim).astype(np.float32)
    inv_freq = (1.0 / scaled_base ** idx).astype(np.float32)
    freq_extra = (1.0 / BASE ** idx).astype(np.float32)
    wavelen = (2.0 * math.pi / freq_extra).astype(np.float32)
    ramp = np.clip((wavelen - BETA_FAST) / (BETA_SLOW - BETA_FAST), 0.0, 1.0).astype(np.float32)
    scale = (1.0 + (SCALE - 1.0) * ramp).astype(np.float32)
    t = np.arange(SL, dtype=np.float32)
    freqs = (t[:, None] / scale[None, :] * inv_freq[None, :]).astype(np.float32)
    emb = np.concatenate([freqs, freqs], -1)
    return (np.cos(emb)[:, ::2].astype(np.float32),
            np.sin(emb)[:, ::2].astype(np.float32))


def _run(nc, in_maps):
    res = run_bass_kernel_spmd(nc, in_maps, core_ids=list(range(NCORE)))
    return res.results


def _pcol(a):
    a = np.asarray(a).reshape(-1)
    return np.ascontiguousarray(a.reshape(-1, 128).T)


def kernel_fused(_TL=None, _runner=None, **inp):
    TL = _TL or (S // 2)
    SL = 2 * TL
    f32 = np.float32
    C = lambda a: np.ascontiguousarray(a, dtype=f32)
    x_full = np.asarray(inp["L0_attn_x"], dtype=f32)
    nc = _get("fused", build_fused, TL)
    ch_t, sh_t = _rope_tables(SL)
    wI = C(np.concatenate([inp["L1_attn_W_Iq"], inp["L1_attn_W_Ik"],
                           inp["L1_attn_W_Iw"], inp["L1_ffn_router_w"]], axis=1))
    w5 = C(np.stack([inp["L1_attn_W_q"], inp["L1_attn_W_k"], inp["L1_attn_W_v"],
                     inp["L1_attn_W_gv"], inp["L1_attn_W_go"]], 0))
    hb = np.asarray(inp["L1_attn_head_bias"], dtype=f32)
    gbias = np.asarray(inp["L1_attn_gate_bias"], dtype=f32)
    woF = np.asarray(inp["L0_ffn_Wo"], dtype=f32)
    woL1 = np.asarray(inp["L1_attn_W_o"], dtype=f32)
    in_maps = []
    for c in range(NCORE):
        b, th = c // 2, c % 2
        hh = th
        i0, i1 = 2 * hh, 2 * hh + 1
        sl = slice(th * TL, (th + 1) * TL)
        im = {
            "x": C(x_full[b, sl]),
            "l0a_wq": C(inp["L0_attn_Wq"]), "l0a_wk": C(inp["L0_attn_Wk"]),
            "l0a_wv": C(inp["L0_attn_Wv"]), "l0a_wo": C(inp["L0_attn_Wo"]),
            "l0a_bo": C(inp["L0_attn_bo"].reshape(1, D)),
            "l0a_g": C(inp["L0_attn_gamma"].reshape(1, D)),
            "l0a_b": C(inp["L0_attn_beta"].reshape(1, D)),
            "l0f_g": C(inp["L0_ffn_ln_g"].reshape(1, D)),
            "l0f_b": C(inp["L0_ffn_ln_b"].reshape(1, D)),
            "l0f_wq": C(inp["L0_ffn_Wq"]), "l0f_wk": C(inp["L0_ffn_Wk"]),
            "l0f_wv": C(inp["L0_ffn_Wv"]),
            "l0f_qs": C(np.asarray(inp["L0_ffn_q_scale"]).reshape(1, D)),
            "l0f_ks": C(np.asarray(inp["L0_ffn_k_scale"]).reshape(1, D)),
            "l0f_wo": C(woF[hh * 512:(hh + 1) * 512, :]),
            "l0f_gw": C(inp["L0_ffn_gate_w"]), "l0f_gb": C(inp["L0_ffn_gate_b"].reshape(1, 8)),
            "l0f_w1": C(inp["L0_ffn_w1"]), "l0f_b1": C(inp["L0_ffn_b1"]),
            "l0f_w2": C(inp["L0_ffn_w2"]), "l0f_b2": C(inp["L0_ffn_b2"]),
            "wI": wI, "rb": C(inp["L1_ffn_router_b"].reshape(1, 8)), "w5": w5,
            "ch": C(ch_t[sl]), "sh": C(sh_t[sl]),
            "hb": C(np.stack([hb[4 * i0:4 * i0 + 4].reshape(1, 4),
                              hb[4 * i1:4 * i1 + 4].reshape(1, 4)])),
            "gbx": C(np.stack([gbias[i0].reshape(1, 1), gbias[i1].reshape(1, 1)])),
            "vema": C(np.asarray(inp["L1_attn_variance_ema"]).reshape(1, 1)),
            "l1a_wo": C(woL1[hh * 512:(hh + 1) * 512, :]),
            "l1f_w1": C(inp["L1_ffn_w1"]), "l1f_w2": C(inp["L1_ffn_w2"]),
            "hoff": np.array([[hh]], dtype=np.int32),
        }
        in_maps.append(im)
    res = _runner(nc, in_maps) if _runner else _run(nc, in_maps)
    out = np.zeros((B, SL, D), f32)
    for c in range(NCORE):
        b, th = c // 2, c % 2
        out[b, th * TL:(th + 1) * TL] = res[c]["x2"]
    return out


FUSED = False


def kernel(**inp):
    if FUSED:
        return kernel_fused(**inp)
    return kernel_unfused(**inp)


def kernel_unfused(**inp):
    TL = S // 2
    f32 = np.float32
    C = lambda a: np.ascontiguousarray(a, dtype=f32)
    x_full = np.asarray(inp["L0_attn_x"], dtype=f32)
    cores = [(c // 2, c % 2) for c in range(NCORE)]
    sl = lambda th: slice(th * TL, (th + 1) * TL)

    nc = _get("pre0a", build_pre0a, TL)
    im = [{"x": C(x_full[b, sl(th)]),
           "wq": C(inp["L0_attn_Wq"]), "wk": C(inp["L0_attn_Wk"]), "wv": C(inp["L0_attn_Wv"])}
          for b, th in cores]
    r1 = _run(nc, im)

    nc = _get("pre0b", build_pre0b, TL)
    qsc = C(np.asarray(inp["L0_ffn_q_scale"]).reshape(1, D))
    ksc = C(np.asarray(inp["L0_ffn_k_scale"]).reshape(1, D))
    im = [{"xn": r1[c]["xn0"],
           "wq": C(inp["L0_ffn_Wq"]), "wk": C(inp["L0_ffn_Wk"]), "wv": C(inp["L0_ffn_Wv"]),
           "ln_g": C(inp["L0_ffn_ln_g"].reshape(1, D)), "ln_b": C(inp["L0_ffn_ln_b"].reshape(1, D)),
           "qsc": qsc, "ksc": ksc}
          for c in range(NCORE)]
    r2 = _run(nc, im)

    nc = _get("attn0", build_attn0, S, 8, QKSCALE, False)
    qf = [np.concatenate([r2[2 * b]["qh"], r2[2 * b + 1]["qh"]], 0) for b in range(B)]
    kf = [np.concatenate([r2[2 * b]["kh"], r2[2 * b + 1]["kh"]], 0) for b in range(B)]
    vf = [np.concatenate([r2[2 * b]["vh"], r2[2 * b + 1]["vh"]], 0) for b in range(B)]
    woF = np.asarray(inp["L0_ffn_Wo"], dtype=f32)
    im = [{"q": np.ascontiguousarray(qf[b][:, hh * 512:(hh + 1) * 512]),
           "k": np.ascontiguousarray(kf[b][:, hh * 512:(hh + 1) * 512]),
           "v": np.ascontiguousarray(vf[b][:, hh * 512:(hh + 1) * 512]),
           "wo": C(woF[hh * 512:(hh + 1) * 512, :])}
          for b, hh in cores]
    r3 = _run(nc, im)

    nc = _get("post0", build_post0, TL)
    im = [{"xn": r1[c]["xn0"], "eq": r1[c]["eq"],
           "ctxa": r1[2 * b]["ctx"], "ctxb": r1[2 * b + 1]["ctx"],
           "wo": C(inp["L0_attn_Wo"]), "bo": C(inp["L0_attn_bo"].reshape(1, D)),
           "gam": C(inp["L0_attn_gamma"].reshape(1, D)), "bet": C(inp["L0_attn_beta"].reshape(1, D)),
           "ya": np.ascontiguousarray(r3[2 * b]["yT"][th]),
           "yb": np.ascontiguousarray(r3[2 * b + 1]["yT"][th]),
           "lng": C(inp["L0_ffn_ln_g"].reshape(1, D)), "lnb": C(inp["L0_ffn_ln_b"].reshape(1, D))}
          for c, (b, th) in enumerate(cores)]
    r4 = _run(nc, im)

    nc = _get("moe0", build_moe0, TL)
    im = [{"h2": r4[c]["h2"], "gw": C(inp["L0_ffn_gate_w"]), "gb": C(inp["L0_ffn_gate_b"].reshape(1, 8)),
           "w1": C(inp["L0_ffn_w1"]), "b1": C(inp["L0_ffn_b1"]),
           "w2": C(inp["L0_ffn_w2"]), "b2": C(inp["L0_ffn_b2"]),
           "lng": C(inp["L0_ffn_ln_g"].reshape(1, D)), "lnb": C(inp["L0_ffn_ln_b"].reshape(1, D)),
           "x": C(x_full[b, sl(th)]), "s0": r4[c]["s0"]}
          for c, (b, th) in enumerate(cores)]
    r5 = _run(nc, im)

    nc = _get("pre1", build_pre1, TL)
    wI = C(np.concatenate([inp["L1_attn_W_Iq"], inp["L1_attn_W_Ik"],
                           inp["L1_attn_W_Iw"], inp["L1_ffn_router_w"]], axis=1))
    w5 = C(np.stack([inp["L1_attn_W_q"], inp["L1_attn_W_k"], inp["L1_attn_W_v"],
                     inp["L1_attn_W_gv"], inp["L1_attn_W_go"]], 0))
    ch_t, sh_t = _rope_tables()
    im = [{"x1": r5[c]["x1"], "wI": wI, "rb": C(inp["L1_ffn_router_b"].reshape(1, 8)),
           "w5": w5, "ch": C(ch_t[sl(th)]), "sh": C(sh_t[sl(th)])}
          for c, (b, th) in enumerate(cores)]
    r6 = _run(nc, im)

    nc = _get("sel1", build_sel1, S)
    qI_full = [np.concatenate([r6[2 * b]["qI"], r6[2 * b + 1]["qI"]], 0) for b in range(B)]
    kI_full = [np.concatenate([r6[2 * b]["kI"], r6[2 * b + 1]["kI"]], 0) for b in range(B)]
    ws_full = [np.concatenate([r6[2 * b]["wsig"], r6[2 * b + 1]["wsig"]], 0) for b in range(B)]
    hb = np.asarray(inp["L1_attn_head_bias"], dtype=f32)
    gbias = np.asarray(inp["L1_attn_gate_bias"], dtype=f32)
    vema = np.asarray(inp["L1_attn_variance_ema"], dtype=f32).reshape(1, 1)
    im = []
    for c in range(NCORE):
        us = [(c * 2 + k) // 4 for k in range(2)], [(c * 2 + k) % 4 for k in range(2)]
        qIu = np.stack([qI_full[us[0][k]][:, us[1][k] * 32:(us[1][k] + 1) * 32] for k in range(2)])
        kIu = np.stack([kI_full[us[0][k]][:, us[1][k] * 32:(us[1][k] + 1) * 32] for k in range(2)])
        wsu = np.stack([ws_full[us[0][k]][:, us[1][k]] for k in range(2)], axis=1)
        hbu = np.stack([hb[4 * us[1][k]:4 * us[1][k] + 4].reshape(1, 4) for k in range(2)])
        gbu = np.stack([gbias[us[1][k]].reshape(1, 1) for k in range(2)])
        im.append({"qI": C(qIu), "kI": C(kIu), "ws": C(wsu), "hb": C(hbu),
                   "gbx": C(gbu), "vema": vema})
    r7 = _run(nc, im)

    thr_full = [np.zeros((S, 16), f32) for _ in range(B)]
    for c in range(NCORE):
        for k in range(2):
            u = c * 2 + k
            bu, iu = u // 4, u % 4
            thr_full[bu][:, 4 * iu:4 * iu + 4] = r7[c]["thr"][:, k, :]

    nc = _get("attn1", build_attn1, S, 8)
    rq = [np.concatenate([r6[2 * b]["rq"], r6[2 * b + 1]["rq"]], 0) for b in range(B)]
    rk = [np.concatenate([r6[2 * b]["rk"], r6[2 * b + 1]["rk"]], 0) for b in range(B)]
    vg = [np.concatenate([r6[2 * b]["vg"], r6[2 * b + 1]["vg"]], 0) for b in range(B)]
    gos = [np.concatenate([r6[2 * b]["gos"], r6[2 * b + 1]["gos"]], 0) for b in range(B)]
    woL1 = np.asarray(inp["L1_attn_W_o"], dtype=f32)
    im = [{"q": np.ascontiguousarray(rq[b][:, hh * 512:(hh + 1) * 512]),
           "k": np.ascontiguousarray(rk[b][:, hh * 512:(hh + 1) * 512]),
           "v": np.ascontiguousarray(vg[b][:, hh * 512:(hh + 1) * 512]),
           "go": np.ascontiguousarray(gos[b][:, hh * 512:(hh + 1) * 512]),
           "wo": C(woL1[hh * 512:(hh + 1) * 512, :]),
           "qIs": np.ascontiguousarray(qI_full[b][:, hh * 64:(hh + 1) * 64]),
           "kIs": np.ascontiguousarray(kI_full[b][:, hh * 64:(hh + 1) * 64]),
           "thr": np.ascontiguousarray(thr_full[b][:, hh * 8:(hh + 1) * 8])}
          for b, hh in cores]
    r8 = _run(nc, im)

    nc = _get("moe1", build_moe1, TL)
    im = [{"xn1": r6[c]["xn1b"], "eps": r6[c]["eps"],
           "w1": C(inp["L1_ffn_w1"]), "w2": C(inp["L1_ffn_w2"]),
           "x1": r5[c]["x1"],
           "ya": np.ascontiguousarray(r8[2 * b]["yT"][th]),
           "yb": np.ascontiguousarray(r8[2 * b + 1]["yT"][th])}
          for c, (b, th) in enumerate(cores)]
    r9 = _run(nc, im)

    out = np.zeros((B, S, D), f32)
    for c, (b, th) in enumerate(cores):
        out[b, sl(th)] = r9[c]["x2"]
    return out


if __name__ == "__main__":
    nc = build_moe0(512)
    print("moe0 ok:", nc.n_instructions(), "instructions")
    nc = build_pre0a(512)
    print("pre0a ok:", nc.n_instructions(), "instructions")
    nc = build_pre0b(512)
    print("pre0b ok:", nc.n_instructions(), "instructions")
    nc = build_attn0(512)
    print("attn0 ok:", nc.n_instructions(), "instructions")
    nc = build_post0(512)
    print("post0 ok:", nc.n_instructions(), "instructions")
